# version 14
# baseline (speedup 1.0000x reference)
import numpy as np
from collections import deque
from contextlib import ExitStack

import concourse.bass as bass
import concourse.mybir as mybir
from concourse.bass_utils import run_bass_kernel_spmd

F32 = mybir.dt.float32
BF16 = mybir.dt.bfloat16
I32 = mybir.dt.int32
AF = mybir.ActivationFunctionType
OP = mybir.AluOpType
AX = mybir.AxisListType.X

B, S, D, H, HD = 4, 4096, 2048, 16, 128
TA, TO = 4096, 2048
DC = 16
SMS = 0.08838834764831845
NCORE = 8


class Sem:

    def __init__(self, h):
        self.h = h
        self.n = 0

    def inc(self, k=1):
        self.n += k
        return self.n


def build_program():
    nc = bass.Bass("TRN2", target_bir_lowering=False)

    x_in = nc.dram_tensor("x", [TA, D], F32, kind="ExternalInput").ap()
    wqk = nc.dram_tensor("wqk", [128, 32 * 2048], F32, kind="ExternalInput").ap()
    wv_d = nc.dram_tensor("wv", [128, 4 * 8192], F32, kind="ExternalInput").ap()
    wo_d = nc.dram_tensor("wo", [128, 16 * 2048], F32, kind="ExternalInput").ap()
    kern_d = nc.dram_tensor("kern", [128, 16 * 2048], F32, kind="ExternalInput").ap()
    kernT_d = nc.dram_tensor("kernT", [128, 16 * 2048], F32, kind="ExternalInput").ap()
    bqkv_d = nc.dram_tensor("bqkv", [128, 48], F32, kind="ExternalInput").ap()
    bvrow_d = nc.dram_tensor("bvrow", [1, 2048], F32, kind="ExternalInput").ap()
    obrow_d = nc.dram_tensor("obrow", [1, 2048], F32, kind="ExternalInput").ap()
    fbrow_d = nc.dram_tensor("fbrow", [1, 2048], F32, kind="ExternalInput").ap()
    out_d = nc.dram_tensor("out", [TO, D], F32, kind="ExternalOutput").ap()

    h1h = nc.dram_tensor("h1h", [TA, D], BF16, kind="Internal").ap()
    khh = nc.dram_tensor("khh", [H, 128, TA], BF16, kind="Internal").ap()
    vhh = nc.dram_tensor("vhh", [32, 128, 2048], BF16, kind="Internal").ap()
    qhh = nc.dram_tensor("qhh", [H, 128, TO], BF16, kind="Internal").ap()
    ctxh = nc.dram_tensor("ctxh", [H, 128, TO], BF16, kind="Internal").ap()
    x2h = nc.dram_tensor("x2h", [TO, D], BF16, kind="Internal").ap()
    xqh = nc.dram_tensor("xqh", [TO, D], BF16, kind="Internal").ap()
    rech = nc.dram_tensor("rech", [64, 512], F32, kind="Internal").ap()
    denh = nc.dram_tensor("denh", [64, 512], F32, kind="Internal").ap()
    nqh = nc.dram_tensor("nqh", [16, 2048], BF16, kind="Internal").ap()
    mkh = nc.dram_tensor("mkh", [1, 16], F32, kind="Internal").ap()
    sksh = nc.dram_tensor("sksh", [2048], F32, kind="Internal").ap()
    skrh = nc.dram_tensor("skrh", [2048], F32, kind="Internal").ap()

    OPS = {e: [] for e in ("sync", "gpsimd", "tensor", "vector", "scalar")}

    def on(eng):
        def dec(fn):
            OPS[eng].append(fn)
            return None
        return dec

    def wt(eng, sem, val=None):
        v = sem.n if val is None else val
        E = {"sync": nc.sync, "gpsimd": nc.gpsimd, "tensor": nc.tensor,
             "vector": nc.vector, "scalar": nc.scalar}[eng]
        OPS[eng].append(lambda E=E, s=sem.h, v=v: E.wait_ge(s, v))

    ctx = ExitStack()
    E = ctx.enter_context

    def sem(name):
        return Sem(E(nc.semaphore(name)))

    ph = sem("ph")
    d_x = [sem("d_x0"), sem("d_x1")]
    d_kt = [sem("d_kt0"), sem("d_kt1")]
    f_kt = [sem("f_kt0"), sem("f_kt1")]
    d_bias = sem("d_bias")
    d_bv = sem("d_bv")
    d_ob = sem("d_ob")
    s_h1 = [sem("s_h1a"), sem("s_h1b")]
    d_th = sem("d_th")
    s_sk = sem("s_sk")
    c_init = sem("c_init")
    c_bias = sem("c_bias")
    p1v = sem("p1v")
    p1s = sem("p1s")
    p1done = sem("p1done")
    d_w = [sem("d_w0"), sem("d_w1")]
    f_w = [sem("f_w0"), sem("f_w1")]
    d_wv = [sem("d_wv0"), sem("d_wv1")]
    f_wv = [sem("f_wv0"), sem("f_wv1")]
    p2mg = sem("p2mg")
    p2ev = sem("p2ev")
    s_k = [sem("s_k0"), sem("s_k1")]
    s_q = [sem("s_q0"), sem("s_q1")]
    s_v = [sem("s_v0"), sem("s_v1")]
    d_sk3 = sem("d_sk3")
    d_mk = sem("d_mk")
    p3bc = sem("p3bc")
    p3be = sem("p3be")
    d_k3 = [sem("d_k30"), sem("d_k31")]
    d_v3 = [sem("d_v30"), sem("d_v31")]
    d_q3 = [sem("d_q30"), sem("d_q31")]
    fr3 = sem("fr3")
    c_badd = sem("c_badd")
    fv3 = sem("fv3")
    d_kn = [sem("d_kn0"), sem("d_kn1")]
    f_kn = [sem("f_kn0"), sem("f_kn1")]
    c_sq3 = sem("c_sq3")
    d_nq = [sem("d_nq0"), sem("d_nq1")]
    s_nq = sem("s_nq")
    c_nm = sem("c_nm")
    c_nr = sem("c_nr")
    c_negC = sem("c_negC")
    mmS = sem("mmS")
    expd = sem("expd")
    mmPV = sem("mmPV")
    mmCtx = sem("mmCtx")
    ev3 = sem("ev3")
    s_ctx = [sem("s_ctx0"), sem("s_ctx1")]
    s_den = [sem("s_den0"), sem("s_den1")]
    c_rec = sem("c_rec")
    s_rec = sem("s_rec")
    d_cq = [sem("d_cq0"), sem("d_cq1")]
    d_rr = [sem("d_rr0"), sem("d_rr1")]
    c_bcB = sem("c_bcB")
    c_ctxn = sem("c_ctxn")
    d_wo = [sem("d_wo0"), sem("d_wo1")]
    d_xr = [sem("d_xr0"), sem("d_xr1")]
    p4mg = sem("p4mg")
    p4ev = sem("p4ev")
    p4v = sem("p4v")
    p4s = sem("p4s")
    c_x2 = sem("c_x2")
    s_x2 = [sem("s_x2a"), sem("s_x2b")]
    d_x2 = [sem("d_x2a"), sem("d_x2b")]
    c_xq = sem("c_xq")
    s_xq = [sem("s_xq0"), sem("s_xq1")]
    d_xqt = [sem("d_xqt0"), sem("d_xqt1")]
    s_h1p = d_nq
    d_th2 = d_xqt[0]
    p5mg = sem("p5mg")
    p5ev = sem("p5ev")
    p5v = sem("p5v")
    p5s = sem("p5s")
    c_out = sem("c_out")
    s_out = sem("s_out")

    def barrier(extra_sync_waits=()):
        tgt = ph.n + 5
        for (sm, v) in extra_sync_waits:
            wt("sync", sm, v)
        for eng in ("sync", "gpsimd", "tensor", "vector", "scalar"):
            ph.inc(1)
            En = {"sync": nc.sync, "gpsimd": nc.gpsimd, "tensor": nc.tensor,
                  "vector": nc.vector, "scalar": nc.scalar}[eng]
            OPS[eng].append(lambda E=En, s=ph.h: E.drain().then_inc(s, 1))
            OPS[eng].append(lambda E=En, s=ph.h, t=tgt: E.wait_ge(s, t))

    ones_bf = E(nc.sbuf_tensor("ones_bf", [128, 1], BF16))
    ones_f = E(nc.sbuf_tensor("ones_f", [1, 128], F32))
    ones_rbf = E(nc.sbuf_tensor("ones_rbf", [1, 128], BF16))
    epsv = E(nc.sbuf_tensor("epsv", [128, 1], F32))
    bqkv = E(nc.sbuf_tensor("bqkv_sb", [128, 48], F32))
    bq_bf = E(nc.sbuf_tensor("bq_bf", [1, 2048], BF16))
    smal = E(nc.sbuf_tensor("smal", [128, 16], F32))
    sm1 = E(nc.sbuf_tensor("sm1", [1, 16], F32))
    skT = E(nc.sbuf_tensor("skT", [128, 16], F32))
    sknT = E(nc.sbuf_tensor("sknT", [128, 16], F32))
    skrT = E(nc.sbuf_tensor("skrT", [128, 16], F32))
    onesm = E(nc.sbuf_tensor("onesm", [128, 128], BF16))
    nkt = E(nc.sbuf_tensor("nkt", [1, 16], F32))
    lnmk = E(nc.sbuf_tensor("lnmk", [1, 16], F32))
    wfc = [E(nc.sbuf_tensor("wfc0", [128, 2048], BF16)),
           E(nc.sbuf_tensor("wfc1", [128, 2048], BF16))]

    st12 = ExitStack()
    h1fm = st12.enter_context(nc.sbuf_tensor("h1fm", [128, 16 * 4096], BF16))
    st1 = ExitStack()
    xst = [st1.enter_context(nc.sbuf_tensor("xst0", [128, 2048], F32)),
           st1.enter_context(nc.sbuf_tensor("xst1", [128, 2048], F32))]
    h1t = [st1.enter_context(nc.sbuf_tensor("h1t0", [128, 2048], BF16)),
           st1.enter_context(nc.sbuf_tensor("h1t1", [128, 2048], BF16))]
    ktst = [st1.enter_context(nc.sbuf_tensor("ktst0", [128, 2048], F32)),
            st1.enter_context(nc.sbuf_tensor("ktst1", [128, 2048], F32))]
    sqtr = st1.enter_context(nc.sbuf_tensor("sqtr", [128, 2048], BF16))
    bqu = st1.enter_context(nc.sbuf_tensor("bqu", [1, 2048], I32))

    @on("vector")
    def _():
        nc.vector.memset(ones_bf[:], 1.0)
        nc.vector.memset(ones_f[:], 1.0)
        nc.vector.memset(ones_rbf[:], 1.0)
        nc.vector.memset(epsv[:], 1e-5)
        nc.vector.memset(smal[:], 0.0)
        nc.vector.memset(onesm[:], 1.0)
        nc.vector.memset(sm1[:], 0.0).then_inc(c_init.h, 1)
    c_init.inc(1)

    @on("sync")
    def _():
        nc.sync.dma_start(bqkv[:], bqkv_d[:]).then_inc(d_bias.h, 16)
    d_bias.inc(16)

    for j in range(16):
        if j >= 2:
            wt("sync", f_kt[j % 2], j // 2)

        @on("sync")
        def _(j=j):
            nc.sync.dma_start(ktst[j % 2][:], kernT_d[:, j * 2048:(j + 1) * 2048]
                              ).then_inc(d_kt[j % 2].h, 16)
        d_kt[j % 2].inc(16)
        wt("vector", d_kt[j % 2])

        @on("vector")
        def _(j=j):
            nc.vector.reduce_max(skT[:, j:j + 1], ktst[j % 2][:], axis=AX,
                                 apply_absolute_value=True).then_inc(f_kt[j % 2].h, 1)
        f_kt[j % 2].inc(1)

    @on("vector")
    def _():
        v = nc.vector
        v.drain()
        v.tensor_scalar_max(skT[:], skT[:], 1e-30)
        v.drain()
        v.reciprocal(skrT[:], skT[:])
        v.drain()
        v.tensor_scalar_mul(skrT[:], skrT[:], 127.0)
        v.tensor_scalar_mul(sknT[:], skT[:], -1.0 / 127.0).then_inc(p1s.h, 1)
    p1s.inc(1)

    wt("sync", f_kt[0], f_kt[0].n)
    wt("sync", f_kt[1], f_kt[1].n)

    @on("sync")
    def _():
        nc.sync.dma_start(ktst[0][0:1, :], fbrow_d[:]).then_inc(d_bias.h, 16)
    d_bias.inc(16)
    wt("vector", d_bias, d_bias.n)

    @on("vector")
    def _():
        v = nc.vector
        fb_r = ktst[0][0:1, :]
        t_r = xst[0][0:1, :]
        a_r = xst[1][0:1, :]
        u_r = bqu[0:1, :]
        v.drain()
        v.reduce_max(sm1[0:1, 0:1], fb_r, axis=AX, apply_absolute_value=True)
        v.drain()
        v.tensor_scalar_max(sm1[0:1, 0:1], sm1[0:1, 0:1], 1e-30)
        v.drain()
        v.reciprocal(sm1[0:1, 1:2], sm1[0:1, 0:1])
        v.drain()
        v.tensor_scalar_mul(sm1[0:1, 2:3], sm1[0:1, 1:2], 127.0)
        v.tensor_scalar_mul(sm1[0:1, 3:4], sm1[0:1, 0:1], -1.0 / 127.0)
        v.drain()
        v.tensor_scalar(t_r, fb_r, sm1[0:1, 2:3], None, OP.mult)
        v.drain()
        v.tensor_copy(u_r, t_r)
        v.drain()
        v.tensor_sub(a_r, t_r, u_r)
        v.drain()
        v.scalar_tensor_tensor(t_r, a_r, 0.5, u_r, OP.is_ge, OP.add)
        v.drain()
        v.scalar_tensor_tensor(a_r, a_r, -0.5, t_r, OP.is_lt, OP.subtract)
        v.drain()
        v.tensor_scalar(bq_bf[:], a_r, sm1[0:1, 3:4], None,
                        OP.mult).then_inc(c_bias.h, 1)
    c_bias.inc(1)
    wt("scalar", p1s, p1s.n)

    @on("scalar")
    def _():
        nc.scalar.dma_start(sksh.rearrange("(c p) -> p c", p=128), sknT[:],
                            allow_slow_non_contiguous=True).then_inc(s_sk.h, 16)
        nc.scalar.dma_start(skrh.rearrange("(c p) -> p c", p=128), skrT[:],
                            allow_slow_non_contiguous=True).then_inc(s_sk.h, 16)
    s_sk.inc(32)

    p1v_base = p1v.n
    for i in range(32):
        if i < 2:
            wt("sync", c_bias, 1)
        if i >= 2:
            if i < 16:
                wt("sync", s_h1[i % 2], 16 * (i // 2))
            elif i < 18:
                wt("sync", s_h1[i % 2], 128)
            else:
                wt("sync", s_h1p[i % 2], 16 * ((i - 16) // 2))
            wt("sync", p1v, p1v_base + (i - 2) * 2 + 2)

        @on("sync")
        def _(i=i):
            nc.sync.dma_start(xst[i % 2][:], x_in[i * 128:(i + 1) * 128, :]
                              ).then_inc(d_x[i % 2].h, 16)
        d_x[i % 2].inc(16)

        wt("scalar", d_x[i % 2])
        if i >= 1:
            wt("scalar", p1v, i * 2 - 1)
        wt("scalar", c_init, 1)

        @on("scalar")
        def _(i=i):
            s = nc.scalar
            s.activation(sqtr[:], xst[i % 2][:], AF.Square,
                         accum_out=smal[:, 0:1]).then_inc(p1s.h, 1)
        p1s.inc(1)

        wt("vector", d_x[i % 2])
        wt("vector", p1s, p1s.n)

        @on("vector")
        def _(i=i):
            v = nc.vector
            v.reduce_sum(smal[:, 1:2], xst[i % 2][:], axis=AX)
            v.drain()
            v.tensor_scalar_mul(smal[:, 2:3], smal[:, 1:2], 1.0 / 2048)
            v.tensor_scalar_mul(smal[:, 3:4], smal[:, 0:1], 1.0 / 2048)
            v.drain()
            v.tensor_mul(smal[:, 4:5], smal[:, 2:3], smal[:, 2:3])
            v.drain()
            v.tensor_sub(smal[:, 4:5], smal[:, 3:4], smal[:, 4:5]
                         ).then_inc(p1v.h, 1)
        p1v.inc(1)

        wt("scalar", p1v, p1v.n)

        @on("scalar")
        def _(i=i):
            s = nc.scalar
            s.activation(smal[:, 5:6], smal[:, 4:5], AF.Ln, bias=epsv[:, 0:1])
            s.drain()
            s.activation(smal[:, 6:7], smal[:, 5:6], AF.Exp, scale=-0.5
                         ).then_inc(p1s.h, 1)
        p1s.inc(1)

        wt("vector", p1s, p1s.n)

        @on("vector")
        def _(i=i):
            v = nc.vector
            v.scalar_tensor_tensor(smal[:, 7:8], smal[:, 2:3], -1.0,
                                   smal[:, 6:7], OP.mult, OP.mult)
            v.drain()
            v.tensor_scalar(h1t[i % 2][:], xst[i % 2][:], smal[:, 6:7],
                            smal[:, 7:8], OP.mult, OP.add).then_inc(p1v.h, 1)
        p1v.inc(1)

        wt("scalar", p1v, p1v.n)

        hsem = s_h1 if i < 16 else s_h1p

        @on("scalar")
        def _(i=i, hsem=hsem):
            nc.scalar.dma_start(h1h[i * 128:(i + 1) * 128, :], h1t[i % 2][:]
                                ).then_inc(hsem[i % 2].h, 16)
        hsem[i % 2].inc(16)

    @on("vector")
    def _():
        nc.vector.drain().then_inc(p1done.h, 1)
    p1done.inc(1)

    wt("sync", s_h1[0], s_h1[0].n)
    wt("sync", s_h1[1], s_h1[1].n)
    for dc in range(DC):
        @on("sync")
        def _(dc=dc):
            nc.sync.dma_start(h1fm[:, dc * 4096:dc * 4096 + 2048],
                              h1h[0:2048, dc * 128:(dc + 1) * 128],
                              transpose=True).then_inc(d_th.h, 16)
        d_th.inc(16)
    wt("sync", s_h1p[0], s_h1p[0].n)
    wt("sync", s_h1p[1], s_h1p[1].n)
    for dc in range(DC):
        @on("sync")
        def _(dc=dc):
            nc.sync.dma_start(h1fm[:, dc * 4096 + 2048:(dc + 1) * 4096],
                              h1h[2048:4096, dc * 128:(dc + 1) * 128],
                              transpose=True).then_inc(d_th2.h, 16)
        d_th2.inc(16)

    st1.close()
    st2 = ExitStack()
    wvb = [st2.enter_context(nc.sbuf_tensor("wvb0", [128, 8192], BF16))]
    bvB = st2.enter_context(nc.sbuf_tensor("bvB", [128, 2048], BF16))
    kout = [st2.enter_context(nc.sbuf_tensor("kout0", [128, 4096], BF16)),
            st2.enter_context(nc.sbuf_tensor("kout1", [128, 4096], BF16))]
    qout = [st2.enter_context(nc.sbuf_tensor("qout0", [128, 2048], BF16)),
            st2.enter_context(nc.sbuf_tensor("qout1", [128, 2048], BF16))]
    vst = [st2.enter_context(nc.sbuf_tensor("vst0", [128, 512], BF16)),
           st2.enter_context(nc.sbuf_tensor("vst1", [128, 512], BF16))]
    sqbuf = st2.enter_context(nc.sbuf_tensor("sqbuf", [128, 4096], BF16))
    nrow2 = st2.enter_context(nc.sbuf_tensor("nrow2", [1, 2048], BF16))
    bv_bf = st2.enter_context(nc.sbuf_tensor("bv_bf", [1, 2048], BF16))

    psA = [E(nc.psum_tensor("psA0", [128, 1024], F32)),
           E(nc.psum_tensor("psA1", [128, 1024], F32))]
    psB = [E(nc.psum_tensor("psB0", [128, 512], F32)),
           E(nc.psum_tensor("psB1", [128, 512], F32))]
    psC = E(nc.psum_tensor("psC", [128, 512], F32))
    psD = E(nc.psum_tensor("psD", [1, 512], F32))


    for fc in range(32):
        if fc >= 2:
            OPS["gpsimd"].append(
                lambda fc=fc: nc.gpsimd.wait_ge(p2mg.h, fw_rel[fc - 2]))

        @on("gpsimd")
        def _(fc=fc):
            nc.gpsimd.dma_start(wfc[fc % 2][:], wqk[:, fc * 2048:(fc + 1) * 2048]
                                ).then_inc(d_w[fc % 2].h, 16)
        d_w[fc % 2].inc(16)

    wt("gpsimd", p1done, 1)

    @on("gpsimd")
    def _():
        nc.gpsimd.dma_start(bv_bf[:], bvrow_d[:]).then_inc(d_bv.h, 16)
    d_bv.inc(16)
    for vg in range(4):
        if vg >= 1:
            OPS["gpsimd"].append(
                lambda vg=vg: nc.gpsimd.wait_ge(p2mg.h, fwv_rel[vg - 1]))

        @on("gpsimd")
        def _(vg=vg):
            nc.gpsimd.dma_start(wvb[0][:], wv_d[:, vg * 8192:(vg + 1) * 8192]
                                ).then_inc(d_wv[vg % 2].h, 16)
        d_wv[vg % 2].inc(16)

    wt("vector", s_h1p[0], s_h1p[0].n)
    wt("vector", s_h1p[1], s_h1p[1].n)

    g2 = 0
    fw_rel = {}
    fwv_rel = {}

    aslots = [(psA[0], 0), (psA[0], 512), (psA[1], 0), (psA[1], 512)]
    pend = {"fn": None}
    nr_end = {"v": 0}

    def flush_norm():
        if pend["fn"] is not None:
            fn = pend["fn"]
            pend["fn"] = None
            fn()

    for fc in range(32):
        wt("tensor", d_w[fc % 2], 16 * (fc // 2 + 1))
        ntt = 4 if fc < 16 else 8
        is_q = fc < 16
        head = fc if is_q else fc - 16
        obuf = qout if is_q else kout
        sst = s_q if is_q else s_k
        par = head % 2
        wt("vector", sst[par], sst[par].n)
        for tt in range(ntt):
            g2 += 1
            wt("tensor", d_th, 256)
            if tt >= 4:
                wt("tensor", d_th2, 256)
            if g2 > 2:
                wt("tensor", p2ev, g2 - 2)
            last = (tt == ntt - 1)
            for dc in range(DC):
                @on("tensor")
                def _(fc=fc, tt=tt, dc=dc, g=g2, last=last):
                    ins = nc.tensor.matmul(
                        psB[g % 2][:],
                        wfc[fc % 2][:, dc * 128:(dc + 1) * 128],
                        h1fm[:, dc * 4096 + tt * 512: dc * 4096 + (tt + 1) * 512],
                        start=(dc == 0), stop=(dc == DC - 1))
                    if dc == DC - 1:
                        ins.then_inc(p2mg.h, 1)
            p2mg.inc(1)
            if last:
                fw_rel[fc] = p2mg.n

            wt("vector", p2mg, p2mg.n)

            @on("vector")
            def _(fc=fc, tt=tt, g=g2, obuf=obuf, par=par):
                nc.vector.tensor_scalar(
                    obuf[par][:, tt * 512:(tt + 1) * 512], psB[g % 2][:],
                    bqkv[:, fc:fc + 1], None, OP.add).then_inc(p2ev.h, 1)
            p2ev.inc(1)
        wt("scalar", p2ev, p2ev.n)

        @on("scalar")
        def _(head=head, is_q=is_q, par=par, obuf=obuf):
            tgt = qhh[head] if is_q else khh[head]
            nc.scalar.dma_start(tgt[:, :], obuf[par][:]
                                ).then_inc((s_q if is_q else s_k)[par].h, 16)
        sst[par].inc(16)

        flush_norm()
        nm_after = c_nm.n
        if is_q and s_nq.n > 0:
            wt("vector", s_nq, s_nq.n)
        wt("vector", c_nm, nm_after)
        sq_rel = []
        for tt in range(ntt):
            @on("vector")
            def _(tt=tt, par=par, obuf=obuf):
                v = nc.vector
                v.drain()
                v.tensor_mul(sqbuf[:, tt * 512:(tt + 1) * 512],
                             obuf[par][:, tt * 512:(tt + 1) * 512],
                             obuf[par][:, tt * 512:(tt + 1) * 512]
                             ).then_inc(c_sq3.h, 1)
            sq_rel.append(c_sq3.inc(1))

        def mk_flush(head=head, is_q=is_q, ntt=ntt, sq_rel=tuple(sq_rel),
                     nr_prev=nr_end["v"]):
            def fn():
                nrel = {}
                for tt in range(ntt):
                    wt("tensor", c_sq3, sq_rel[tt])
                    if tt == 0 and nr_prev > 0:
                        wt("tensor", c_nr, nr_prev)
                    if tt >= 4:
                        wt("tensor", c_nr, nrel[tt - 4])

                    @on("tensor")
                    def _(tt=tt):
                        pa, off = aslots[tt % 4]
                        nc.tensor.matmul(pa[0:1, off:off + 512],
                                         ones_bf[:, 0:1],
                                         sqbuf[:, tt * 512:(tt + 1) * 512],
                                         start=True, stop=True
                                         ).then_inc(c_nm.h, 1)
                    c_nm.inc(1)
                    wt("vector", c_nm, c_nm.n)

                    @on("vector")
                    def _(tt=tt, is_q=is_q):
                        v = nc.vector
                        pa, off = aslots[tt % 4]
                        v.drain()
                        if is_q:
                            v.tensor_copy(nrow2[0:1, tt * 512:(tt + 1) * 512],
                                          pa[0:1, off:off + 512]
                                          ).then_inc(c_nr.h, 1)
                        else:
                            v.reduce_max(nkt[0:1, tt:tt + 1],
                                         pa[0:1, off:off + 512],
                                         axis=AX).then_inc(c_nr.h, 1)
                    nrel[tt] = c_nr.inc(1)
                if is_q:
                    @on("vector")
                    def _():
                        v = nc.vector
                        v.drain()
                        v.tensor_scalar_max(nrow2[0:1, :], nrow2[0:1, :], 1e-6
                                            ).then_inc(c_nr.h, 1)
                    c_nr.inc(1)
                    wt("scalar", c_nr, c_nr.n)
                    if head >= 2:
                        wt("scalar", s_nq, s_nq.n - 16)

                    @on("scalar")
                    def _(head=head):
                        nc.scalar.dma_start(nqh[head:head + 1, :],
                                            nrow2[0:1, :]).then_inc(s_nq.h, 16)
                    s_nq.inc(16)
                else:
                    @on("vector")
                    def _(head=head):
                        v = nc.vector
                        v.drain()
                        v.reduce_max(sm1[0:1, head:head + 1], nkt[0:1, 0:8],
                                     axis=AX).then_inc(c_nr.h, 1)
                    c_nr.inc(1)
                nr_end["v"] = c_nr.n
            return fn
        pend["fn"] = mk_flush()

    flush_norm()

    @on("vector")
    def _():
        v = nc.vector
        v.drain()
        v.tensor_scalar_max(sm1[0:1, 0:16], sm1[0:1, 0:16], 1e-12
                            ).then_inc(c_nr.h, 1)
    c_nr.inc(1)
    wt("scalar", c_nr, c_nr.n)

    @on("scalar")
    def _():
        nc.scalar.dma_start(mkh[0:1, :], sm1[0:1, 0:16]).then_inc(s_nq.h, 16)
    s_nq.inc(16)

    wt("tensor", d_th, 256)
    wt("tensor", d_th2, 256)
    wt("tensor", c_bias, 1)
    wt("tensor", d_bias, d_bias.n)
    wt("tensor", d_bv, 16)
    for i in range(4):
        g2 += 1
        if g2 > 2:
            wt("tensor", p2ev, g2 - 2)

        @on("tensor")
        def _(i=i, g=g2):
            nc.tensor.matmul(psB[g % 2][:], ones_rbf[0:1, :],
                             bv_bf[0:1, i * 512:(i + 1) * 512],
                             start=True, stop=True).then_inc(p2mg.h, 1)
        p2mg.inc(1)
        wt("vector", p2mg, p2mg.n)

        @on("vector")
        def _(i=i, g=g2):
            nc.vector.tensor_scalar(bvB[:, i * 512:(i + 1) * 512], psB[g % 2][:],
                                    1.0 / 128.0, None, OP.mult
                                    ).then_inc(p2ev.h, 1)
        p2ev.inc(1)
    vgrp = 0
    for vg in range(4):
        wt("tensor", d_wv[vg % 2], 16 * (vg // 2 + 1))
        for tc in range(32):
            g2 += 1
            vgrp += 1
            if g2 > 2:
                wt("tensor", p2ev, g2 - 2)
            wt("vector", s_v[vgrp % 2], s_v[vgrp % 2].n)
            @on("tensor")
            def _(vg=vg, g=g2):
                nc.tensor.matmul(psB[g % 2][:], onesm[:],
                                 bvB[:, vg * 512:(vg + 1) * 512],
                                 start=True, stop=False, skip_group_check=True)
            for dc in range(DC):
                @on("tensor")
                def _(vg=vg, tc=tc, dc=dc, g=g2, fin=(dc == DC - 1)):
                    ins = nc.tensor.matmul(
                        psB[g % 2][:],
                        h1fm[:, dc * 4096 + tc * 128: dc * 4096 + (tc + 1) * 128],
                        wvb[0][:, dc * 512:(dc + 1) * 512],
                        start=False, stop=fin, skip_group_check=True)
                    if fin:
                        ins.then_inc(p2mg.h, 1)
            p2mg.inc(1)
            if tc == 31:
                fwv_rel[vg] = p2mg.n

            wt("vector", p2mg, p2mg.n)

            @on("vector")
            def _(g=g2, vp=vgrp % 2):
                nc.vector.tensor_copy(vst[vp][:], psB[g % 2][:]
                                      ).then_inc(p2ev.h, 1)
            p2ev.inc(1)
            wt("scalar", p2ev, p2ev.n)

            @on("scalar")
            def _(vg=vg, tc=tc, vp=vgrp % 2):
                nc.scalar.dma_start(vhh[tc][:, vg * 512:(vg + 1) * 512],
                                    vst[vp][:]).then_inc(s_v[vp].h, 16)
            s_v[vgrp % 2].inc(16)

    barrier(extra_sync_waits=[(s_k[0], s_k[0].n), (s_k[1], s_k[1].n),
                              (s_q[0], s_q[0].n), (s_q[1], s_q[1].n),
                              (s_v[0], s_v[0].n), (s_v[1], s_v[1].n),
                              (s_sk, 32)])

    st2.close()
    st12.close()
    st3a = ExitStack()
    kq = st3a.enter_context(nc.sbuf_tensor("kq", [128, 16 * 2048], BF16))
    st3 = ExitStack()
    Kh = [st3.enter_context(nc.sbuf_tensor("Kh0", [128, 4096], BF16)),
          st3.enter_context(nc.sbuf_tensor("Kh1", [128, 4096], BF16))]
    Vh = [st3.enter_context(nc.sbuf_tensor("Vh0", [128, 4096], BF16)),
          st3.enter_context(nc.sbuf_tensor("Vh1", [128, 4096], BF16))]
    Qh = [st3.enter_context(nc.sbuf_tensor("Qh0", [128, 2048], BF16)),
          st3.enter_context(nc.sbuf_tensor("Qh1", [128, 2048], BF16))]
    PT = [st3.enter_context(nc.sbuf_tensor("PT0", [128, 2048], BF16)),
          st3.enter_context(nc.sbuf_tensor("PT1", [128, 2048], BF16))]
    negCB = st3.enter_context(nc.sbuf_tensor("negCB", [128, 512], BF16))
    ctxst = [st3.enter_context(nc.sbuf_tensor("ctxst0", [128, 2048], BF16)),
             st3.enter_context(nc.sbuf_tensor("ctxst1", [128, 2048], BF16))]
    nq_sb = st3.enter_context(nc.sbuf_tensor("nq_sb", [1, 2048], BF16))
    dstg = st3.enter_context(nc.sbuf_tensor("dstg", [64, 512], F32))
    rcps = st3.enter_context(nc.sbuf_tensor("rcps", [64, 512], F32))
    den_sb = [st3.enter_context(nc.sbuf_tensor("den_sb0", [1, 512], F32)),
              st3.enter_context(nc.sbuf_tensor("den_sb1", [1, 512], F32))]
    SKnb = st3.enter_context(nc.sbuf_tensor("SKnb", [128, 2048], F32))
    RSKb = st3.enter_context(nc.sbuf_tensor("RSKb", [128, 2048], F32))
    kst = [st3.enter_context(nc.sbuf_tensor("kst0", [128, 2048], F32))]
    kst.append(kst[0])
    kqt = st3.enter_context(nc.sbuf_tensor("kqt", [128, 2048], F32))
    negCrow = [st3.enter_context(nc.sbuf_tensor("negCrow0", [1, 2048], BF16)),
               st3.enter_context(nc.sbuf_tensor("negCrow1", [1, 2048], BF16))]
    skrow = kst[0][0:1, :]
    skrow2 = kqt[0:1, :]
    kqu = st3.enter_context(nc.sbuf_tensor("kqu", [128, 2048], I32))
    kqa = st3.enter_context(nc.sbuf_tensor("kqa", [128, 2048], F32))

    @on("sync")
    def _():
        nc.sync.dma_start(skrow, sksh.rearrange("(o f) -> o f", o=1)
                          ).then_inc(d_sk3.h, 16)
        nc.sync.dma_start(skrow2, skrh.rearrange("(o f) -> o f", o=1)
                          ).then_inc(d_sk3.h, 16)
    d_sk3.inc(32)
    wt("tensor", d_sk3, 32)
    for i in range(8):
        src = skrow if i < 4 else skrow2
        dstt = SKnb if i < 4 else RSKb
        col = (i % 4) * 512
        if i >= 2:
            wt("tensor", p3be, i - 1)

        @on("tensor")
        def _(i=i, src=src, col=col):
            nc.tensor.matmul(psB[i % 2][:], ones_f[0:1, :],
                             src[0:1, col:col + 512], start=True, stop=True
                             ).then_inc(p3bc.h, 1)
        p3bc.inc(1)
        wt("vector", p3bc, p3bc.n)

        @on("vector")
        def _(i=i, dstt=dstt, col=col):
            nc.vector.tensor_copy(dstt[:, col:col + 512], psB[i % 2][:]
                                  ).then_inc(p3be.h, 1)
        p3be.inc(1)

    wt("sync", s_nq, s_nq.n)

    @on("sync")
    def _():
        nc.sync.dma_start(nkt[0:1, :], mkh[0:1, :]).then_inc(d_mk.h, 16)
    d_mk.inc(16)
    wt("scalar", d_mk, d_mk.n)

    @on("scalar")
    def _():
        nc.scalar.activation(lnmk[0:1, :], nkt[0:1, :], AF.Ln
                             ).then_inc(p1s.h, 1)
    p1s.inc(1)
    wt("vector", p1s, p1s.n)

    @on("vector")
    def _():
        nc.vector.drain()
        nc.vector.tensor_scalar_mul(lnmk[0:1, :], lnmk[0:1, :], 0.5
                                    ).then_inc(c_nr.h, 1)
    c_nr.inc(1)

    def emit_cchain(hh):
        par2 = hh % 2
        if hh >= 1:
            wt("sync", c_negC, hh)
        wt("sync", c_nr, 1)

        @on("sync")
        def _(hh=hh):
            nc.sync.dma_start(nq_sb[0:1, :], nqh[hh:hh + 1, :]
                              ).then_inc(d_nq[hh % 2].h, 16)
        d_nq[hh % 2].inc(16)
        wt("scalar", d_nq[hh % 2], d_nq[hh % 2].n)

        @on("scalar")
        def _():
            nc.scalar.activation(kqa[0:1, :], nq_sb[0:1, :], AF.Ln
                                 ).then_inc(p1s.h, 1)
        p1s.inc(1)
        wt("vector", p1s, p1s.n)

        @on("vector")
        def _(hh=hh):
            v = nc.vector
            v.drain()
            v.tensor_scalar(kqt[0:1, :], kqa[0:1, :], 0.5,
                            lnmk[0:1, hh:hh + 1], OP.mult, OP.add
                            ).then_inc(c_sq3.h, 1)
        c_sq3.inc(1)
        wt("scalar", c_sq3, c_sq3.n)

        @on("scalar")
        def _():
            nc.scalar.activation(kqa[0:1, :], kqt[0:1, :], AF.Exp
                                 ).then_inc(p1s.h, 1)
        p1s.inc(1)
        wt("vector", p1s, p1s.n)

        @on("vector")
        def _(par2=par2):
            nc.vector.drain()
            nc.vector.tensor_scalar(negCrow[par2][0:1, :], kqa[0:1, :],
                                    -1.001 / 128.0, -1.0 / 128.0,
                                    OP.mult, OP.add).then_inc(c_negC.h, 1)
        c_negC.inc(1)

    emit_cchain(0)

    smm_rel = {}
    pvh_rel = {}
    for h in range(H):
        par = h % 2
        if h >= 2:
            wt("sync", mmS, smm_rel[h - 2])
            wt("sync", mmPV, pvh_rel[h - 2])

        @on("sync")
        def _(h=h, par=par):
            nc.sync.dma_start(Kh[par][:], khh[h][:, :]).then_inc(d_k3[par].h, 16)
            nc.sync.dma_start(Vh[par][:],
                              vhh[:, :, h * 128:(h + 1) * 128].transpose([1, 0, 2])
                              ).then_inc(d_v3[par].h, 16)
            nc.sync.dma_start(Qh[par][:], qhh[h][:, :]).then_inc(d_q3[par].h, 16)
        d_k3[par].inc(16)
        d_v3[par].inc(16)
        d_q3[par].inc(16)

        if h <= 1:
            wt("sync", p3be, 8)
        if h >= 1:
            wt("sync", f_kn[0], h)

        @on("sync")
        def _(h=h):
            nc.sync.dma_start(kst[0][:], kern_d[:, h * 2048:(h + 1) * 2048]
                              ).then_inc(d_kn[0].h, 16)
        d_kn[0].inc(16)

        quantq = deque()

        def _qp1(h=h):
            wt("vector", d_kn[0], 16 * (h + 1))
            wt("vector", p3be, p3be.n)

            @on("vector")
            def _():
                v = nc.vector
                v.tensor_mul(kqt[:], kst[0][:], RSKb[:])
                v.drain()

        def _qp2():
            @on("vector")
            def _():
                v = nc.vector
                v.tensor_copy(kqu[:], kqt[:])
                v.drain()

        def _qp3():
            @on("vector")
            def _():
                v = nc.vector
                v.tensor_sub(kqa[:], kqt[:], kqu[:])
                v.drain()

        def _qp4():
            @on("vector")
            def _():
                v = nc.vector
                v.scalar_tensor_tensor(kqt[:], kqa[:], 0.5, kqu[:],
                                       OP.is_ge, OP.add)
                v.drain()

        def _qp5():
            @on("vector")
            def _():
                v = nc.vector
                v.scalar_tensor_tensor(kqa[:], kqa[:], -0.5, kqt[:], OP.is_lt,
                                       OP.subtract)
                v.drain()

        def _qp6(h=h):
            @on("vector")
            def _(h=h):
                nc.vector.tensor_mul(kq[:, h * 2048:(h + 1) * 2048], kqa[:],
                                     SKnb[:]).then_inc(f_kn[0].h, 1)
            f_kn[0].inc(1)
        for _p in (_qp1, _qp2, _qp3, _qp4, _qp5, _qp6):
            quantq.append(_p)

        wt("tensor", c_negC, c_negC.n)
        wt("tensor", d_v3[par], 16 * (h // 2 + 1))
        wt("tensor", d_k3[par], 16 * (h // 2 + 1))
        wt("tensor", d_q3[par], 16 * (h // 2 + 1))
        exp_rel = {-1: 0}
        pv_rel = {-2: 0, -1: 0}
        def emit_pvden(qt, ss):
            sp = ss % 2
            first = (ss == 0)
            last = (ss == 7)
            wt("tensor", expd, exp_rel[ss])
            for j in range(4):
                kc = ss * 4 + j

                @on("tensor")
                def _(par=par, kc=kc, j=j, first=first, last=last, sp=sp):
                    nc.tensor.matmul(
                        psC[:], Vh[par][:, kc * 128:(kc + 1) * 128],
                        PT[sp][:, j * 512:(j + 1) * 512],
                        start=(first and j == 0), stop=(last and j == 3),
                        skip_group_check=True)
            for j in range(4):
                @on("tensor")
                def _(j=j, sp=sp, first=first, last=last, fin=(j == 3)):
                    ins = nc.tensor.matmul(
                        psD[0:1, :], ones_bf[:, 0:1],
                        PT[sp][:, j * 512:(j + 1) * 512],
                        start=(first and j == 0), stop=(last and j == 3),
                        skip_group_check=True)
                    if fin:
                        ins.then_inc(mmPV.h, 1)
            pv_rel[ss] = mmPV.inc(1)
            if qt == 3 and ss == 7:
                pvh_rel[h] = mmPV.n

        halves = None
        for qt in range(4):
            wt("tensor", ev3, ev3.n)
            wt("tensor", c_negC, h + 1)

            @on("tensor")
            def _(qt=qt, par=par):
                nc.tensor.matmul(psB[0][:], ones_rbf[0:1, :],
                                 negCrow[par][0:1, qt * 512:(qt + 1) * 512],
                                 start=True, stop=True).then_inc(c_nm.h, 1)
            c_nm.inc(1)
            wt("vector", c_nm, c_nm.n)

            @on("vector")
            def _():
                v = nc.vector
                v.drain()
                v.tensor_copy(negCB[:], psB[0][:]).then_inc(c_nr.h, 1)
            c_nr.inc(1)
            wt("tensor", c_nr, c_nr.n)

            hv = [(psA[0], 0), (psA[0], 512), (psA[1], 0), (psA[1], 512)]
            for ss in range(8):
                sp = ss % 2
                wt("tensor", expd, exp_rel.get(ss - 1, 0))
                for j in range(4):
                    @on("tensor")
                    def _(j=j, hv=hv):
                        pa, off = hv[j]
                        nc.tensor.matmul(pa[:, off:off + 512], onesm[:],
                                         negCB[:], start=True, stop=False,
                                         skip_group_check=True)
                for j in range(4):
                    kc = ss * 4 + j

                    @on("tensor")
                    def _(par=par, qt=qt, kc=kc, j=j, fin=(j == 3), hv=hv):
                        pa, off = hv[j]
                        ins = nc.tensor.matmul(
                            pa[:, off:off + 512],
                            Kh[par][:, kc * 128:(kc + 1) * 128],
                            Qh[par][:, qt * 512:(qt + 1) * 512],
                            start=False, stop=True, skip_group_check=True)
                        if fin:
                            ins.then_inc(mmS.h, 1)
                mmS.inc(1)
                if qt == 3 and ss == 7:
                    smm_rel[h] = mmS.n
                if qt >= 2 and quantq:
                    quantq.popleft()()

                wt("scalar", mmS, mmS.n)
                wt("scalar", mmPV, pv_rel.get(ss - 2, 0))

                @on("scalar")
                def _(sp=sp):
                    nc.scalar.activation(PT[sp][:, 0:1024], psA[0][:], AF.Exp,
                                         scale=SMS)
                    nc.scalar.activation(PT[sp][:, 1024:2048], psA[1][:],
                                         AF.Exp, scale=SMS).then_inc(expd.h, 1)
                exp_rel[ss] = expd.inc(1)

                if ss >= 1:
                    emit_pvden(qt, ss - 1)
            emit_pvden(qt, 7)
            if qt == 1 and h + 1 < H:
                emit_cchain(h + 1)
            exp_rel = {-1: exp_rel[7]}
            pv_rel = {-2: pv_rel[6], -1: pv_rel[7]}

            dpar = (h * 4 + qt) % 2
            wt("vector", mmPV, mmPV.n)
            if qt == 0:
                wt("vector", s_ctx[par], s_ctx[par].n)
            if h * 4 + qt >= 2:
                wt("vector", s_den[dpar], s_den[dpar].n)

            @on("vector")
            def _(h=h, qt=qt, par=par, dpar=dpar):
                v = nc.vector
                v.tensor_copy(ctxst[par][:, qt * 512:(qt + 1) * 512], psC[:])
                v.tensor_copy(den_sb[dpar][0:1, :], psD[0:1, :])
                v.drain().then_inc(ev3.h, 1)
            ev3.inc(1)
            wt("scalar", ev3, ev3.n)

            @on("scalar")
            def _(h=h, qt=qt, dpar=dpar):
                nc.scalar.dma_start(denh[h * 4 + qt:h * 4 + qt + 1, :],
                                    den_sb[dpar][0:1, :]
                                    ).then_inc(s_den[dpar].h, 16)
            s_den[dpar].inc(16)

        wt("scalar", ev3, ev3.n)

        @on("scalar")
        def _(h=h, par=par):
            nc.scalar.dma_start(ctxh[h][:, :], ctxst[par][:]
                                ).then_inc(s_ctx[par].h, 16)
        s_ctx[par].inc(16)

        while quantq:
            quantq.popleft()()

    wt("sync", s_den[0], s_den[0].n)
    wt("sync", s_den[1], s_den[1].n)

    @on("sync")
    def _():
        nc.sync.dma_start(dstg[:], denh[:, :]).then_inc(d_sk3.h, 16)
    d_sk3.inc(16)
    wt("vector", d_sk3, d_sk3.n)

    @on("vector")
    def _():
        v = nc.vector
        v.reciprocal(rcps[:], dstg[:])
        v.drain().then_inc(c_rec.h, 1)
    c_rec.inc(1)
    wt("scalar", c_rec, c_rec.n)

    @on("scalar")
    def _():
        nc.scalar.dma_start(rech.rearrange("(q h) f -> h q f", h=16), rcps[:]
                            ).then_inc(s_rec.h, 16)
    s_rec.inc(16)

    barrier(extra_sync_waits=[(s_ctx[0], s_ctx[0].n), (s_ctx[1], s_ctx[1].n),
                              (s_rec, 16)])
    st3.close()
    st4 = ExitStack()
    ctxq = [st4.enter_context(nc.sbuf_tensor("ctxq0", [128, 8192], BF16)),
            st4.enter_context(nc.sbuf_tensor("ctxq1", [128, 8192], BF16))]
    recB = st4.enter_context(nc.sbuf_tensor("recB", [128, 512], F32))
    recrow = [st4.enter_context(nc.sbuf_tensor("recrow0", [1, 1024], F32)),
              st4.enter_context(nc.sbuf_tensor("recrow1", [1, 1024], F32))]
    wob = [st4.enter_context(nc.sbuf_tensor("wob0", [128, 8192], BF16)),
           st4.enter_context(nc.sbuf_tensor("wob1", [128, 8192], BF16))]
    y1 = st4.enter_context(nc.sbuf_tensor("y1", [128, 2048], F32))
    xres = [st4.enter_context(nc.sbuf_tensor("xres0", [128, 2048], F32)),
            st4.enter_context(nc.sbuf_tensor("xres1", [128, 2048], F32))]
    obB = st4.enter_context(nc.sbuf_tensor("obB", [128, 2048], BF16))
    sqt4 = st4.enter_context(nc.sbuf_tensor("sqt4", [128, 2048], BF16))
    x2st = [st4.enter_context(nc.sbuf_tensor("x2st0", [128, 2048], BF16)),
            st4.enter_context(nc.sbuf_tensor("x2st1", [128, 2048], BF16))]
    xmod4 = st4.enter_context(nc.sbuf_tensor("xmod4", [128, 2048], F32))
    ob_bf = st4.enter_context(nc.sbuf_tensor("ob_bf", [1, 2048], BF16))

    @on("gpsimd")
    def _():
        nc.gpsimd.dma_start(ob_bf[:], obrow_d[:]).then_inc(d_ob.h, 16)
    d_ob.inc(16)
    wt("tensor", d_ob, 16)
    for i in range(4):
        if i >= 2:
            wt("tensor", p4ev, p4ev.n - 1)

        @on("tensor")
        def _(i=i):
            nc.tensor.matmul(psB[i % 2][:], ones_rbf[0:1, :],
                             ob_bf[0:1, i * 512:(i + 1) * 512],
                             start=True, stop=True).then_inc(p4mg.h, 1)
        p4mg.inc(1)
        wt("vector", p4mg, p4mg.n)

        @on("vector")
        def _(i=i):
            nc.vector.tensor_scalar(obB[:, i * 512:(i + 1) * 512], psB[i % 2][:],
                                    1.0 / 128.0, None, OP.mult
                                    ).then_inc(p4ev.h, 1)
        p4ev.inc(1)
    wt("tensor", p4ev, p4ev.n)
    p4ev_base = p4ev.n

    g4 = 0
    rr_rel = {}
    for qt in range(4):
        par = qt % 2
        if qt >= 2:
            wt("sync", c_ctxn, 16 * (qt - 1))

        @on("sync")
        def _(qt=qt, par=par):
            nc.sync.dma_start(
                ctxq[par][:],
                ctxh[:, :, qt * 512:(qt + 1) * 512].transpose([1, 0, 2])
                ).then_inc(d_cq[par].h, 16)
        d_cq[par].inc(16)

        wt("tensor", d_cq[par], 16 * (qt // 2 + 1))
        wt("vector", d_cq[par], 16 * (qt // 2 + 1))
        bnk3 = [psB[0], psB[1], psC]
        for h in range(H):
            if h % 2 == 0:
                hp = h // 2
                if qt * 8 + hp >= 2:
                    wt("sync", c_nm, rr_rel["v%d" % (hp % 2)])

                @on("sync")
                def _(qt=qt, hp=hp):
                    nc.sync.dma_start(recrow[hp % 2][:],
                                      rech[qt * 16 + hp * 2: qt * 16 + hp * 2 + 2, :]
                                      ).then_inc(d_rr[hp % 2].h, 16)
                d_rr[hp % 2].inc(16)
            wt("tensor", d_rr[(h // 2) % 2], d_rr[(h // 2) % 2].n)
            if h >= 3 or qt >= 1:
                wt("tensor", c_ctxn, max(c_ctxn.n - 2, 0))

            @on("tensor")
            def _(h=h, qt=qt):
                nc.tensor.matmul(bnk3[h % 3][:],
                                 ones_f[0:1, :],
                                 recrow[(h // 2) % 2][0:1, (h % 2) * 512:(h % 2) * 512 + 512],
                                 start=True, stop=True).then_inc(c_nm.h, 1)
            c_nm.inc(1)
            if h % 2 == 1:
                rr_rel["v%d" % ((h // 2) % 2)] = c_nm.n
            wt("vector", c_nm, c_nm.n)

            @on("vector")
            def _(h=h, par=par):
                v = nc.vector
                v.drain()
                v.tensor_copy(recB[:], bnk3[h % 3][:])
                v.drain()
                v.tensor_mul(ctxq[par][:, h * 512:(h + 1) * 512],
                             ctxq[par][:, h * 512:(h + 1) * 512], recB[:]
                             ).then_inc(c_ctxn.h, 1)
            c_ctxn.inc(1)

        wt("tensor", c_ctxn, c_ctxn.n)
        for tc in range(4):
            gi = qt * 4 + tc
            if gi >= 2:
                wt("sync", c_x2, gi - 1)

            @on("sync")
            def _(qt=qt, tc=tc, gi=gi):
                r0 = qt * 512 + tc * 128
                nc.sync.dma_start(xres[gi % 2][:], x_in[r0:r0 + 128, :]
                                  ).then_inc(d_xr[gi % 2].h, 16)
            d_xr[gi % 2].inc(16)

            for fg in range(4):
                wi = (tc * 4 + fg) % 2
                nld = qt * 16 + tc * 4 + fg
                if nld >= 2:
                    wt("gpsimd", p4mg, p4mg.n - 1)

                @on("gpsimd")
                def _(fg=fg, wi=wi):
                    nc.gpsimd.dma_start(
                        wob[wi][:],
                        wo_d.rearrange("p (dc f) -> p dc f", dc=16)[:, :, fg * 512:(fg + 1) * 512]
                        ).then_inc(d_wo[wi].h, 16)
                d_wo[wi].inc(16)

                g4 += 1
                wt("tensor", d_wo[wi], d_wo[wi].n)
                if g4 > 2:
                    wt("tensor", p4ev, p4ev_base + g4 - 2)
                @on("tensor")
                def _(fg=fg, g=g4):
                    nc.tensor.matmul(psA[g % 2][:, 0:512], onesm[:],
                                     obB[:, fg * 512:(fg + 1) * 512],
                                     start=True, stop=False,
                                     skip_group_check=True)
                for dc in range(DC):
                    @on("tensor")
                    def _(tc=tc, dc=dc, g=g4, wi=wi, par=par, fin=(dc == DC - 1)):
                        ins = nc.tensor.matmul(
                            psA[g % 2][:, 0:512],
                            ctxq[par][:, dc * 512 + tc * 128: dc * 512 + (tc + 1) * 128],
                            wob[wi][:, dc * 512:(dc + 1) * 512],
                            start=False, stop=fin, skip_group_check=True)
                        if fin:
                            ins.then_inc(p4mg.h, 1)
                p4mg.inc(1)

                wt("vector", p4mg, p4mg.n)
                if fg == 0 and gi >= 1:
                    wt("vector", p4s, p4s.n)

                @on("vector")
                def _(fg=fg, g=g4):
                    if fg == 0:
                        nc.vector.drain()
                    nc.vector.tensor_copy(y1[:, fg * 512:(fg + 1) * 512],
                                          psA[g % 2][:, 0:512]
                                          ).then_inc(p4ev.h, 1)
                p4ev.inc(1)

            wt("scalar", p4ev, p4ev.n)
            if gi >= 1:
                wt("scalar", p4v, p4v.n)

            @on("scalar")
            def _():
                nc.scalar.activation(sqt4[:], y1[:], AF.Square,
                                     accum_out=smal[:, 0:1]).then_inc(p4s.h, 1)
            p4s.inc(1)
            wt("vector", p4s, p4s.n)

            @on("vector")
            def _():
                v = nc.vector
                v.reduce_sum(smal[:, 1:2], y1[:], axis=AX)
                v.drain()
                v.tensor_scalar_mul(smal[:, 2:3], smal[:, 1:2], 1.0 / 2048)
                v.tensor_scalar_mul(smal[:, 3:4], smal[:, 0:1], 1.0 / 2048)
                v.drain()
                v.tensor_mul(smal[:, 4:5], smal[:, 2:3], smal[:, 2:3])
                v.drain()
                v.tensor_sub(smal[:, 4:5], smal[:, 3:4], smal[:, 4:5]
                             ).then_inc(p4v.h, 1)
            p4v.inc(1)
            wt("scalar", p4v, p4v.n)

            @on("scalar")
            def _():
                s = nc.scalar
                s.activation(smal[:, 5:6], smal[:, 4:5], AF.Ln, bias=epsv[:, 0:1])
                s.drain()
                s.activation(smal[:, 6:7], smal[:, 5:6], AF.Exp, scale=-0.5
                             ).then_inc(p4s.h, 1)
            p4s.inc(1)
            wt("vector", p4s, p4s.n)
            wt("vector", d_xr[gi % 2], d_xr[gi % 2].n)
            if gi >= 2:
                wt("vector", s_x2[gi % 2], s_x2[gi % 2].n)

            @on("vector")
            def _(gi=gi):
                v = nc.vector
                v.scalar_tensor_tensor(smal[:, 7:8], smal[:, 2:3], -1.0,
                                       smal[:, 6:7], OP.mult, OP.mult)
                v.drain()
                v.tensor_scalar(xmod4[:], xres[gi % 2][:], smal[:, 7:8], None,
                                OP.add)
                v.drain()
                v.scalar_tensor_tensor(x2st[gi % 2][:], y1[:], smal[:, 6:7],
                                       xmod4[:], OP.mult, OP.add
                                       ).then_inc(c_x2.h, 1)
            c_x2.inc(1)
            wt("scalar", c_x2, c_x2.n)

            @on("scalar")
            def _(qt=qt, tc=tc, gi=gi):
                r0 = qt * 512 + tc * 128
                nc.scalar.dma_start(x2h[r0:r0 + 128, :], x2st[gi % 2][:]
                                    ).then_inc(s_x2[gi % 2].h, 16)
            s_x2[gi % 2].inc(16)

    barrier(extra_sync_waits=[(s_x2[0], s_x2[0].n), (s_x2[1], s_x2[1].n)])
    st4.close()
    st5 = ExitStack()
    x2q = st5.enter_context(nc.sbuf_tensor("x2q", [128, 4 * 2048], BF16))
    h2 = st5.enter_context(nc.sbuf_tensor("h2", [128, 2048], F32))
    qtt = st5.enter_context(nc.sbuf_tensor("qtt", [128, 2048], F32))
    quu = st5.enter_context(nc.sbuf_tensor("quu", [128, 2048], I32))
    qaa = st5.enter_context(nc.sbuf_tensor("qaa", [128, 2048], F32))
    xqtile = [st5.enter_context(nc.sbuf_tensor("xqtile0", [128, 2048], BF16)),
              st5.enter_context(nc.sbuf_tensor("xqtile1", [128, 2048], BF16))]
    xqfm = [st5.enter_context(nc.sbuf_tensor("xqfm0", [128, 8192], BF16)),
            st5.enter_context(nc.sbuf_tensor("xqfm1", [128, 8192], BF16))]
    zt = st5.enter_context(nc.sbuf_tensor("zt", [128, 2048], F32))
    yq = st5.enter_context(nc.sbuf_tensor("yq", [128, 2048], F32))
    sqt5 = st5.enter_context(nc.sbuf_tensor("sqt5", [128, 2048], BF16))
    xmod5 = st5.enter_context(nc.sbuf_tensor("xmod5", [128, 2048], F32))
    outt = st5.enter_context(nc.sbuf_tensor("outt0", [128, 2048], F32))
    bqB = st5.enter_context(nc.sbuf_tensor("bqB", [128, 2048], BF16))

    wt("tensor", c_bias, 1)
    for i in range(4):
        if i >= 2:
            wt("tensor", p5ev, p5ev.n - 1)

        @on("tensor")
        def _(i=i):
            nc.tensor.matmul(psB[i % 2][:], ones_rbf[0:1, :],
                             bq_bf[0:1, i * 512:(i + 1) * 512],
                             start=True, stop=True).then_inc(p5mg.h, 1)
        p5mg.inc(1)
        wt("vector", p5mg, p5mg.n)

        @on("vector")
        def _(i=i):
            nc.vector.tensor_scalar(bqB[:, i * 512:(i + 1) * 512], psB[i % 2][:],
                                    1.0 / 128.0, None, OP.mult
                                    ).then_inc(p5ev.h, 1)
        p5ev.inc(1)
    wt("tensor", p5ev, p5ev.n)
    p5mg_base = p5mg.n
    p5ev_base = p5ev.n
    for qt in range(4):
        par = qt % 2
        for tc in range(4):
            gi = qt * 4 + tc
            if qt >= 1:
                wt("sync", c_out, qt * 4)

            @on("sync")
            def _(qt=qt, tc=tc, par=par):
                r0 = qt * 512 + tc * 128
                nc.sync.dma_start(x2q[:, tc * 2048:(tc + 1) * 2048],
                                  x2h[r0:r0 + 128, :]).then_inc(d_x2[par].h, 16)
            d_x2[par].inc(16)

        for tc in range(4):
            gi = qt * 4 + tc
            wt("scalar", d_x2[par], d_x2[par].n)
            if gi >= 1:
                wt("scalar", p5v, p5v.n)

            @on("scalar")
            def _(tc=tc):
                nc.scalar.activation(sqt5[:], x2q[:, tc * 2048:(tc + 1) * 2048],
                                     AF.Square, accum_out=smal[:, 0:1]
                                     ).then_inc(p5s.h, 1)
            p5s.inc(1)
            wt("vector", d_x2[par], d_x2[par].n)
            wt("vector", p5s, p5s.n)

            @on("vector")
            def _(tc=tc):
                v = nc.vector
                v.reduce_sum(smal[:, 1:2], x2q[:, tc * 2048:(tc + 1) * 2048],
                             axis=AX)
                v.drain()
                v.tensor_scalar_mul(smal[:, 2:3], smal[:, 1:2], 1.0 / 2048)
                v.tensor_scalar_mul(smal[:, 3:4], smal[:, 0:1], 1.0 / 2048)
                v.drain()
                v.tensor_mul(smal[:, 4:5], smal[:, 2:3], smal[:, 2:3])
                v.drain()
                v.tensor_sub(smal[:, 4:5], smal[:, 3:4], smal[:, 4:5]
                             ).then_inc(p5v.h, 1)
            p5v.inc(1)
            wt("scalar", p5v, p5v.n)

            @on("scalar")
            def _():
                s = nc.scalar
                s.activation(smal[:, 5:6], smal[:, 4:5], AF.Ln, bias=epsv[:, 0:1])
                s.drain()
                s.activation(smal[:, 6:7], smal[:, 5:6], AF.Exp, scale=-0.5
                             ).then_inc(p5s.h, 1)
            p5s.inc(1)
            wt("vector", p5s, p5s.n)
            if gi >= 2:
                wt("vector", s_xq[gi % 2], s_xq[gi % 2].n)

            @on("vector")
            def _(tc=tc, gi=gi):
                v = nc.vector
                v.scalar_tensor_tensor(smal[:, 7:8], smal[:, 2:3], -1.0,
                                       smal[:, 6:7], OP.mult, OP.mult)
                v.drain()
                v.tensor_scalar(h2[:], x2q[:, tc * 2048:(tc + 1) * 2048],
                                smal[:, 6:7], smal[:, 7:8], OP.mult, OP.add)
                v.drain()
                v.reduce_max(smal[:, 8:9], h2[:], axis=AX,
                             apply_absolute_value=True)
                v.drain()
                v.tensor_scalar_max(smal[:, 8:9], smal[:, 8:9], 1e-30)
                v.drain()
                v.reciprocal(smal[:, 9:10], smal[:, 8:9])
                v.drain()
                v.tensor_scalar_mul(smal[:, 10:11], smal[:, 9:10], 127.0)
                v.tensor_scalar_mul(smal[:, 11:12], smal[:, 8:9], -1.0 / 127.0)
                v.drain()
                v.tensor_scalar(qtt[:], h2[:], smal[:, 10:11], None, OP.mult)
                v.drain()
                v.tensor_copy(quu[:], qtt[:])
                v.drain()
                v.tensor_sub(qaa[:], qtt[:], quu[:])
                v.drain()
                v.scalar_tensor_tensor(qtt[:], qaa[:], 0.5, quu[:],
                                       OP.is_ge, OP.add)
                v.drain()
                v.scalar_tensor_tensor(qtt[:], qaa[:], 1.5, qtt[:],
                                       OP.is_ge, OP.add)
                v.drain()
                v.scalar_tensor_tensor(h2[:], qaa[:], -0.5, qtt[:],
                                       OP.is_lt, OP.subtract)
                v.drain()
                v.scalar_tensor_tensor(h2[:], qaa[:], -1.5, h2[:],
                                       OP.is_lt, OP.add)
                v.drain()
                v.tensor_scalar(xqtile[gi % 2][:], h2[:], smal[:, 11:12], None,
                                OP.mult).then_inc(c_xq.h, 1)
            c_xq.inc(1)
            wt("scalar", c_xq, c_xq.n)

            @on("scalar")
            def _(qt=qt, tc=tc, gi=gi):
                r0 = qt * 512 + tc * 128
                nc.scalar.dma_start(xqh[r0:r0 + 128, :], xqtile[gi % 2][:]
                                    ).then_inc(s_xq[gi % 2].h, 16)
            s_xq[gi % 2].inc(16)

        wt("sync", s_xq[0], s_xq[0].n)
        wt("sync", s_xq[1], s_xq[1].n)
        if qt >= 2:
            wt("sync", p5mg, p5mg.n - 4)
        for dc in range(DC):
            @on("sync")
            def _(qt=qt, dc=dc, par=par):
                nc.sync.dma_start(
                    xqfm[par][:, dc * 512:(dc + 1) * 512],
                    xqh[qt * 512:(qt + 1) * 512, dc * 128:(dc + 1) * 128],
                    transpose=True).then_inc(d_xqt[par].h, 16)
            d_xqt[par].inc(16)

        wt("tensor", d_xqt[par], d_xqt[par].n)
        wt("tensor", c_bias, 1)
        for tc in range(4):
            gi = qt * 4 + tc
            for fg in range(4):
                g5 = qt * 16 + tc * 4 + fg + 1
                if g5 > 2:
                    wt("tensor", p5ev, p5ev_base + g5 - 2)

                @on("tensor")
                def _(fg=fg, g=g5):
                    nc.tensor.matmul(psA[g % 2][:, 0:512], onesm[:],
                                     bqB[:, fg * 512:(fg + 1) * 512],
                                     start=True, stop=False,
                                     skip_group_check=True)
                for dc in range(DC):
                    @on("tensor")
                    def _(tc=tc, dc=dc, fg=fg, par=par, g=g5, fin=(dc == DC - 1)):
                        ins = nc.tensor.matmul(
                            psA[g % 2][:, 0:512],
                            xqfm[par][:, dc * 512 + tc * 128: dc * 512 + (tc + 1) * 128],
                            kq[:, dc * 2048 + fg * 512: dc * 2048 + (fg + 1) * 512],
                            start=False, stop=fin, skip_group_check=True)
                        if fin:
                            ins.then_inc(p5mg.h, 1)
                p5mg.inc(1)
                wt("scalar", p5mg, p5mg.n)

                @on("scalar")
                def _(fg=fg, g=g5):
                    nc.scalar.activation(zt[:, fg * 512:(fg + 1) * 512],
                                         psA[g % 2][:, 0:512], AF.Relu
                                         ).then_inc(p5ev.h, 1)
                p5ev.inc(1)

            wt("vector", p5ev, p5ev.n)
            if gi >= 1:
                wt("vector", s_out, 16 * gi)

            @on("vector")
            def _(tc=tc, gi=gi):
                v = nc.vector
                v.reduce_max(smal[:, 8:9], zt[:], axis=AX)
                v.drain()
                v.tensor_scalar_max(smal[:, 8:9], smal[:, 8:9], 1e-30)
                v.drain()
                v.reciprocal(smal[:, 9:10], smal[:, 8:9])
                v.drain()
                v.tensor_scalar_mul(smal[:, 10:11], smal[:, 9:10], 127.0)
                v.tensor_scalar_mul(smal[:, 11:12], smal[:, 8:9], -1.0 / 127.0)
                v.drain()
                v.tensor_scalar(qtt[:], zt[:], smal[:, 10:11], None, OP.mult)
                v.drain()
                v.tensor_copy(quu[:], qtt[:])
                v.drain()
                v.tensor_sub(qaa[:], qtt[:], quu[:])
                v.drain()
                v.scalar_tensor_tensor(qtt[:], qaa[:], 0.5, quu[:],
                                       OP.is_ge, OP.add)
                v.drain()
                v.scalar_tensor_tensor(qtt[:], qaa[:], 1.5, qtt[:],
                                       OP.is_ge, OP.add)
                v.drain()
                v.scalar_tensor_tensor(zt[:], qaa[:], -0.5, qtt[:],
                                       OP.is_lt, OP.subtract)
                v.drain()
                v.scalar_tensor_tensor(zt[:], qaa[:], -1.5, zt[:],
                                       OP.is_lt, OP.add)
                v.drain()
                v.tensor_scalar(yq[:], zt[:], smal[:, 11:12], None, OP.mult
                                ).then_inc(p5v.h, 1)
            p5v.inc(1)
            wt("scalar", p5v, p5v.n)

            @on("scalar")
            def _():
                nc.scalar.activation(sqt5[:], yq[:], AF.Square,
                                     accum_out=smal[:, 0:1]).then_inc(p5s.h, 1)
            p5s.inc(1)
            wt("vector", p5s, p5s.n)

            @on("vector")
            def _():
                v = nc.vector
                v.reduce_sum(smal[:, 1:2], yq[:], axis=AX)
                v.drain()
                v.tensor_scalar_mul(smal[:, 2:3], smal[:, 1:2], 1.0 / 2048)
                v.tensor_scalar_mul(smal[:, 3:4], smal[:, 0:1], 1.0 / 2048)
                v.drain()
                v.tensor_mul(smal[:, 4:5], smal[:, 2:3], smal[:, 2:3])
                v.drain()
                v.tensor_sub(smal[:, 4:5], smal[:, 3:4], smal[:, 4:5]
                             ).then_inc(p5v.h, 1)
            p5v.inc(1)
            wt("scalar", p5v, p5v.n)

            @on("scalar")
            def _():
                s = nc.scalar
                s.activation(smal[:, 5:6], smal[:, 4:5], AF.Ln, bias=epsv[:, 0:1])
                s.drain()
                s.activation(smal[:, 6:7], smal[:, 5:6], AF.Exp, scale=-0.5
                             ).then_inc(p5s.h, 1)
            p5s.inc(1)
            wt("vector", p5s, p5s.n)

            @on("vector")
            def _(tc=tc, gi=gi):
                v = nc.vector
                v.scalar_tensor_tensor(smal[:, 7:8], smal[:, 2:3], -1.0,
                                       smal[:, 6:7], OP.mult, OP.mult)
                v.drain()
                v.tensor_scalar(xmod5[:], x2q[:, tc * 2048:(tc + 1) * 2048],
                                smal[:, 7:8], None, OP.add)
                v.drain()
                v.scalar_tensor_tensor(outt[:], yq[:], smal[:, 6:7],
                                       xmod5[:], OP.mult, OP.add
                                       ).then_inc(c_out.h, 1)
            c_out.inc(1)
            wt("scalar", c_out, c_out.n)

            @on("scalar")
            def _(qt=qt, tc=tc, gi=gi):
                r0 = qt * 512 + tc * 128
                nc.scalar.dma_start(out_d[r0:r0 + 128, :], outt[:]
                                    ).then_inc(s_out.h, 16)
            s_out.inc(16)

    wt("scalar", s_out, s_out.n)

    with nc.Block() as blk:
        @blk.sync
        def _(eng):
            for f in OPS["sync"]:
                f()

        @blk.gpsimd
        def _(eng):
            for f in OPS["gpsimd"]:
                f()

        @blk.tensor
        def _(eng):
            for f in OPS["tensor"]:
                f()

        @blk.vector
        def _(eng):
            for f in OPS["vector"]:
                f()

        @blk.scalar
        def _(eng):
            for f in OPS["scalar"]:
                f()

    st5.close()
    st3a.close()
    ctx.close()
    return nc


def _arrange(inputs):
    Wqkv = np.asarray(inputs["L0_attn_Wqkv_w"], np.float32)
    Wo = np.asarray(inputs["L0_attn_out_w"], np.float32)
    bq = np.asarray(inputs["L0_attn_Wqkv_b"], np.float32)
    ob = np.asarray(inputs["L0_attn_out_b"], np.float32)
    kern = np.asarray(inputs["L0_ffn_kernel"], np.float32)
    fb = np.asarray(inputs["L0_ffn_bias"], np.float32)

    WqkvT = Wqkv.T
    wqk = np.ascontiguousarray(
        WqkvT[:, :4096].reshape(16, 128, 32, 128).transpose(1, 2, 0, 3)
        .reshape(128, 32 * 2048))
    wv = np.ascontiguousarray(
        WqkvT[:, 4096:].reshape(16, 128, 4, 512).transpose(1, 2, 0, 3)
        .reshape(128, 4 * 8192))
    wo = np.ascontiguousarray(
        Wo.T.reshape(16, 128, 2048).transpose(1, 0, 2).reshape(128, 16 * 2048))
    kn = np.ascontiguousarray(
        kern.reshape(16, 128, 2048).transpose(1, 0, 2).reshape(128, 16 * 2048))
    kt = np.ascontiguousarray(
        kern.T.reshape(16, 128, 2048).transpose(1, 0, 2).reshape(128, 16 * 2048))
    bqkv = np.ascontiguousarray(bq[:6144].reshape(48, 128).T)
    return {
        "wqk": wqk, "wv": wv, "wo": wo, "kern": kn, "kernT": kt,
        "bqkv": bqkv,
        "bvrow": np.ascontiguousarray(bq[4096:6144][None, :]),
        "obrow": np.ascontiguousarray(ob[None, :]),
        "fbrow": np.ascontiguousarray(fb[None, :]),
    }


def kernel(L0_attn_hidden_states, L0_attn_Wqkv_w, L0_attn_Wqkv_b,
           L0_attn_out_w, L0_attn_out_b, L0_ffn_kernel, L0_ffn_bias):
    x = np.asarray(L0_attn_hidden_states, np.float32)
    shared = _arrange(dict(
        L0_attn_Wqkv_w=L0_attn_Wqkv_w, L0_attn_Wqkv_b=L0_attn_Wqkv_b,
        L0_attn_out_w=L0_attn_out_w, L0_attn_out_b=L0_attn_out_b,
        L0_ffn_kernel=L0_ffn_kernel, L0_ffn_bias=L0_ffn_bias))

    nc = build_program()
    in_maps = []
    for c in range(NCORE):
        seq, half = c // 2, c % 2
        own = x[seq, half * 2048:(half + 1) * 2048, :]
        par = x[seq, (1 - half) * 2048:(2 - half) * 2048, :]
        xa = np.ascontiguousarray(np.concatenate([own, par], axis=0))
        m = {"x": xa}
        m.update(shared)
        in_maps.append(m)

    res = run_bass_kernel_spmd(nc, in_maps, core_ids=list(range(NCORE)))
    out = np.empty((4, 4096, 2048), np.float32)
    for c in range(NCORE):
        seq, half = c // 2, c % 2
        out[seq, half * 2048:(half + 1) * 2048, :] = res.results[c]["out"]
    return out


# version 15
# speedup vs baseline: 1.1096x; 1.1096x over previous
import numpy as np
from collections import deque
from contextlib import ExitStack

import concourse.bass as bass
import concourse.mybir as mybir
from concourse.bass_utils import run_bass_kernel_spmd

F32 = mybir.dt.float32
BF16 = mybir.dt.bfloat16
I32 = mybir.dt.int32
AF = mybir.ActivationFunctionType
OP = mybir.AluOpType
AX = mybir.AxisListType.X

B, S, D, H, HD = 4, 4096, 2048, 16, 128
TA, TO = 4096, 2048
DC = 16
SMS = 0.08838834764831845
NCORE = 8


class Sem:

    def __init__(self, h):
        self.h = h
        self.n = 0

    def inc(self, k=1):
        self.n += k
        return self.n


def build_program():
    nc = bass.Bass("TRN2", target_bir_lowering=False)

    x_in = nc.dram_tensor("x", [TA, D], F32, kind="ExternalInput").ap()
    wqk = nc.dram_tensor("wqk", [128, 32 * 2048], F32, kind="ExternalInput").ap()
    wv_d = nc.dram_tensor("wv", [128, 4 * 8192], F32, kind="ExternalInput").ap()
    wo_d = nc.dram_tensor("wo", [128, 16 * 2048], F32, kind="ExternalInput").ap()
    kern_d = nc.dram_tensor("kern", [128, 16 * 2048], F32, kind="ExternalInput").ap()
    kernT_d = nc.dram_tensor("kernT", [128, 16 * 2048], F32, kind="ExternalInput").ap()
    bqkv_d = nc.dram_tensor("bqkv", [128, 48], F32, kind="ExternalInput").ap()
    bvrow_d = nc.dram_tensor("bvrow", [1, 2048], F32, kind="ExternalInput").ap()
    obrow_d = nc.dram_tensor("obrow", [1, 2048], F32, kind="ExternalInput").ap()
    fbrow_d = nc.dram_tensor("fbrow", [1, 2048], F32, kind="ExternalInput").ap()
    out_d = nc.dram_tensor("out", [TO, D], F32, kind="ExternalOutput").ap()

    h1h = nc.dram_tensor("h1h", [TA, D], BF16, kind="Internal").ap()
    khh = nc.dram_tensor("khh", [H, 128, TA], BF16, kind="Internal").ap()
    vhh = nc.dram_tensor("vhh", [32, 128, 2048], BF16, kind="Internal").ap()
    qhh = nc.dram_tensor("qhh", [H, 128, TO], BF16, kind="Internal").ap()
    ctxh = nc.dram_tensor("ctxh", [H, 128, TO], BF16, kind="Internal").ap()
    x2h = nc.dram_tensor("x2h", [TO, D], BF16, kind="Internal").ap()
    xqh = nc.dram_tensor("xqh", [TO, D], BF16, kind="Internal").ap()
    rech = nc.dram_tensor("rech", [64, 512], F32, kind="Internal").ap()
    denh = nc.dram_tensor("denh", [64, 512], F32, kind="Internal").ap()
    nqh = nc.dram_tensor("nqh", [16, 2048], BF16, kind="Internal").ap()
    mkh = nc.dram_tensor("mkh", [1, 16], F32, kind="Internal").ap()
    sksh = nc.dram_tensor("sksh", [2048], F32, kind="Internal").ap()
    skrh = nc.dram_tensor("skrh", [2048], F32, kind="Internal").ap()

    OPS = {e: [] for e in ("sync", "gpsimd", "tensor", "vector", "scalar")}

    def on(eng):
        def dec(fn):
            OPS[eng].append(fn)
            return None
        return dec

    def wt(eng, sem, val=None):
        v = sem.n if val is None else val
        E = {"sync": nc.sync, "gpsimd": nc.gpsimd, "tensor": nc.tensor,
             "vector": nc.vector, "scalar": nc.scalar}[eng]
        OPS[eng].append(lambda E=E, s=sem.h, v=v: E.wait_ge(s, v))

    ctx = ExitStack()
    E = ctx.enter_context

    def sem(name):
        return Sem(E(nc.semaphore(name)))

    ph = sem("ph")
    d_x = [sem("d_x0"), sem("d_x1")]
    d_kt = [sem("d_kt0"), sem("d_kt1")]
    f_kt = [sem("f_kt0"), sem("f_kt1")]
    d_bias = sem("d_bias")
    d_bv = sem("d_bv")
    d_ob = sem("d_ob")
    s_h1 = [sem("s_h1a"), sem("s_h1b")]
    d_th = sem("d_th")
    s_sk = sem("s_sk")
    c_init = sem("c_init")
    c_bias = sem("c_bias")
    p1v = sem("p1v")
    p1s = sem("p1s")
    p1done = sem("p1done")
    d_w = [sem("d_w0"), sem("d_w1")]
    f_w = [sem("f_w0"), sem("f_w1")]
    d_wv = [sem("d_wv0"), sem("d_wv1")]
    f_wv = [sem("f_wv0"), sem("f_wv1")]
    p2mg = sem("p2mg")
    p2ev = sem("p2ev")
    s_k = [sem("s_k0"), sem("s_k1")]
    s_q = [sem("s_q0"), sem("s_q1")]
    s_v = [sem("s_v0"), sem("s_v1")]
    d_sk3 = sem("d_sk3")
    d_mk = sem("d_mk")
    p3bc = sem("p3bc")
    p3be = sem("p3be")
    d_k3 = [sem("d_k30"), sem("d_k31")]
    d_v3 = [sem("d_v30"), sem("d_v31")]
    d_q3 = [sem("d_q30"), sem("d_q31")]
    fr3 = sem("fr3")
    c_badd = sem("c_badd")
    fv3 = sem("fv3")
    d_kn = [sem("d_kn0"), sem("d_kn1")]
    f_kn = [sem("f_kn0"), sem("f_kn1")]
    c_sq3 = sem("c_sq3")
    d_nq = [sem("d_nq0"), sem("d_nq1")]
    s_nq = sem("s_nq")
    c_nm = sem("c_nm")
    c_nr = sem("c_nr")
    c_negC = sem("c_negC")
    mmS = sem("mmS")
    expd = sem("expd")
    mmPV = sem("mmPV")
    mmCtx = sem("mmCtx")
    ev3 = sem("ev3")
    s_ctx = [sem("s_ctx0"), sem("s_ctx1")]
    s_den = [sem("s_den0"), sem("s_den1")]
    c_rec = sem("c_rec")
    s_rec = sem("s_rec")
    d_cq = [sem("d_cq0"), sem("d_cq1")]
    d_rr = [sem("d_rr0"), sem("d_rr1")]
    c_bcB = sem("c_bcB")
    c_ctxn = sem("c_ctxn")
    d_wo = [sem("d_wo0"), sem("d_wo1")]
    d_xr = [sem("d_xr0"), sem("d_xr1")]
    p4mg = sem("p4mg")
    p4ev = sem("p4ev")
    p4v = sem("p4v")
    p4s = sem("p4s")
    c_x2 = sem("c_x2")
    s_x2 = [sem("s_x2a"), sem("s_x2b")]
    d_x2 = [sem("d_x2a"), sem("d_x2b")]
    c_xq = sem("c_xq")
    s_xq = [sem("s_xq0"), sem("s_xq1")]
    d_xqt = [sem("d_xqt0"), sem("d_xqt1")]
    s_h1p = d_nq
    d_th2 = d_xqt[0]
    p5mg = sem("p5mg")
    p5ev = sem("p5ev")
    p5v = sem("p5v")
    p5s = sem("p5s")
    c_out = sem("c_out")
    s_out = sem("s_out")

    def barrier(extra_sync_waits=()):
        tgt = ph.n + 5
        for (sm, v) in extra_sync_waits:
            wt("sync", sm, v)
        for eng in ("sync", "gpsimd", "tensor", "vector", "scalar"):
            ph.inc(1)
            En = {"sync": nc.sync, "gpsimd": nc.gpsimd, "tensor": nc.tensor,
                  "vector": nc.vector, "scalar": nc.scalar}[eng]
            OPS[eng].append(lambda E=En, s=ph.h: E.drain().then_inc(s, 1))
            OPS[eng].append(lambda E=En, s=ph.h, t=tgt: E.wait_ge(s, t))

    ones_bf = E(nc.sbuf_tensor("ones_bf", [128, 1], BF16))
    ones_f = E(nc.sbuf_tensor("ones_f", [1, 128], F32))
    ones_rbf = E(nc.sbuf_tensor("ones_rbf", [1, 128], BF16))
    epsv = E(nc.sbuf_tensor("epsv", [128, 1], F32))
    bqkv = E(nc.sbuf_tensor("bqkv_sb", [128, 48], F32))
    bq_bf = E(nc.sbuf_tensor("bq_bf", [1, 2048], BF16))
    smal = E(nc.sbuf_tensor("smal", [128, 16], F32))
    sm1 = E(nc.sbuf_tensor("sm1", [1, 16], F32))
    skT = E(nc.sbuf_tensor("skT", [128, 16], F32))
    sknT = E(nc.sbuf_tensor("sknT", [128, 16], F32))
    skrT = E(nc.sbuf_tensor("skrT", [128, 16], F32))
    onesm = E(nc.sbuf_tensor("onesm", [128, 128], BF16))
    nkt = E(nc.sbuf_tensor("nkt", [1, 16], F32))
    lnmk = E(nc.sbuf_tensor("lnmk", [1, 16], F32))
    wfc = [E(nc.sbuf_tensor("wfc0", [128, 2048], BF16)),
           E(nc.sbuf_tensor("wfc1", [128, 2048], BF16))]

    st12 = ExitStack()
    h1fm = st12.enter_context(nc.sbuf_tensor("h1fm", [128, 16 * 4096], BF16))
    st1 = ExitStack()
    xst = [st1.enter_context(nc.sbuf_tensor("xst0", [128, 2048], F32)),
           st1.enter_context(nc.sbuf_tensor("xst1", [128, 2048], F32))]
    h1t = [st1.enter_context(nc.sbuf_tensor("h1t0", [128, 2048], BF16)),
           st1.enter_context(nc.sbuf_tensor("h1t1", [128, 2048], BF16))]
    ktst = [st1.enter_context(nc.sbuf_tensor("ktst0", [128, 2048], F32)),
            st1.enter_context(nc.sbuf_tensor("ktst1", [128, 2048], F32))]
    sqtr = st1.enter_context(nc.sbuf_tensor("sqtr", [128, 2048], BF16))
    bqu = st1.enter_context(nc.sbuf_tensor("bqu", [1, 2048], I32))

    @on("vector")
    def _():
        nc.vector.memset(ones_bf[:], 1.0)
        nc.vector.memset(ones_f[:], 1.0)
        nc.vector.memset(ones_rbf[:], 1.0)
        nc.vector.memset(epsv[:], 1e-5)
        nc.vector.memset(smal[:], 0.0)
        nc.vector.memset(onesm[:], 1.0)
        nc.vector.memset(sm1[:], 0.0).then_inc(c_init.h, 1)
    c_init.inc(1)

    @on("sync")
    def _():
        nc.sync.dma_start(bqkv[:], bqkv_d[:]).then_inc(d_bias.h, 16)
    d_bias.inc(16)

    for j in range(16):
        if j >= 2:
            wt("sync", f_kt[j % 2], j // 2)

        @on("sync")
        def _(j=j):
            nc.sync.dma_start(ktst[j % 2][:], kernT_d[:, j * 2048:(j + 1) * 2048]
                              ).then_inc(d_kt[j % 2].h, 16)
        d_kt[j % 2].inc(16)
        wt("vector", d_kt[j % 2])

        @on("vector")
        def _(j=j):
            nc.vector.reduce_max(skT[:, j:j + 1], ktst[j % 2][:], axis=AX,
                                 apply_absolute_value=True).then_inc(f_kt[j % 2].h, 1)
        f_kt[j % 2].inc(1)

    @on("vector")
    def _():
        v = nc.vector
        v.drain()
        v.tensor_scalar_max(skT[:], skT[:], 1e-30)
        v.drain()
        v.reciprocal(skrT[:], skT[:])
        v.drain()
        v.tensor_scalar_mul(skrT[:], skrT[:], 127.0)
        v.tensor_scalar_mul(sknT[:], skT[:], -1.0 / 127.0).then_inc(p1s.h, 1)
    p1s.inc(1)

    wt("sync", f_kt[0], f_kt[0].n)
    wt("sync", f_kt[1], f_kt[1].n)

    @on("sync")
    def _():
        nc.sync.dma_start(ktst[0][0:1, :], fbrow_d[:]).then_inc(d_bias.h, 16)
    d_bias.inc(16)
    wt("vector", d_bias, d_bias.n)

    @on("vector")
    def _():
        v = nc.vector
        fb_r = ktst[0][0:1, :]
        t_r = xst[0][0:1, :]
        a_r = xst[1][0:1, :]
        u_r = bqu[0:1, :]
        v.drain()
        v.reduce_max(sm1[0:1, 0:1], fb_r, axis=AX, apply_absolute_value=True)
        v.drain()
        v.tensor_scalar_max(sm1[0:1, 0:1], sm1[0:1, 0:1], 1e-30)
        v.drain()
        v.reciprocal(sm1[0:1, 1:2], sm1[0:1, 0:1])
        v.drain()
        v.tensor_scalar_mul(sm1[0:1, 2:3], sm1[0:1, 1:2], 127.0)
        v.tensor_scalar_mul(sm1[0:1, 3:4], sm1[0:1, 0:1], -1.0 / 127.0)
        v.drain()
        v.tensor_scalar(t_r, fb_r, sm1[0:1, 2:3], None, OP.mult)
        v.drain()
        v.tensor_copy(u_r, t_r)
        v.drain()
        v.tensor_sub(a_r, t_r, u_r)
        v.drain()
        v.scalar_tensor_tensor(t_r, a_r, 0.5, u_r, OP.is_ge, OP.add)
        v.drain()
        v.scalar_tensor_tensor(a_r, a_r, -0.5, t_r, OP.is_lt, OP.subtract)
        v.drain()
        v.tensor_scalar(bq_bf[:], a_r, sm1[0:1, 3:4], None,
                        OP.mult).then_inc(c_bias.h, 1)
    c_bias.inc(1)
    wt("scalar", p1s, p1s.n)

    @on("scalar")
    def _():
        nc.scalar.dma_start(sksh.rearrange("(c p) -> p c", p=128), sknT[:],
                            allow_slow_non_contiguous=True).then_inc(s_sk.h, 16)
        nc.scalar.dma_start(skrh.rearrange("(c p) -> p c", p=128), skrT[:],
                            allow_slow_non_contiguous=True).then_inc(s_sk.h, 16)
    s_sk.inc(32)

    p1v_base = p1v.n
    for i in range(32):
        if i < 2:
            wt("sync", c_bias, 1)
        if i >= 2:
            if i < 16:
                wt("sync", s_h1[i % 2], 16 * (i // 2))
            elif i < 18:
                wt("sync", s_h1[i % 2], 128)
            else:
                wt("sync", s_h1p[i % 2], 16 * ((i - 16) // 2))
            wt("sync", p1v, p1v_base + (i - 2) * 2 + 2)

        @on("sync")
        def _(i=i):
            nc.sync.dma_start(xst[i % 2][:], x_in[i * 128:(i + 1) * 128, :]
                              ).then_inc(d_x[i % 2].h, 16)
        d_x[i % 2].inc(16)

        wt("scalar", d_x[i % 2])
        if i >= 1:
            wt("scalar", p1v, i * 2 - 1)
        wt("scalar", c_init, 1)

        @on("scalar")
        def _(i=i):
            s = nc.scalar
            s.activation(sqtr[:], xst[i % 2][:], AF.Square,
                         accum_out=smal[:, 0:1]).then_inc(p1s.h, 1)
        p1s.inc(1)

        wt("vector", d_x[i % 2])
        wt("vector", p1s, p1s.n)

        @on("vector")
        def _(i=i):
            v = nc.vector
            v.reduce_sum(smal[:, 1:2], xst[i % 2][:], axis=AX)
            v.drain()
            v.tensor_scalar_mul(smal[:, 2:3], smal[:, 1:2], 1.0 / 2048)
            v.tensor_scalar_mul(smal[:, 3:4], smal[:, 0:1], 1.0 / 2048)
            v.drain()
            v.tensor_mul(smal[:, 4:5], smal[:, 2:3], smal[:, 2:3])
            v.drain()
            v.tensor_sub(smal[:, 4:5], smal[:, 3:4], smal[:, 4:5]
                         ).then_inc(p1v.h, 1)
        p1v.inc(1)

        wt("scalar", p1v, p1v.n)

        @on("scalar")
        def _(i=i):
            s = nc.scalar
            s.activation(smal[:, 5:6], smal[:, 4:5], AF.Ln, bias=epsv[:, 0:1])
            s.drain()
            s.activation(smal[:, 6:7], smal[:, 5:6], AF.Exp, scale=-0.5
                         ).then_inc(p1s.h, 1)
        p1s.inc(1)

        wt("vector", p1s, p1s.n)

        @on("vector")
        def _(i=i):
            v = nc.vector
            v.scalar_tensor_tensor(smal[:, 7:8], smal[:, 2:3], -1.0,
                                   smal[:, 6:7], OP.mult, OP.mult)
            v.drain()
            v.tensor_scalar(h1t[i % 2][:], xst[i % 2][:], smal[:, 6:7],
                            smal[:, 7:8], OP.mult, OP.add).then_inc(p1v.h, 1)
        p1v.inc(1)

        wt("scalar", p1v, p1v.n)

        hsem = s_h1 if i < 16 else s_h1p

        @on("scalar")
        def _(i=i, hsem=hsem):
            nc.scalar.dma_start(h1h[i * 128:(i + 1) * 128, :], h1t[i % 2][:]
                                ).then_inc(hsem[i % 2].h, 16)
        hsem[i % 2].inc(16)

    @on("vector")
    def _():
        nc.vector.drain().then_inc(p1done.h, 1)
    p1done.inc(1)

    wt("sync", s_h1[0], s_h1[0].n)
    wt("sync", s_h1[1], s_h1[1].n)
    for dc in range(DC):
        @on("sync")
        def _(dc=dc):
            nc.sync.dma_start(h1fm[:, dc * 4096:dc * 4096 + 2048],
                              h1h[0:2048, dc * 128:(dc + 1) * 128],
                              transpose=True).then_inc(d_th.h, 16)
        d_th.inc(16)
    wt("sync", s_h1p[0], s_h1p[0].n)
    wt("sync", s_h1p[1], s_h1p[1].n)
    for dc in range(DC):
        @on("sync")
        def _(dc=dc):
            nc.sync.dma_start(h1fm[:, dc * 4096 + 2048:(dc + 1) * 4096],
                              h1h[2048:4096, dc * 128:(dc + 1) * 128],
                              transpose=True).then_inc(d_th2.h, 16)
        d_th2.inc(16)

    st1.close()
    st2 = ExitStack()
    wvb = [st2.enter_context(nc.sbuf_tensor("wvb0", [128, 8192], BF16))]
    bvB = st2.enter_context(nc.sbuf_tensor("bvB", [128, 2048], BF16))
    kout = [st2.enter_context(nc.sbuf_tensor("kout0", [128, 4096], BF16)),
            st2.enter_context(nc.sbuf_tensor("kout1", [128, 4096], BF16))]
    qout = [st2.enter_context(nc.sbuf_tensor("qout0", [128, 2048], BF16)),
            st2.enter_context(nc.sbuf_tensor("qout1", [128, 2048], BF16))]
    vst = [st2.enter_context(nc.sbuf_tensor("vst0", [128, 512], BF16)),
           st2.enter_context(nc.sbuf_tensor("vst1", [128, 512], BF16))]
    sqbuf = st2.enter_context(nc.sbuf_tensor("sqbuf", [128, 4096], BF16))
    nrow2 = st2.enter_context(nc.sbuf_tensor("nrow2", [1, 2048], BF16))
    bv_bf = st2.enter_context(nc.sbuf_tensor("bv_bf", [1, 2048], BF16))

    psA = [E(nc.psum_tensor("psA0", [128, 1024], F32)),
           E(nc.psum_tensor("psA1", [128, 1024], F32))]
    psB = [E(nc.psum_tensor("psB0", [128, 512], F32)),
           E(nc.psum_tensor("psB1", [128, 512], F32))]
    psC = E(nc.psum_tensor("psC", [128, 512], F32))
    psD = E(nc.psum_tensor("psD", [1, 512], F32))


    for fc in range(32):
        if fc >= 2:
            OPS["gpsimd"].append(
                lambda fc=fc: nc.gpsimd.wait_ge(p2mg.h, fw_rel[fc - 2]))

        @on("gpsimd")
        def _(fc=fc):
            nc.gpsimd.dma_start(wfc[fc % 2][:], wqk[:, fc * 2048:(fc + 1) * 2048]
                                ).then_inc(d_w[fc % 2].h, 16)
        d_w[fc % 2].inc(16)

    wt("gpsimd", p1done, 1)

    @on("gpsimd")
    def _():
        nc.gpsimd.dma_start(bv_bf[:], bvrow_d[:]).then_inc(d_bv.h, 16)
    d_bv.inc(16)
    for vg in range(4):
        if vg >= 1:
            OPS["gpsimd"].append(
                lambda vg=vg: nc.gpsimd.wait_ge(p2mg.h, fwv_rel[vg - 1]))

        @on("gpsimd")
        def _(vg=vg):
            nc.gpsimd.dma_start(wvb[0][:], wv_d[:, vg * 8192:(vg + 1) * 8192]
                                ).then_inc(d_wv[vg % 2].h, 16)
        d_wv[vg % 2].inc(16)

    wt("vector", s_h1p[0], s_h1p[0].n)
    wt("vector", s_h1p[1], s_h1p[1].n)

    g2 = 0
    fw_rel = {}
    fwv_rel = {}

    aslots = [(psA[0], 0), (psA[0], 512), (psA[1], 0), (psA[1], 512)]
    pend = {"fn": None}
    nr_end = {"v": 0}

    def flush_norm():
        if pend["fn"] is not None:
            fn = pend["fn"]
            pend["fn"] = None
            fn()

    for fc in range(32):
        wt("tensor", d_w[fc % 2], 16 * (fc // 2 + 1))
        ntt = 4 if fc < 16 else 8
        is_q = fc < 16
        head = fc if is_q else fc - 16
        obuf = qout if is_q else kout
        sst = s_q if is_q else s_k
        par = head % 2
        wt("vector", sst[par], sst[par].n)
        for tt in range(ntt):
            g2 += 1
            wt("tensor", d_th, 256)
            if tt >= 4:
                wt("tensor", d_th2, 256)
            if g2 > 2:
                wt("tensor", p2ev, g2 - 2)
            last = (tt == ntt - 1)
            for dc in range(DC):
                @on("tensor")
                def _(fc=fc, tt=tt, dc=dc, g=g2, last=last):
                    ins = nc.tensor.matmul(
                        psB[g % 2][:],
                        wfc[fc % 2][:, dc * 128:(dc + 1) * 128],
                        h1fm[:, dc * 4096 + tt * 512: dc * 4096 + (tt + 1) * 512],
                        start=(dc == 0), stop=(dc == DC - 1))
                    if dc == DC - 1:
                        ins.then_inc(p2mg.h, 1)
            p2mg.inc(1)
            if last:
                fw_rel[fc] = p2mg.n

            wt("vector", p2mg, p2mg.n)

            @on("vector")
            def _(fc=fc, tt=tt, g=g2, obuf=obuf, par=par):
                nc.vector.tensor_scalar(
                    obuf[par][:, tt * 512:(tt + 1) * 512], psB[g % 2][:],
                    bqkv[:, fc:fc + 1], None, OP.add).then_inc(p2ev.h, 1)
            p2ev.inc(1)
        wt("scalar", p2ev, p2ev.n)

        @on("scalar")
        def _(head=head, is_q=is_q, par=par, obuf=obuf):
            tgt = qhh[head] if is_q else khh[head]
            nc.scalar.dma_start(tgt[:, :], obuf[par][:]
                                ).then_inc((s_q if is_q else s_k)[par].h, 16)
        sst[par].inc(16)

        flush_norm()
        nm_after = c_nm.n
        if is_q and s_nq.n > 0:
            wt("vector", s_nq, s_nq.n)
        wt("vector", c_nm, nm_after)
        sq_rel = []
        for tt in range(ntt):
            @on("vector")
            def _(tt=tt, par=par, obuf=obuf):
                v = nc.vector
                v.drain()
                v.tensor_mul(sqbuf[:, tt * 512:(tt + 1) * 512],
                             obuf[par][:, tt * 512:(tt + 1) * 512],
                             obuf[par][:, tt * 512:(tt + 1) * 512]
                             ).then_inc(c_sq3.h, 1)
            sq_rel.append(c_sq3.inc(1))

        def mk_flush(head=head, is_q=is_q, ntt=ntt, sq_rel=tuple(sq_rel),
                     nr_prev=nr_end["v"]):
            def fn():
                nrel = {}
                for tt in range(ntt):
                    wt("tensor", c_sq3, sq_rel[tt])
                    if tt == 0 and nr_prev > 0:
                        wt("tensor", c_nr, nr_prev)
                    if tt >= 4:
                        wt("tensor", c_nr, nrel[tt - 4])

                    @on("tensor")
                    def _(tt=tt):
                        pa, off = aslots[tt % 4]
                        nc.tensor.matmul(pa[0:1, off:off + 512],
                                         ones_bf[:, 0:1],
                                         sqbuf[:, tt * 512:(tt + 1) * 512],
                                         start=True, stop=True
                                         ).then_inc(c_nm.h, 1)
                    c_nm.inc(1)
                    wt("vector", c_nm, c_nm.n)

                    @on("vector")
                    def _(tt=tt, is_q=is_q):
                        v = nc.vector
                        pa, off = aslots[tt % 4]
                        v.drain()
                        if is_q:
                            v.tensor_copy(nrow2[0:1, tt * 512:(tt + 1) * 512],
                                          pa[0:1, off:off + 512]
                                          ).then_inc(c_nr.h, 1)
                        else:
                            v.reduce_max(nkt[0:1, tt:tt + 1],
                                         pa[0:1, off:off + 512],
                                         axis=AX).then_inc(c_nr.h, 1)
                    nrel[tt] = c_nr.inc(1)
                if is_q:
                    @on("vector")
                    def _():
                        v = nc.vector
                        v.drain()
                        v.tensor_scalar_max(nrow2[0:1, :], nrow2[0:1, :], 1e-6
                                            ).then_inc(c_nr.h, 1)
                    c_nr.inc(1)
                    wt("scalar", c_nr, c_nr.n)
                    if head >= 2:
                        wt("scalar", s_nq, s_nq.n - 16)

                    @on("scalar")
                    def _(head=head):
                        nc.scalar.dma_start(nqh[head:head + 1, :],
                                            nrow2[0:1, :]).then_inc(s_nq.h, 16)
                    s_nq.inc(16)
                else:
                    @on("vector")
                    def _(head=head):
                        v = nc.vector
                        v.drain()
                        v.reduce_max(sm1[0:1, head:head + 1], nkt[0:1, 0:8],
                                     axis=AX).then_inc(c_nr.h, 1)
                    c_nr.inc(1)
                nr_end["v"] = c_nr.n
            return fn
        pend["fn"] = mk_flush()

    flush_norm()

    @on("vector")
    def _():
        v = nc.vector
        v.drain()
        v.tensor_scalar_max(sm1[0:1, 0:16], sm1[0:1, 0:16], 1e-12
                            ).then_inc(c_nr.h, 1)
    c_nr.inc(1)
    wt("scalar", c_nr, c_nr.n)

    @on("scalar")
    def _():
        nc.scalar.dma_start(mkh[0:1, :], sm1[0:1, 0:16]).then_inc(s_nq.h, 16)
    s_nq.inc(16)

    wt("tensor", d_th, 256)
    wt("tensor", d_th2, 256)
    wt("tensor", c_bias, 1)
    wt("tensor", d_bias, d_bias.n)
    wt("tensor", d_bv, 16)
    for i in range(4):
        g2 += 1
        if g2 > 2:
            wt("tensor", p2ev, g2 - 2)

        @on("tensor")
        def _(i=i, g=g2):
            nc.tensor.matmul(psB[g % 2][:], ones_rbf[0:1, :],
                             bv_bf[0:1, i * 512:(i + 1) * 512],
                             start=True, stop=True).then_inc(p2mg.h, 1)
        p2mg.inc(1)
        wt("vector", p2mg, p2mg.n)

        @on("vector")
        def _(i=i, g=g2):
            nc.vector.tensor_scalar(bvB[:, i * 512:(i + 1) * 512], psB[g % 2][:],
                                    1.0 / 128.0, None, OP.mult
                                    ).then_inc(p2ev.h, 1)
        p2ev.inc(1)
    vgrp = 0
    for vg in range(4):
        wt("tensor", d_wv[vg % 2], 16 * (vg // 2 + 1))
        for tc in range(32):
            g2 += 1
            vgrp += 1
            if g2 > 2:
                wt("tensor", p2ev, g2 - 2)
            wt("vector", s_v[vgrp % 2], s_v[vgrp % 2].n)
            @on("tensor")
            def _(vg=vg, g=g2):
                nc.tensor.matmul(psB[g % 2][:], onesm[:],
                                 bvB[:, vg * 512:(vg + 1) * 512],
                                 start=True, stop=False, skip_group_check=True)
            for dc in range(DC):
                @on("tensor")
                def _(vg=vg, tc=tc, dc=dc, g=g2, fin=(dc == DC - 1)):
                    ins = nc.tensor.matmul(
                        psB[g % 2][:],
                        h1fm[:, dc * 4096 + tc * 128: dc * 4096 + (tc + 1) * 128],
                        wvb[0][:, dc * 512:(dc + 1) * 512],
                        start=False, stop=fin, skip_group_check=True)
                    if fin:
                        ins.then_inc(p2mg.h, 1)
            p2mg.inc(1)
            if tc == 31:
                fwv_rel[vg] = p2mg.n

            wt("vector", p2mg, p2mg.n)

            @on("vector")
            def _(g=g2, vp=vgrp % 2):
                nc.vector.tensor_copy(vst[vp][:], psB[g % 2][:]
                                      ).then_inc(p2ev.h, 1)
            p2ev.inc(1)
            wt("scalar", p2ev, p2ev.n)

            @on("scalar")
            def _(vg=vg, tc=tc, vp=vgrp % 2):
                nc.scalar.dma_start(vhh[tc][:, vg * 512:(vg + 1) * 512],
                                    vst[vp][:]).then_inc(s_v[vp].h, 16)
            s_v[vgrp % 2].inc(16)

    barrier(extra_sync_waits=[(s_k[0], s_k[0].n), (s_k[1], s_k[1].n),
                              (s_q[0], s_q[0].n), (s_q[1], s_q[1].n),
                              (s_v[0], s_v[0].n), (s_v[1], s_v[1].n),
                              (s_sk, 32)])

    st2.close()
    st12.close()
    st3a = ExitStack()
    kq = st3a.enter_context(nc.sbuf_tensor("kq", [128, 16 * 2048], BF16))
    st3 = ExitStack()
    Kh = [st3.enter_context(nc.sbuf_tensor("Kh0", [128, 4096], BF16)),
          st3.enter_context(nc.sbuf_tensor("Kh1", [128, 4096], BF16))]
    Vh = [st3.enter_context(nc.sbuf_tensor("Vh0", [128, 4096], BF16)),
          st3.enter_context(nc.sbuf_tensor("Vh1", [128, 4096], BF16))]
    Qh = [st3.enter_context(nc.sbuf_tensor("Qh0", [128, 2048], BF16)),
          st3.enter_context(nc.sbuf_tensor("Qh1", [128, 2048], BF16))]
    PT = [st3.enter_context(nc.sbuf_tensor("PT0", [128, 2048], BF16)),
          st3.enter_context(nc.sbuf_tensor("PT1", [128, 2048], BF16))]
    negCB = st3.enter_context(nc.sbuf_tensor("negCB", [128, 512], BF16))
    ctxst = [st3.enter_context(nc.sbuf_tensor("ctxst0", [128, 2048], BF16)),
             st3.enter_context(nc.sbuf_tensor("ctxst1", [128, 2048], BF16))]
    nq_sb = st3.enter_context(nc.sbuf_tensor("nq_sb", [1, 2048], BF16))
    dstg = st3.enter_context(nc.sbuf_tensor("dstg", [64, 512], F32))
    rcps = st3.enter_context(nc.sbuf_tensor("rcps", [64, 512], F32))
    den_sb = [st3.enter_context(nc.sbuf_tensor("den_sb0", [1, 512], F32)),
              st3.enter_context(nc.sbuf_tensor("den_sb1", [1, 512], F32))]
    SKnb = st3.enter_context(nc.sbuf_tensor("SKnb", [128, 2048], F32))
    RSKb = st3.enter_context(nc.sbuf_tensor("RSKb", [128, 2048], F32))
    kst = [st3.enter_context(nc.sbuf_tensor("kst0", [128, 2048], F32))]
    kst.append(kst[0])
    kqt = st3.enter_context(nc.sbuf_tensor("kqt", [128, 2048], F32))
    negCrow = [st3.enter_context(nc.sbuf_tensor("negCrow0", [1, 2048], BF16)),
               st3.enter_context(nc.sbuf_tensor("negCrow1", [1, 2048], BF16))]
    skrow = kst[0][0:1, :]
    skrow2 = kqt[0:1, :]
    kqu = st3.enter_context(nc.sbuf_tensor("kqu", [128, 2048], I32))
    kqa = st3.enter_context(nc.sbuf_tensor("kqa", [128, 2048], F32))

    @on("sync")
    def _():
        nc.sync.dma_start(skrow, sksh.rearrange("(o f) -> o f", o=1)
                          ).then_inc(d_sk3.h, 16)
        nc.sync.dma_start(skrow2, skrh.rearrange("(o f) -> o f", o=1)
                          ).then_inc(d_sk3.h, 16)
    d_sk3.inc(32)
    wt("tensor", d_sk3, 32)
    for i in range(8):
        src = skrow if i < 4 else skrow2
        dstt = SKnb if i < 4 else RSKb
        col = (i % 4) * 512
        if i >= 2:
            wt("tensor", p3be, i - 1)

        @on("tensor")
        def _(i=i, src=src, col=col):
            nc.tensor.matmul(psB[i % 2][:], ones_f[0:1, :],
                             src[0:1, col:col + 512], start=True, stop=True
                             ).then_inc(p3bc.h, 1)
        p3bc.inc(1)
        wt("vector", p3bc, p3bc.n)

        @on("vector")
        def _(i=i, dstt=dstt, col=col):
            nc.vector.tensor_copy(dstt[:, col:col + 512], psB[i % 2][:]
                                  ).then_inc(p3be.h, 1)
        p3be.inc(1)

    wt("sync", s_nq, s_nq.n)

    @on("sync")
    def _():
        nc.sync.dma_start(nkt[0:1, :], mkh[0:1, :]).then_inc(d_mk.h, 16)
    d_mk.inc(16)
    wt("scalar", d_mk, d_mk.n)

    @on("scalar")
    def _():
        nc.scalar.activation(lnmk[0:1, :], nkt[0:1, :], AF.Ln
                             ).then_inc(p1s.h, 1)
    p1s.inc(1)
    wt("vector", p1s, p1s.n)

    @on("vector")
    def _():
        nc.vector.drain()
        nc.vector.tensor_scalar_mul(lnmk[0:1, :], lnmk[0:1, :], 0.5
                                    ).then_inc(c_nr.h, 1)
    c_nr.inc(1)

    def emit_cchain(hh):
        par2 = hh % 2
        if hh >= 1:
            wt("sync", c_negC, hh)
        wt("sync", c_nr, 1)

        @on("sync")
        def _(hh=hh):
            nc.sync.dma_start(nq_sb[0:1, :], nqh[hh:hh + 1, :]
                              ).then_inc(d_nq[hh % 2].h, 16)
        d_nq[hh % 2].inc(16)
        wt("scalar", d_nq[hh % 2], d_nq[hh % 2].n)

        @on("scalar")
        def _():
            nc.scalar.activation(kqa[0:1, :], nq_sb[0:1, :], AF.Ln
                                 ).then_inc(p1s.h, 1)
        p1s.inc(1)
        wt("vector", p1s, p1s.n)

        @on("vector")
        def _(hh=hh):
            v = nc.vector
            v.drain()
            v.tensor_scalar(kqt[0:1, :], kqa[0:1, :], 0.5,
                            lnmk[0:1, hh:hh + 1], OP.mult, OP.add
                            ).then_inc(c_sq3.h, 1)
        c_sq3.inc(1)
        wt("scalar", c_sq3, c_sq3.n)

        @on("scalar")
        def _():
            nc.scalar.activation(kqa[0:1, :], kqt[0:1, :], AF.Exp
                                 ).then_inc(p1s.h, 1)
        p1s.inc(1)
        wt("vector", p1s, p1s.n)

        @on("vector")
        def _(par2=par2):
            nc.vector.drain()
            nc.vector.tensor_scalar(negCrow[par2][0:1, :], kqa[0:1, :],
                                    -1.001 / 128.0, -1.0 / 128.0,
                                    OP.mult, OP.add).then_inc(c_negC.h, 1)
        c_negC.inc(1)

    emit_cchain(0)

    smm_rel = {}
    pvh_rel = {}
    for h in range(H):
        par = h % 2
        if h >= 2:
            wt("sync", mmS, smm_rel[h - 2])
            wt("sync", mmPV, pvh_rel[h - 2])

        @on("sync")
        def _(h=h, par=par):
            nc.sync.dma_start(Kh[par][:], khh[h][:, :]).then_inc(d_k3[par].h, 16)
            nc.sync.dma_start(Vh[par][:],
                              vhh[:, :, h * 128:(h + 1) * 128].transpose([1, 0, 2])
                              ).then_inc(d_v3[par].h, 16)
            nc.sync.dma_start(Qh[par][:], qhh[h][:, :]).then_inc(d_q3[par].h, 16)
        d_k3[par].inc(16)
        d_v3[par].inc(16)
        d_q3[par].inc(16)

        if h <= 1:
            wt("sync", p3be, 8)
        if h >= 1:
            wt("sync", f_kn[0], h)

        @on("sync")
        def _(h=h):
            nc.sync.dma_start(kst[0][:], kern_d[:, h * 2048:(h + 1) * 2048]
                              ).then_inc(d_kn[0].h, 16)
        d_kn[0].inc(16)

        quantq = deque()

        def _qp1(h=h):
            wt("vector", d_kn[0], 16 * (h + 1))
            wt("vector", p3be, p3be.n)

            @on("vector")
            def _():
                v = nc.vector
                v.tensor_mul(kqt[:], kst[0][:], RSKb[:])
                v.drain()

        def _qp2():
            @on("vector")
            def _():
                v = nc.vector
                v.tensor_copy(kqu[:], kqt[:])
                v.drain()

        def _qp3():
            @on("vector")
            def _():
                v = nc.vector
                v.tensor_sub(kqa[:], kqt[:], kqu[:])
                v.drain()

        def _qp4():
            @on("vector")
            def _():
                v = nc.vector
                v.scalar_tensor_tensor(kqt[:], kqa[:], 0.5, kqu[:],
                                       OP.is_ge, OP.add)
                v.drain()

        def _qp5():
            @on("vector")
            def _():
                v = nc.vector
                v.scalar_tensor_tensor(kqa[:], kqa[:], -0.5, kqt[:], OP.is_lt,
                                       OP.subtract)
                v.drain()

        def _qp6(h=h):
            @on("vector")
            def _(h=h):
                nc.vector.tensor_mul(kq[:, h * 2048:(h + 1) * 2048], kqa[:],
                                     SKnb[:]).then_inc(f_kn[0].h, 1)
            f_kn[0].inc(1)
        for _p in (_qp1, _qp2, _qp3, _qp4, _qp5, _qp6):
            quantq.append(_p)

        wt("tensor", c_negC, c_negC.n)
        wt("tensor", d_v3[par], 16 * (h // 2 + 1))
        wt("tensor", d_k3[par], 16 * (h // 2 + 1))
        wt("tensor", d_q3[par], 16 * (h // 2 + 1))
        exp_rel = {-1: 0}
        pv_rel = {-2: 0, -1: 0}
        def emit_pvden(qt, ss):
            sp = ss % 2
            first = (ss == 0)
            last = (ss == 7)
            wt("tensor", expd, exp_rel[ss])
            for j in range(4):
                kc = ss * 4 + j

                @on("tensor")
                def _(par=par, kc=kc, j=j, first=first, last=last, sp=sp):
                    nc.tensor.matmul(
                        psC[:], Vh[par][:, kc * 128:(kc + 1) * 128],
                        PT[sp][:, j * 512:(j + 1) * 512],
                        start=(first and j == 0), stop=(last and j == 3),
                        skip_group_check=True)
            for j in range(4):
                @on("tensor")
                def _(j=j, sp=sp, first=first, last=last, fin=(j == 3)):
                    ins = nc.tensor.matmul(
                        psD[0:1, :], ones_bf[:, 0:1],
                        PT[sp][:, j * 512:(j + 1) * 512],
                        start=(first and j == 0), stop=(last and j == 3),
                        skip_group_check=True)
                    if fin:
                        ins.then_inc(mmPV.h, 1)
            pv_rel[ss] = mmPV.inc(1)
            if qt == 3 and ss == 7:
                pvh_rel[h] = mmPV.n

        halves = None
        for qt in range(4):
            wt("tensor", ev3, ev3.n)
            wt("tensor", c_negC, h + 1)

            @on("tensor")
            def _(qt=qt, par=par):
                nc.tensor.matmul(psB[0][:], ones_rbf[0:1, :],
                                 negCrow[par][0:1, qt * 512:(qt + 1) * 512],
                                 start=True, stop=True).then_inc(c_nm.h, 1)
            c_nm.inc(1)
            wt("vector", c_nm, c_nm.n)

            @on("vector")
            def _():
                v = nc.vector
                v.drain()
                v.tensor_copy(negCB[:], psB[0][:]).then_inc(c_nr.h, 1)
            c_nr.inc(1)
            wt("tensor", c_nr, c_nr.n)

            hv = [(psA[0], 0), (psA[0], 512), (psA[1], 0), (psA[1], 512)]
            for ss in range(8):
                sp = ss % 2
                wt("tensor", expd, exp_rel.get(ss - 1, 0))
                for j in range(4):
                    @on("tensor")
                    def _(j=j, hv=hv):
                        pa, off = hv[j]
                        nc.tensor.matmul(pa[:, off:off + 512], onesm[:],
                                         negCB[:], start=True, stop=False,
                                         skip_group_check=True)
                for j in range(4):
                    kc = ss * 4 + j

                    @on("tensor")
                    def _(par=par, qt=qt, kc=kc, j=j, fin=(j == 3), hv=hv):
                        pa, off = hv[j]
                        ins = nc.tensor.matmul(
                            pa[:, off:off + 512],
                            Kh[par][:, kc * 128:(kc + 1) * 128],
                            Qh[par][:, qt * 512:(qt + 1) * 512],
                            start=False, stop=True, skip_group_check=True)
                        if fin:
                            ins.then_inc(mmS.h, 1)
                mmS.inc(1)
                if qt == 3 and ss == 7:
                    smm_rel[h] = mmS.n
                if qt >= 2 and quantq:
                    quantq.popleft()()

                wt("scalar", mmS, mmS.n)
                wt("scalar", mmPV, pv_rel.get(ss - 2, 0))

                @on("scalar")
                def _(sp=sp):
                    nc.scalar.activation(PT[sp][:, 0:1024], psA[0][:], AF.Exp,
                                         scale=SMS)
                    nc.scalar.activation(PT[sp][:, 1024:2048], psA[1][:],
                                         AF.Exp, scale=SMS).then_inc(expd.h, 1)
                exp_rel[ss] = expd.inc(1)

                if ss >= 1:
                    emit_pvden(qt, ss - 1)
            emit_pvden(qt, 7)
            if qt == 1 and h + 1 < H:
                emit_cchain(h + 1)
            exp_rel = {-1: exp_rel[7]}
            pv_rel = {-2: pv_rel[6], -1: pv_rel[7]}

            dpar = (h * 4 + qt) % 2
            wt("vector", mmPV, mmPV.n)
            if qt == 0:
                wt("vector", s_ctx[par], s_ctx[par].n)
            if h * 4 + qt >= 2:
                wt("vector", s_den[dpar], s_den[dpar].n)

            @on("vector")
            def _(h=h, qt=qt, par=par, dpar=dpar):
                v = nc.vector
                v.tensor_copy(ctxst[par][:, qt * 512:(qt + 1) * 512], psC[:])
                v.tensor_copy(den_sb[dpar][0:1, :], psD[0:1, :])
                v.drain().then_inc(ev3.h, 1)
            ev3.inc(1)
            wt("scalar", ev3, ev3.n)

            @on("scalar")
            def _(h=h, qt=qt, dpar=dpar):
                nc.scalar.dma_start(denh[h * 4 + qt:h * 4 + qt + 1, :],
                                    den_sb[dpar][0:1, :]
                                    ).then_inc(s_den[dpar].h, 16)
            s_den[dpar].inc(16)

        wt("scalar", ev3, ev3.n)

        @on("scalar")
        def _(h=h, par=par):
            nc.scalar.dma_start(ctxh[h][:, :], ctxst[par][:]
                                ).then_inc(s_ctx[par].h, 16)
        s_ctx[par].inc(16)

        while quantq:
            quantq.popleft()()

    wt("sync", s_den[0], s_den[0].n)
    wt("sync", s_den[1], s_den[1].n)

    @on("sync")
    def _():
        nc.sync.dma_start(dstg[:], denh[:, :]).then_inc(d_sk3.h, 16)
    d_sk3.inc(16)
    wt("vector", d_sk3, d_sk3.n)

    @on("vector")
    def _():
        v = nc.vector
        v.reciprocal(rcps[:], dstg[:])
        v.drain().then_inc(c_rec.h, 1)
    c_rec.inc(1)
    wt("scalar", c_rec, c_rec.n)

    @on("scalar")
    def _():
        nc.scalar.dma_start(rech.rearrange("(q h) f -> h q f", h=16), rcps[:]
                            ).then_inc(s_rec.h, 16)
    s_rec.inc(16)

    barrier(extra_sync_waits=[(s_ctx[0], s_ctx[0].n), (s_ctx[1], s_ctx[1].n),
                              (s_rec, 16)])
    st3.close()
    st4 = ExitStack()
    ctxq = [st4.enter_context(nc.sbuf_tensor("ctxq0", [128, 8192], BF16)),
            st4.enter_context(nc.sbuf_tensor("ctxq1", [128, 8192], BF16))]
    recrow = [st4.enter_context(nc.sbuf_tensor("recrow0", [1, 1024], F32))]
    recrow.append(recrow[0])
    wob = st4.enter_context(nc.sbuf_tensor("wob", [128, 32768], BF16))
    y1 = st4.enter_context(nc.sbuf_tensor("y1", [128, 2048], F32))
    xres = [st4.enter_context(nc.sbuf_tensor("xres0", [128, 2048], BF16)),
            st4.enter_context(nc.sbuf_tensor("xres1", [128, 2048], BF16))]
    obB = st4.enter_context(nc.sbuf_tensor("obB", [128, 2048], BF16))
    sqt4 = st4.enter_context(nc.sbuf_tensor("sqt4", [128, 2048], BF16))
    x2st = [st4.enter_context(nc.sbuf_tensor("x2st0", [128, 2048], BF16))]
    x2st.append(x2st[0])
    ob_bf = sqt4[0:1, :]

    @on("gpsimd")
    def _():
        nc.gpsimd.dma_start(ob_bf[:], obrow_d[:]).then_inc(d_ob.h, 16)
    d_ob.inc(16)
    for fg in range(4):
        @on("gpsimd")
        def _(fg=fg):
            nc.gpsimd.dma_start(
                wob[:, fg * 8192:(fg + 1) * 8192],
                wo_d.rearrange("p (dc f) -> p dc f", dc=16)[:, :,
                                                            fg * 512:(fg + 1) * 512]
                ).then_inc(d_wo[fg // 2].h, 16)
        d_wo[fg // 2].inc(16)
    wt("tensor", d_ob, 16)
    for i in range(4):
        if i >= 2:
            wt("tensor", p4ev, p4ev.n - 1)

        @on("tensor")
        def _(i=i):
            nc.tensor.matmul(psB[i % 2][:], ones_rbf[0:1, :],
                             ob_bf[0:1, i * 512:(i + 1) * 512],
                             start=True, stop=True).then_inc(p4mg.h, 1)
        p4mg.inc(1)
        wt("vector", p4mg, p4mg.n)

        @on("vector")
        def _(i=i):
            nc.vector.tensor_scalar(obB[:, i * 512:(i + 1) * 512], psB[i % 2][:],
                                    1.0 / 128.0, None, OP.mult
                                    ).then_inc(p4ev.h, 1)
        p4ev.inc(1)
    wt("tensor", p4ev, p4ev.n)
    p4ev_base = p4ev.n

    g4 = 0
    rr_rel = {}
    for qt in range(4):
        par = qt % 2
        if qt >= 2:
            wt("sync", c_ctxn, 16 * (qt - 1))

        @on("sync")
        def _(qt=qt, par=par):
            nc.sync.dma_start(
                ctxq[par][:],
                ctxh[:, :, qt * 512:(qt + 1) * 512].transpose([1, 0, 2])
                ).then_inc(d_cq[par].h, 16)
        d_cq[par].inc(16)

        wt("tensor", d_cq[par], 16 * (qt // 2 + 1))
        wt("vector", d_cq[par], 16 * (qt // 2 + 1))
        bnk3 = [psB[0], psB[1], psC]
        for h in range(H):
            if h % 2 == 0:
                hp = h // 2
                if qt * 8 + hp >= 1:
                    wt("sync", c_nm, rr_rel["last"])

                @on("sync")
                def _(qt=qt, hp=hp):
                    nc.sync.dma_start(recrow[hp % 2][:],
                                      rech[qt * 16 + hp * 2: qt * 16 + hp * 2 + 2, :]
                                      ).then_inc(d_rr[hp % 2].h, 16)
                d_rr[hp % 2].inc(16)
            wt("tensor", d_rr[(h // 2) % 2], d_rr[(h // 2) % 2].n)
            if h >= 3 or qt >= 1:
                wt("tensor", c_ctxn, max(c_ctxn.n - 2, 0))

            @on("tensor")
            def _(h=h, qt=qt):
                nc.tensor.matmul(bnk3[h % 3][:],
                                 ones_f[0:1, :],
                                 recrow[(h // 2) % 2][0:1, (h % 2) * 512:(h % 2) * 512 + 512],
                                 start=True, stop=True).then_inc(c_nm.h, 1)
            c_nm.inc(1)
            if h % 2 == 1:
                rr_rel["last"] = c_nm.n
            wt("vector", c_nm, c_nm.n)

            @on("vector")
            def _(h=h, par=par):
                nc.vector.tensor_mul(ctxq[par][:, h * 512:(h + 1) * 512],
                                     ctxq[par][:, h * 512:(h + 1) * 512],
                                     bnk3[h % 3][:]).then_inc(c_ctxn.h, 1)
            c_ctxn.inc(1)

        wt("tensor", c_ctxn, c_ctxn.n)
        for tc in range(4):
            gi = qt * 4 + tc
            if gi >= 2:
                wt("gpsimd", c_x2, gi - 1)

            @on("gpsimd")
            def _(qt=qt, tc=tc, gi=gi):
                r0 = qt * 512 + tc * 128
                nc.gpsimd.dma_start(xres[gi % 2][:], x_in[r0:r0 + 128, :]
                                    ).then_inc(d_xr[gi % 2].h, 16)
            d_xr[gi % 2].inc(16)

            for fg in range(4):
                g4 += 1
                wt("tensor", d_wo[fg // 2], 32)
                if g4 > 2:
                    wt("tensor", p4ev, p4ev_base + g4 - 2)
                @on("tensor")
                def _(fg=fg, g=g4):
                    nc.tensor.matmul(psA[g % 2][:, 0:512], onesm[:],
                                     obB[:, fg * 512:(fg + 1) * 512],
                                     start=True, stop=False,
                                     skip_group_check=True)
                for dc in range(DC):
                    @on("tensor")
                    def _(tc=tc, dc=dc, g=g4, fg=fg, par=par, fin=(dc == DC - 1)):
                        ins = nc.tensor.matmul(
                            psA[g % 2][:, 0:512],
                            ctxq[par][:, dc * 512 + tc * 128: dc * 512 + (tc + 1) * 128],
                            wob[:, fg * 8192 + dc * 512:fg * 8192 + (dc + 1) * 512],
                            start=False, stop=fin, skip_group_check=True)
                        if fin:
                            ins.then_inc(p4mg.h, 1)
                p4mg.inc(1)

                wt("vector", p4mg, p4mg.n)
                if fg == 0 and gi >= 1:
                    wt("vector", p4s, p4s.n)

                @on("vector")
                def _(fg=fg, g=g4):
                    if fg == 0:
                        nc.vector.drain()
                    nc.vector.tensor_copy(y1[:, fg * 512:(fg + 1) * 512],
                                          psA[g % 2][:, 0:512]
                                          ).then_inc(p4ev.h, 1)
                p4ev.inc(1)

            wt("scalar", p4ev, p4ev.n)
            if gi >= 1:
                wt("scalar", p4v, p4v.n)

            @on("scalar")
            def _():
                nc.scalar.activation(sqt4[:], y1[:], AF.Square,
                                     accum_out=smal[:, 0:1]).then_inc(p4s.h, 1)
            p4s.inc(1)
            wt("vector", p4s, p4s.n)

            @on("vector")
            def _():
                v = nc.vector
                v.reduce_sum(smal[:, 1:2], y1[:], axis=AX)
                v.drain()
                v.tensor_scalar_mul(smal[:, 2:3], smal[:, 1:2], 1.0 / 2048)
                v.tensor_scalar_mul(smal[:, 3:4], smal[:, 0:1], 1.0 / 2048)
                v.drain()
                v.tensor_mul(smal[:, 4:5], smal[:, 2:3], smal[:, 2:3])
                v.drain()
                v.tensor_sub(smal[:, 4:5], smal[:, 3:4], smal[:, 4:5]
                             ).then_inc(p4v.h, 1)
            p4v.inc(1)
            wt("scalar", p4v, p4v.n)

            @on("scalar")
            def _():
                s = nc.scalar
                s.activation(smal[:, 5:6], smal[:, 4:5], AF.Ln, bias=epsv[:, 0:1])
                s.drain()
                s.activation(smal[:, 6:7], smal[:, 5:6], AF.Exp, scale=-0.5
                             ).then_inc(p4s.h, 1)
            p4s.inc(1)
            wt("vector", p4s, p4s.n)
            wt("vector", d_xr[gi % 2], d_xr[gi % 2].n)
            if gi >= 1:
                wt("vector", s_x2[(gi + 1) % 2], s_x2[(gi + 1) % 2].n)

            @on("vector")
            def _(gi=gi):
                v = nc.vector
                v.tensor_scalar(y1[:], y1[:], smal[:, 2:3], None, OP.subtract)
                v.drain()
                v.scalar_tensor_tensor(x2st[gi % 2][:], y1[:], smal[:, 6:7],
                                       xres[gi % 2][:], OP.mult, OP.add
                                       ).then_inc(c_x2.h, 1)
            c_x2.inc(1)
            wt("scalar", c_x2, c_x2.n)

            @on("scalar")
            def _(qt=qt, tc=tc, gi=gi):
                r0 = qt * 512 + tc * 128
                nc.scalar.dma_start(x2h[r0:r0 + 128, :], x2st[gi % 2][:]
                                    ).then_inc(s_x2[gi % 2].h, 16)
            s_x2[gi % 2].inc(16)

    barrier(extra_sync_waits=[(s_x2[0], s_x2[0].n), (s_x2[1], s_x2[1].n)])
    st4.close()
    st5 = ExitStack()
    x2q = st5.enter_context(nc.sbuf_tensor("x2q", [128, 4 * 2048], BF16))
    h2 = st5.enter_context(nc.sbuf_tensor("h2", [128, 2048], F32))
    qtt = st5.enter_context(nc.sbuf_tensor("qtt", [128, 2048], F32))
    quu = st5.enter_context(nc.sbuf_tensor("quu", [128, 2048], I32))
    qaa = st5.enter_context(nc.sbuf_tensor("qaa", [128, 2048], F32))
    xqtile = [st5.enter_context(nc.sbuf_tensor("xqtile0", [128, 2048], BF16)),
              st5.enter_context(nc.sbuf_tensor("xqtile1", [128, 2048], BF16))]
    xqfm = [st5.enter_context(nc.sbuf_tensor("xqfm0", [128, 8192], BF16)),
            st5.enter_context(nc.sbuf_tensor("xqfm1", [128, 8192], BF16))]
    zt = st5.enter_context(nc.sbuf_tensor("zt", [128, 2048], F32))
    yq = st5.enter_context(nc.sbuf_tensor("yq", [128, 2048], F32))
    sqt5 = st5.enter_context(nc.sbuf_tensor("sqt5", [128, 2048], BF16))
    xmod5 = st5.enter_context(nc.sbuf_tensor("xmod5", [128, 2048], F32))
    outt = st5.enter_context(nc.sbuf_tensor("outt0", [128, 2048], F32))
    bqB = st5.enter_context(nc.sbuf_tensor("bqB", [128, 2048], BF16))

    wt("tensor", c_bias, 1)
    for i in range(4):
        if i >= 2:
            wt("tensor", p5ev, p5ev.n - 1)

        @on("tensor")
        def _(i=i):
            nc.tensor.matmul(psB[i % 2][:], ones_rbf[0:1, :],
                             bq_bf[0:1, i * 512:(i + 1) * 512],
                             start=True, stop=True).then_inc(p5mg.h, 1)
        p5mg.inc(1)
        wt("vector", p5mg, p5mg.n)

        @on("vector")
        def _(i=i):
            nc.vector.tensor_scalar(bqB[:, i * 512:(i + 1) * 512], psB[i % 2][:],
                                    1.0 / 128.0, None, OP.mult
                                    ).then_inc(p5ev.h, 1)
        p5ev.inc(1)
    wt("tensor", p5ev, p5ev.n)
    p5mg_base = p5mg.n
    p5ev_base = p5ev.n
    for qt in range(4):
        par = qt % 2
        for tc in range(4):
            gi = qt * 4 + tc
            if qt >= 1:
                wt("sync", c_out, qt * 4)

            @on("sync")
            def _(qt=qt, tc=tc, par=par):
                r0 = qt * 512 + tc * 128
                nc.sync.dma_start(x2q[:, tc * 2048:(tc + 1) * 2048],
                                  x2h[r0:r0 + 128, :]).then_inc(d_x2[par].h, 16)
            d_x2[par].inc(16)

        for tc in range(4):
            gi = qt * 4 + tc
            wt("scalar", d_x2[par], d_x2[par].n)
            if gi >= 1:
                wt("scalar", p5v, p5v.n)

            @on("scalar")
            def _(tc=tc):
                nc.scalar.activation(sqt5[:], x2q[:, tc * 2048:(tc + 1) * 2048],
                                     AF.Square, accum_out=smal[:, 0:1]
                                     ).then_inc(p5s.h, 1)
            p5s.inc(1)
            wt("vector", d_x2[par], d_x2[par].n)
            wt("vector", p5s, p5s.n)

            @on("vector")
            def _(tc=tc):
                v = nc.vector
                v.reduce_sum(smal[:, 1:2], x2q[:, tc * 2048:(tc + 1) * 2048],
                             axis=AX)
                v.drain()
                v.tensor_scalar_mul(smal[:, 2:3], smal[:, 1:2], 1.0 / 2048)
                v.tensor_scalar_mul(smal[:, 3:4], smal[:, 0:1], 1.0 / 2048)
                v.drain()
                v.tensor_mul(smal[:, 4:5], smal[:, 2:3], smal[:, 2:3])
                v.drain()
                v.tensor_sub(smal[:, 4:5], smal[:, 3:4], smal[:, 4:5]
                             ).then_inc(p5v.h, 1)
            p5v.inc(1)
            wt("scalar", p5v, p5v.n)

            @on("scalar")
            def _():
                s = nc.scalar
                s.activation(smal[:, 5:6], smal[:, 4:5], AF.Ln, bias=epsv[:, 0:1])
                s.drain()
                s.activation(smal[:, 6:7], smal[:, 5:6], AF.Exp, scale=-0.5
                             ).then_inc(p5s.h, 1)
            p5s.inc(1)
            wt("vector", p5s, p5s.n)
            if gi >= 2:
                wt("vector", s_xq[gi % 2], s_xq[gi % 2].n)

            @on("vector")
            def _(tc=tc, gi=gi):
                v = nc.vector
                v.scalar_tensor_tensor(smal[:, 7:8], smal[:, 2:3], -1.0,
                                       smal[:, 6:7], OP.mult, OP.mult)
                v.drain()
                v.tensor_scalar(h2[:], x2q[:, tc * 2048:(tc + 1) * 2048],
                                smal[:, 6:7], smal[:, 7:8], OP.mult, OP.add)
                v.drain()
                v.reduce_max(smal[:, 8:9], h2[:], axis=AX,
                             apply_absolute_value=True)
                v.drain()
                v.tensor_scalar_max(smal[:, 8:9], smal[:, 8:9], 1e-30)
                v.drain()
                v.reciprocal(smal[:, 9:10], smal[:, 8:9])
                v.drain()
                v.tensor_scalar_mul(smal[:, 10:11], smal[:, 9:10], 127.0)
                v.tensor_scalar_mul(smal[:, 11:12], smal[:, 8:9], -1.0 / 127.0)
                v.drain()
                v.tensor_scalar(qtt[:], h2[:], smal[:, 10:11], None, OP.mult)
                v.drain()
                v.tensor_copy(quu[:], qtt[:])
                v.drain()
                v.tensor_sub(qaa[:], qtt[:], quu[:])
                v.drain()
                v.scalar_tensor_tensor(qtt[:], qaa[:], 0.5, quu[:],
                                       OP.is_ge, OP.add)
                v.drain()
                v.scalar_tensor_tensor(qtt[:], qaa[:], 1.5, qtt[:],
                                       OP.is_ge, OP.add)
                v.drain()
                v.scalar_tensor_tensor(h2[:], qaa[:], -0.5, qtt[:],
                                       OP.is_lt, OP.subtract)
                v.drain()
                v.scalar_tensor_tensor(h2[:], qaa[:], -1.5, h2[:],
                                       OP.is_lt, OP.add)
                v.drain()
                v.tensor_scalar(xqtile[gi % 2][:], h2[:], smal[:, 11:12], None,
                                OP.mult).then_inc(c_xq.h, 1)
            c_xq.inc(1)
            wt("scalar", c_xq, c_xq.n)

            @on("scalar")
            def _(qt=qt, tc=tc, gi=gi):
                r0 = qt * 512 + tc * 128
                nc.scalar.dma_start(xqh[r0:r0 + 128, :], xqtile[gi % 2][:]
                                    ).then_inc(s_xq[gi % 2].h, 16)
            s_xq[gi % 2].inc(16)

        wt("sync", s_xq[0], s_xq[0].n)
        wt("sync", s_xq[1], s_xq[1].n)
        if qt >= 2:
            wt("sync", p5mg, p5mg.n - 4)
        for dc in range(DC):
            @on("sync")
            def _(qt=qt, dc=dc, par=par):
                nc.sync.dma_start(
                    xqfm[par][:, dc * 512:(dc + 1) * 512],
                    xqh[qt * 512:(qt + 1) * 512, dc * 128:(dc + 1) * 128],
                    transpose=True).then_inc(d_xqt[par].h, 16)
            d_xqt[par].inc(16)

        wt("tensor", d_xqt[par], d_xqt[par].n)
        wt("tensor", c_bias, 1)
        for tc in range(4):
            gi = qt * 4 + tc
            for fg in range(4):
                g5 = qt * 16 + tc * 4 + fg + 1
                if g5 > 2:
                    wt("tensor", p5ev, p5ev_base + g5 - 2)

                @on("tensor")
                def _(fg=fg, g=g5):
                    nc.tensor.matmul(psA[g % 2][:, 0:512], onesm[:],
                                     bqB[:, fg * 512:(fg + 1) * 512],
                                     start=True, stop=False,
                                     skip_group_check=True)
                for dc in range(DC):
                    @on("tensor")
                    def _(tc=tc, dc=dc, fg=fg, par=par, g=g5, fin=(dc == DC - 1)):
                        ins = nc.tensor.matmul(
                            psA[g % 2][:, 0:512],
                            xqfm[par][:, dc * 512 + tc * 128: dc * 512 + (tc + 1) * 128],
                            kq[:, dc * 2048 + fg * 512: dc * 2048 + (fg + 1) * 512],
                            start=False, stop=fin, skip_group_check=True)
                        if fin:
                            ins.then_inc(p5mg.h, 1)
                p5mg.inc(1)
                wt("scalar", p5mg, p5mg.n)

                @on("scalar")
                def _(fg=fg, g=g5):
                    nc.scalar.activation(zt[:, fg * 512:(fg + 1) * 512],
                                         psA[g % 2][:, 0:512], AF.Relu
                                         ).then_inc(p5ev.h, 1)
                p5ev.inc(1)

            wt("vector", p5ev, p5ev.n)
            if gi >= 1:
                wt("vector", s_out, 16 * gi)

            @on("vector")
            def _(tc=tc, gi=gi):
                v = nc.vector
                v.reduce_max(smal[:, 8:9], zt[:], axis=AX)
                v.drain()
                v.tensor_scalar_max(smal[:, 8:9], smal[:, 8:9], 1e-30)
                v.drain()
                v.reciprocal(smal[:, 9:10], smal[:, 8:9])
                v.drain()
                v.tensor_scalar_mul(smal[:, 10:11], smal[:, 9:10], 127.0)
                v.tensor_scalar_mul(smal[:, 11:12], smal[:, 8:9], -1.0 / 127.0)
                v.drain()
                v.tensor_scalar(qtt[:], zt[:], smal[:, 10:11], None, OP.mult)
                v.drain()
                v.tensor_copy(quu[:], qtt[:])
                v.drain()
                v.tensor_sub(qaa[:], qtt[:], quu[:])
                v.drain()
                v.scalar_tensor_tensor(qtt[:], qaa[:], 0.5, quu[:],
                                       OP.is_ge, OP.add)
                v.drain()
                v.scalar_tensor_tensor(qtt[:], qaa[:], 1.5, qtt[:],
                                       OP.is_ge, OP.add)
                v.drain()
                v.scalar_tensor_tensor(zt[:], qaa[:], -0.5, qtt[:],
                                       OP.is_lt, OP.subtract)
                v.drain()
                v.scalar_tensor_tensor(zt[:], qaa[:], -1.5, zt[:],
                                       OP.is_lt, OP.add)
                v.drain()
                v.tensor_scalar(yq[:], zt[:], smal[:, 11:12], None, OP.mult
                                ).then_inc(p5v.h, 1)
            p5v.inc(1)
            wt("scalar", p5v, p5v.n)

            @on("scalar")
            def _():
                nc.scalar.activation(sqt5[:], yq[:], AF.Square,
                                     accum_out=smal[:, 0:1]).then_inc(p5s.h, 1)
            p5s.inc(1)
            wt("vector", p5s, p5s.n)

            @on("vector")
            def _():
                v = nc.vector
                v.reduce_sum(smal[:, 1:2], yq[:], axis=AX)
                v.drain()
                v.tensor_scalar_mul(smal[:, 2:3], smal[:, 1:2], 1.0 / 2048)
                v.tensor_scalar_mul(smal[:, 3:4], smal[:, 0:1], 1.0 / 2048)
                v.drain()
                v.tensor_mul(smal[:, 4:5], smal[:, 2:3], smal[:, 2:3])
                v.drain()
                v.tensor_sub(smal[:, 4:5], smal[:, 3:4], smal[:, 4:5]
                             ).then_inc(p5v.h, 1)
            p5v.inc(1)
            wt("scalar", p5v, p5v.n)

            @on("scalar")
            def _():
                s = nc.scalar
                s.activation(smal[:, 5:6], smal[:, 4:5], AF.Ln, bias=epsv[:, 0:1])
                s.drain()
                s.activation(smal[:, 6:7], smal[:, 5:6], AF.Exp, scale=-0.5
                             ).then_inc(p5s.h, 1)
            p5s.inc(1)
            wt("vector", p5s, p5s.n)

            @on("vector")
            def _(tc=tc, gi=gi):
                v = nc.vector
                v.scalar_tensor_tensor(smal[:, 7:8], smal[:, 2:3], -1.0,
                                       smal[:, 6:7], OP.mult, OP.mult)
                v.drain()
                v.tensor_scalar(xmod5[:], x2q[:, tc * 2048:(tc + 1) * 2048],
                                smal[:, 7:8], None, OP.add)
                v.drain()
                v.scalar_tensor_tensor(outt[:], yq[:], smal[:, 6:7],
                                       xmod5[:], OP.mult, OP.add
                                       ).then_inc(c_out.h, 1)
            c_out.inc(1)
            wt("scalar", c_out, c_out.n)

            @on("scalar")
            def _(qt=qt, tc=tc, gi=gi):
                r0 = qt * 512 + tc * 128
                nc.scalar.dma_start(out_d[r0:r0 + 128, :], outt[:]
                                    ).then_inc(s_out.h, 16)
            s_out.inc(16)

    wt("scalar", s_out, s_out.n)

    with nc.Block() as blk:
        @blk.sync
        def _(eng):
            for f in OPS["sync"]:
                f()

        @blk.gpsimd
        def _(eng):
            for f in OPS["gpsimd"]:
                f()

        @blk.tensor
        def _(eng):
            for f in OPS["tensor"]:
                f()

        @blk.vector
        def _(eng):
            for f in OPS["vector"]:
                f()

        @blk.scalar
        def _(eng):
            for f in OPS["scalar"]:
                f()

    st5.close()
    st3a.close()
    ctx.close()
    return nc


def _arrange(inputs):
    Wqkv = np.asarray(inputs["L0_attn_Wqkv_w"], np.float32)
    Wo = np.asarray(inputs["L0_attn_out_w"], np.float32)
    bq = np.asarray(inputs["L0_attn_Wqkv_b"], np.float32)
    ob = np.asarray(inputs["L0_attn_out_b"], np.float32)
    kern = np.asarray(inputs["L0_ffn_kernel"], np.float32)
    fb = np.asarray(inputs["L0_ffn_bias"], np.float32)

    WqkvT = Wqkv.T
    wqk = np.ascontiguousarray(
        WqkvT[:, :4096].reshape(16, 128, 32, 128).transpose(1, 2, 0, 3)
        .reshape(128, 32 * 2048))
    wv = np.ascontiguousarray(
        WqkvT[:, 4096:].reshape(16, 128, 4, 512).transpose(1, 2, 0, 3)
        .reshape(128, 4 * 8192))
    wo = np.ascontiguousarray(
        Wo.T.reshape(16, 128, 2048).transpose(1, 0, 2).reshape(128, 16 * 2048))
    kn = np.ascontiguousarray(
        kern.reshape(16, 128, 2048).transpose(1, 0, 2).reshape(128, 16 * 2048))
    kt = np.ascontiguousarray(
        kern.T.reshape(16, 128, 2048).transpose(1, 0, 2).reshape(128, 16 * 2048))
    bqkv = np.ascontiguousarray(bq[:6144].reshape(48, 128).T)
    return {
        "wqk": wqk, "wv": wv, "wo": wo, "kern": kn, "kernT": kt,
        "bqkv": bqkv,
        "bvrow": np.ascontiguousarray(bq[4096:6144][None, :]),
        "obrow": np.ascontiguousarray(ob[None, :]),
        "fbrow": np.ascontiguousarray(fb[None, :]),
    }


def kernel(L0_attn_hidden_states, L0_attn_Wqkv_w, L0_attn_Wqkv_b,
           L0_attn_out_w, L0_attn_out_b, L0_ffn_kernel, L0_ffn_bias):
    x = np.asarray(L0_attn_hidden_states, np.float32)
    shared = _arrange(dict(
        L0_attn_Wqkv_w=L0_attn_Wqkv_w, L0_attn_Wqkv_b=L0_attn_Wqkv_b,
        L0_attn_out_w=L0_attn_out_w, L0_attn_out_b=L0_attn_out_b,
        L0_ffn_kernel=L0_ffn_kernel, L0_ffn_bias=L0_ffn_bias))

    nc = build_program()
    in_maps = []
    for c in range(NCORE):
        seq, half = c // 2, c % 2
        own = x[seq, half * 2048:(half + 1) * 2048, :]
        par = x[seq, (1 - half) * 2048:(2 - half) * 2048, :]
        xa = np.ascontiguousarray(np.concatenate([own, par], axis=0))
        m = {"x": xa}
        m.update(shared)
        in_maps.append(m)

    res = run_bass_kernel_spmd(nc, in_maps, core_ids=list(range(NCORE)))
    out = np.empty((4, 4096, 2048), np.float32)
    for c in range(NCORE):
        seq, half = c // 2, c % 2
        out[seq, half * 2048:(half + 1) * 2048, :] = res.results[c]["out"]
    return out


# version 16 (final)
# speedup vs baseline: 1.1421x; 1.0293x over previous
import numpy as np
from collections import deque
from contextlib import ExitStack

import concourse.bass as bass
import concourse.mybir as mybir
from concourse.bass_utils import run_bass_kernel_spmd

F32 = mybir.dt.float32
BF16 = mybir.dt.bfloat16
I32 = mybir.dt.int32
AF = mybir.ActivationFunctionType
OP = mybir.AluOpType
AX = mybir.AxisListType.X

B, S, D, H, HD = 4, 4096, 2048, 16, 128
TA, TO = 4096, 2048
DC = 16
SMS = 0.08838834764831845
NCORE = 8


class Sem:

    def __init__(self, h):
        self.h = h
        self.n = 0

    def inc(self, k=1):
        self.n += k
        return self.n


def build_program():
    nc = bass.Bass("TRN2", target_bir_lowering=False)

    x_in = nc.dram_tensor("x", [TA, D], F32, kind="ExternalInput").ap()
    wqk = nc.dram_tensor("wqk", [128, 32 * 2048], F32, kind="ExternalInput").ap()
    wv_d = nc.dram_tensor("wv", [128, 4 * 8192], F32, kind="ExternalInput").ap()
    wo_d = nc.dram_tensor("wo", [128, 16 * 2048], F32, kind="ExternalInput").ap()
    kern_d = nc.dram_tensor("kern", [128, 16 * 2048], F32, kind="ExternalInput").ap()
    kernT_d = nc.dram_tensor("kernT", [128, 16 * 2048], F32, kind="ExternalInput").ap()
    bqkv_d = nc.dram_tensor("bqkv", [128, 48], F32, kind="ExternalInput").ap()
    bvrow_d = nc.dram_tensor("bvrow", [1, 2048], F32, kind="ExternalInput").ap()
    obrow_d = nc.dram_tensor("obrow", [1, 2048], F32, kind="ExternalInput").ap()
    fbrow_d = nc.dram_tensor("fbrow", [1, 2048], F32, kind="ExternalInput").ap()
    out_d = nc.dram_tensor("out", [TO, D], F32, kind="ExternalOutput").ap()

    h1h = nc.dram_tensor("h1h", [TA, D], BF16, kind="Internal").ap()
    khh = nc.dram_tensor("khh", [H, 128, TA], BF16, kind="Internal").ap()
    vhh = nc.dram_tensor("vhh", [32, 128, 2048], BF16, kind="Internal").ap()
    qhh = nc.dram_tensor("qhh", [H, 128, TO], BF16, kind="Internal").ap()
    ctxh = nc.dram_tensor("ctxh", [H, 128, TO], BF16, kind="Internal").ap()
    x2h = nc.dram_tensor("x2h", [TO, D], BF16, kind="Internal").ap()
    xqh = nc.dram_tensor("xqh", [TO, D], BF16, kind="Internal").ap()
    rech = nc.dram_tensor("rech", [64, 512], F32, kind="Internal").ap()
    denh = nc.dram_tensor("denh", [64, 512], F32, kind="Internal").ap()
    nqh = nc.dram_tensor("nqh", [16, 2048], BF16, kind="Internal").ap()
    mkh = nc.dram_tensor("mkh", [1, 16], F32, kind="Internal").ap()
    sksh = nc.dram_tensor("sksh", [2048], F32, kind="Internal").ap()
    skrh = nc.dram_tensor("skrh", [2048], F32, kind="Internal").ap()

    OPS = {e: [] for e in ("sync", "gpsimd", "tensor", "vector", "scalar")}

    def on(eng):
        def dec(fn):
            OPS[eng].append(fn)
            return None
        return dec

    def wt(eng, sem, val=None):
        v = sem.n if val is None else val
        E = {"sync": nc.sync, "gpsimd": nc.gpsimd, "tensor": nc.tensor,
             "vector": nc.vector, "scalar": nc.scalar}[eng]
        OPS[eng].append(lambda E=E, s=sem.h, v=v: E.wait_ge(s, v))

    ctx = ExitStack()
    E = ctx.enter_context

    def sem(name):
        return Sem(E(nc.semaphore(name)))

    ph = sem("ph")
    d_x = [sem("d_x0"), sem("d_x1")]
    d_kt = [sem("d_kt0"), sem("d_kt1")]
    f_kt = [sem("f_kt0"), sem("f_kt1")]
    d_bias = sem("d_bias")
    d_bv = sem("d_bv")
    d_ob = sem("d_ob")
    s_h1 = [sem("s_h1a"), sem("s_h1b")]
    d_th = sem("d_th")
    s_sk = sem("s_sk")
    c_init = sem("c_init")
    c_bias = sem("c_bias")
    p1v = sem("p1v")
    p1s = sem("p1s")
    p1done = sem("p1done")
    d_w = [sem("d_w0"), sem("d_w1")]
    f_w = [sem("f_w0"), sem("f_w1")]
    d_wv = [sem("d_wv0"), sem("d_wv1")]
    f_wv = [sem("f_wv0"), sem("f_wv1")]
    p2mg = sem("p2mg")
    p2ev = sem("p2ev")
    s_k = [sem("s_k0"), sem("s_k1")]
    s_q = [sem("s_q0"), sem("s_q1")]
    s_v = [sem("s_v0"), sem("s_v1")]
    d_sk3 = sem("d_sk3")
    d_mk = sem("d_mk")
    p3bc = sem("p3bc")
    p3be = sem("p3be")
    d_k3 = [sem("d_k30"), sem("d_k31")]
    d_v3 = [sem("d_v30"), sem("d_v31")]
    d_q3 = [sem("d_q30"), sem("d_q31")]
    fr3 = sem("fr3")
    c_badd = sem("c_badd")
    fv3 = sem("fv3")
    d_kn = [sem("d_kn0"), sem("d_kn1")]
    f_kn = [sem("f_kn0"), sem("f_kn1")]
    c_sq3 = sem("c_sq3")
    d_nq = [sem("d_nq0"), sem("d_nq1")]
    s_nq = sem("s_nq")
    c_nm = sem("c_nm")
    c_nr = sem("c_nr")
    c_negC = sem("c_negC")
    mmS = sem("mmS")
    expd = sem("expd")
    mmPV = sem("mmPV")
    mmCtx = sem("mmCtx")
    ev3 = sem("ev3")
    s_ctx = [sem("s_ctx0"), sem("s_ctx1")]
    s_den = [sem("s_den0"), sem("s_den1")]
    c_rec = sem("c_rec")
    s_rec = sem("s_rec")
    d_cq = [sem("d_cq0"), sem("d_cq1")]
    d_rr = [sem("d_rr0"), sem("d_rr1")]
    c_bcB = sem("c_bcB")
    c_ctxn = sem("c_ctxn")
    d_wo = [sem("d_wo0"), sem("d_wo1")]
    d_xr = [sem("d_xr0"), sem("d_xr1")]
    p4mg = sem("p4mg")
    p4ev = sem("p4ev")
    p4v = sem("p4v")
    p4s = sem("p4s")
    c_x2 = sem("c_x2")
    s_x2 = [sem("s_x2a"), sem("s_x2b")]
    d_x2 = [sem("d_x2a"), sem("d_x2b")]
    c_xq = sem("c_xq")
    s_xq = [sem("s_xq0"), sem("s_xq1")]
    d_xqt = [sem("d_xqt0"), sem("d_xqt1")]
    s_h1p = d_nq
    d_th2 = d_xqt[0]
    p5mg = sem("p5mg")
    p5ev = sem("p5ev")
    p5v = sem("p5v")
    p5s = sem("p5s")
    c_out = sem("c_out")
    s_out = sem("s_out")

    def barrier(extra_sync_waits=()):
        tgt = ph.n + 5
        for (sm, v) in extra_sync_waits:
            wt("sync", sm, v)
        for eng in ("sync", "gpsimd", "tensor", "vector", "scalar"):
            ph.inc(1)
            En = {"sync": nc.sync, "gpsimd": nc.gpsimd, "tensor": nc.tensor,
                  "vector": nc.vector, "scalar": nc.scalar}[eng]
            OPS[eng].append(lambda E=En, s=ph.h: E.drain().then_inc(s, 1))
            OPS[eng].append(lambda E=En, s=ph.h, t=tgt: E.wait_ge(s, t))

    ones_bf = E(nc.sbuf_tensor("ones_bf", [128, 1], BF16))
    ones_f = E(nc.sbuf_tensor("ones_f", [1, 128], F32))
    ones_rbf = E(nc.sbuf_tensor("ones_rbf", [1, 128], BF16))
    epsv = E(nc.sbuf_tensor("epsv", [128, 1], F32))
    bqkv = E(nc.sbuf_tensor("bqkv_sb", [128, 48], F32))
    bq_bf = E(nc.sbuf_tensor("bq_bf", [1, 2048], BF16))
    smal = E(nc.sbuf_tensor("smal", [128, 16], F32))
    sm1 = E(nc.sbuf_tensor("sm1", [1, 16], F32))
    skT = E(nc.sbuf_tensor("skT", [128, 16], F32))
    sknT = E(nc.sbuf_tensor("sknT", [128, 16], F32))
    skrT = E(nc.sbuf_tensor("skrT", [128, 16], F32))
    onesm = E(nc.sbuf_tensor("onesm", [128, 128], BF16))
    nkt = E(nc.sbuf_tensor("nkt", [1, 16], F32))
    lnmk = E(nc.sbuf_tensor("lnmk", [1, 16], F32))
    wfc = [E(nc.sbuf_tensor("wfc0", [128, 2048], BF16)),
           E(nc.sbuf_tensor("wfc1", [128, 2048], BF16))]

    st12 = ExitStack()
    h1fm = st12.enter_context(nc.sbuf_tensor("h1fm", [128, 16 * 4096], BF16))
    st1 = ExitStack()
    xst = [st1.enter_context(nc.sbuf_tensor("xst0", [128, 2048], F32)),
           st1.enter_context(nc.sbuf_tensor("xst1", [128, 2048], F32))]
    h1t = [st1.enter_context(nc.sbuf_tensor("h1t0", [128, 2048], BF16)),
           st1.enter_context(nc.sbuf_tensor("h1t1", [128, 2048], BF16))]
    ktst = [st1.enter_context(nc.sbuf_tensor("ktst0", [128, 2048], F32)),
            st1.enter_context(nc.sbuf_tensor("ktst1", [128, 2048], F32))]
    sqtr = st1.enter_context(nc.sbuf_tensor("sqtr", [128, 2048], BF16))
    bqu = st1.enter_context(nc.sbuf_tensor("bqu", [1, 2048], I32))

    @on("vector")
    def _():
        nc.vector.memset(ones_bf[:], 1.0)
        nc.vector.memset(ones_f[:], 1.0)
        nc.vector.memset(ones_rbf[:], 1.0)
        nc.vector.memset(epsv[:], 1e-5)
        nc.vector.memset(smal[:], 0.0)
        nc.vector.memset(onesm[:], 1.0)
        nc.vector.memset(sm1[:], 0.0).then_inc(c_init.h, 1)
    c_init.inc(1)

    @on("sync")
    def _():
        nc.sync.dma_start(bqkv[:], bqkv_d[:]).then_inc(d_bias.h, 16)
    d_bias.inc(16)

    for j in range(16):
        if j >= 2:
            wt("sync", f_kt[j % 2], j // 2)

        @on("sync")
        def _(j=j):
            nc.sync.dma_start(ktst[j % 2][:], kernT_d[:, j * 2048:(j + 1) * 2048]
                              ).then_inc(d_kt[j % 2].h, 16)
        d_kt[j % 2].inc(16)
        wt("vector", d_kt[j % 2])

        @on("vector")
        def _(j=j):
            nc.vector.reduce_max(skT[:, j:j + 1], ktst[j % 2][:], axis=AX,
                                 apply_absolute_value=True).then_inc(f_kt[j % 2].h, 1)
        f_kt[j % 2].inc(1)

    @on("vector")
    def _():
        v = nc.vector
        v.drain()
        v.tensor_scalar_max(skT[:], skT[:], 1e-30)
        v.drain()
        v.reciprocal(skrT[:], skT[:])
        v.drain()
        v.tensor_scalar_mul(skrT[:], skrT[:], 127.0)
        v.tensor_scalar_mul(sknT[:], skT[:], -1.0 / 127.0).then_inc(p1s.h, 1)
    p1s.inc(1)

    wt("sync", f_kt[0], f_kt[0].n)
    wt("sync", f_kt[1], f_kt[1].n)

    @on("sync")
    def _():
        nc.sync.dma_start(ktst[0][0:1, :], fbrow_d[:]).then_inc(d_bias.h, 16)
    d_bias.inc(16)
    wt("vector", d_bias, d_bias.n)

    @on("vector")
    def _():
        v = nc.vector
        fb_r = ktst[0][0:1, :]
        t_r = xst[0][0:1, :]
        a_r = xst[1][0:1, :]
        u_r = bqu[0:1, :]
        v.drain()
        v.reduce_max(sm1[0:1, 0:1], fb_r, axis=AX, apply_absolute_value=True)
        v.drain()
        v.tensor_scalar_max(sm1[0:1, 0:1], sm1[0:1, 0:1], 1e-30)
        v.drain()
        v.reciprocal(sm1[0:1, 1:2], sm1[0:1, 0:1])
        v.drain()
        v.tensor_scalar_mul(sm1[0:1, 2:3], sm1[0:1, 1:2], 127.0)
        v.tensor_scalar_mul(sm1[0:1, 3:4], sm1[0:1, 0:1], -1.0 / 127.0)
        v.drain()
        v.tensor_scalar(t_r, fb_r, sm1[0:1, 2:3], None, OP.mult)
        v.drain()
        v.tensor_copy(u_r, t_r)
        v.drain()
        v.tensor_sub(a_r, t_r, u_r)
        v.drain()
        v.scalar_tensor_tensor(t_r, a_r, 0.5, u_r, OP.is_ge, OP.add)
        v.drain()
        v.scalar_tensor_tensor(a_r, a_r, -0.5, t_r, OP.is_lt, OP.subtract)
        v.drain()
        v.tensor_scalar(bq_bf[:], a_r, sm1[0:1, 3:4], None,
                        OP.mult).then_inc(c_bias.h, 1)
    c_bias.inc(1)
    wt("scalar", p1s, p1s.n)

    @on("scalar")
    def _():
        nc.scalar.dma_start(sksh.rearrange("(c p) -> p c", p=128), sknT[:],
                            allow_slow_non_contiguous=True).then_inc(s_sk.h, 16)
        nc.scalar.dma_start(skrh.rearrange("(c p) -> p c", p=128), skrT[:],
                            allow_slow_non_contiguous=True).then_inc(s_sk.h, 16)
    s_sk.inc(32)

    p1v_base = p1v.n
    for i in range(32):
        if i < 2:
            wt("sync", c_bias, 1)
        if i >= 2:
            if i < 16:
                wt("sync", s_h1[i % 2], 16 * (i // 2))
            elif i < 18:
                wt("sync", s_h1[i % 2], 128)
            else:
                wt("sync", s_h1p[i % 2], 16 * ((i - 16) // 2))
            wt("sync", p1v, p1v_base + (i - 2) * 2 + 2)

        @on("sync")
        def _(i=i):
            nc.sync.dma_start(xst[i % 2][:], x_in[i * 128:(i + 1) * 128, :]
                              ).then_inc(d_x[i % 2].h, 16)
        d_x[i % 2].inc(16)

        wt("scalar", d_x[i % 2])
        if i >= 1:
            wt("scalar", p1v, i * 2 - 1)
        wt("scalar", c_init, 1)

        @on("scalar")
        def _(i=i):
            s = nc.scalar
            s.activation(sqtr[:], xst[i % 2][:], AF.Square,
                         accum_out=smal[:, 0:1]).then_inc(p1s.h, 1)
        p1s.inc(1)

        wt("vector", d_x[i % 2])
        wt("vector", p1s, p1s.n)

        @on("vector")
        def _(i=i):
            v = nc.vector
            v.reduce_sum(smal[:, 1:2], xst[i % 2][:], axis=AX)
            v.drain()
            v.tensor_scalar_mul(smal[:, 2:3], smal[:, 1:2], 1.0 / 2048)
            v.tensor_scalar_mul(smal[:, 3:4], smal[:, 0:1], 1.0 / 2048)
            v.drain()
            v.tensor_mul(smal[:, 4:5], smal[:, 2:3], smal[:, 2:3])
            v.drain()
            v.tensor_sub(smal[:, 4:5], smal[:, 3:4], smal[:, 4:5]
                         ).then_inc(p1v.h, 1)
        p1v.inc(1)

        wt("scalar", p1v, p1v.n)

        @on("scalar")
        def _(i=i):
            s = nc.scalar
            s.activation(smal[:, 5:6], smal[:, 4:5], AF.Ln, bias=epsv[:, 0:1])
            s.drain()
            s.activation(smal[:, 6:7], smal[:, 5:6], AF.Exp, scale=-0.5
                         ).then_inc(p1s.h, 1)
        p1s.inc(1)

        wt("vector", p1s, p1s.n)

        @on("vector")
        def _(i=i):
            v = nc.vector
            v.scalar_tensor_tensor(smal[:, 7:8], smal[:, 2:3], -1.0,
                                   smal[:, 6:7], OP.mult, OP.mult)
            v.drain()
            v.tensor_scalar(h1t[i % 2][:], xst[i % 2][:], smal[:, 6:7],
                            smal[:, 7:8], OP.mult, OP.add).then_inc(p1v.h, 1)
        p1v.inc(1)

        wt("scalar", p1v, p1v.n)

        hsem = s_h1 if i < 16 else s_h1p

        @on("scalar")
        def _(i=i, hsem=hsem):
            nc.scalar.dma_start(h1h[i * 128:(i + 1) * 128, :], h1t[i % 2][:]
                                ).then_inc(hsem[i % 2].h, 16)
        hsem[i % 2].inc(16)

    @on("vector")
    def _():
        nc.vector.drain().then_inc(p1done.h, 1)
    p1done.inc(1)

    wt("sync", s_h1[0], s_h1[0].n)
    wt("sync", s_h1[1], s_h1[1].n)
    for dc in range(DC):
        @on("sync")
        def _(dc=dc):
            nc.sync.dma_start(h1fm[:, dc * 4096:dc * 4096 + 2048],
                              h1h[0:2048, dc * 128:(dc + 1) * 128],
                              transpose=True).then_inc(d_th.h, 16)
        d_th.inc(16)
    wt("sync", s_h1p[0], s_h1p[0].n)
    wt("sync", s_h1p[1], s_h1p[1].n)
    for dc in range(DC):
        @on("sync")
        def _(dc=dc):
            nc.sync.dma_start(h1fm[:, dc * 4096 + 2048:(dc + 1) * 4096],
                              h1h[2048:4096, dc * 128:(dc + 1) * 128],
                              transpose=True).then_inc(d_th2.h, 16)
        d_th2.inc(16)

    st1.close()
    st2 = ExitStack()
    wvb = [st2.enter_context(nc.sbuf_tensor("wvb0", [128, 8192], BF16))]
    bvB = st2.enter_context(nc.sbuf_tensor("bvB", [128, 2048], BF16))
    kout = [st2.enter_context(nc.sbuf_tensor("kout0", [128, 4096], BF16)),
            st2.enter_context(nc.sbuf_tensor("kout1", [128, 4096], BF16))]
    qout = [st2.enter_context(nc.sbuf_tensor("qout0", [128, 2048], BF16)),
            st2.enter_context(nc.sbuf_tensor("qout1", [128, 2048], BF16))]
    vst = [st2.enter_context(nc.sbuf_tensor("vst0", [128, 512], BF16)),
           st2.enter_context(nc.sbuf_tensor("vst1", [128, 512], BF16))]
    sqbuf = st2.enter_context(nc.sbuf_tensor("sqbuf", [128, 4096], BF16))
    nrow2 = st2.enter_context(nc.sbuf_tensor("nrow2", [1, 2048], BF16))
    bv_bf = st2.enter_context(nc.sbuf_tensor("bv_bf", [1, 2048], BF16))

    psA = [E(nc.psum_tensor("psA0", [128, 1024], F32)),
           E(nc.psum_tensor("psA1", [128, 1024], F32))]
    psB = [E(nc.psum_tensor("psB0", [128, 512], F32)),
           E(nc.psum_tensor("psB1", [128, 512], F32))]
    psC = E(nc.psum_tensor("psC", [128, 512], F32))
    psD = E(nc.psum_tensor("psD", [1, 512], F32))


    for fc in range(32):
        if fc >= 2:
            OPS["gpsimd"].append(
                lambda fc=fc: nc.gpsimd.wait_ge(p2mg.h, fw_rel[fc - 2]))

        @on("gpsimd")
        def _(fc=fc):
            nc.gpsimd.dma_start(wfc[fc % 2][:], wqk[:, fc * 2048:(fc + 1) * 2048]
                                ).then_inc(d_w[fc % 2].h, 16)
        d_w[fc % 2].inc(16)

    wt("gpsimd", p1done, 1)

    @on("gpsimd")
    def _():
        nc.gpsimd.dma_start(bv_bf[:], bvrow_d[:]).then_inc(d_bv.h, 16)
    d_bv.inc(16)
    for vg in range(4):
        if vg >= 1:
            OPS["gpsimd"].append(
                lambda vg=vg: nc.gpsimd.wait_ge(p2mg.h, fwv_rel[vg - 1]))

        @on("gpsimd")
        def _(vg=vg):
            nc.gpsimd.dma_start(wvb[0][:], wv_d[:, vg * 8192:(vg + 1) * 8192]
                                ).then_inc(d_wv[vg % 2].h, 16)
        d_wv[vg % 2].inc(16)

    wt("vector", s_h1p[0], s_h1p[0].n)
    wt("vector", s_h1p[1], s_h1p[1].n)

    g2 = 0
    fw_rel = {}
    fwv_rel = {}

    aslots = [(psA[0], 0), (psA[0], 512), (psA[1], 0), (psA[1], 512)]
    pend = {"fn": None}
    nr_end = {"v": 0}

    def flush_norm():
        if pend["fn"] is not None:
            fn = pend["fn"]
            pend["fn"] = None
            fn()

    for fc in range(32):
        wt("tensor", d_w[fc % 2], 16 * (fc // 2 + 1))
        ntt = 4 if fc < 16 else 8
        is_q = fc < 16
        head = fc if is_q else fc - 16
        obuf = qout if is_q else kout
        sst = s_q if is_q else s_k
        par = head % 2
        wt("vector", sst[par], sst[par].n)
        for tt in range(ntt):
            g2 += 1
            wt("tensor", d_th, 256)
            if tt >= 4:
                wt("tensor", d_th2, 256)
            if g2 > 2:
                wt("tensor", p2ev, g2 - 2)
            last = (tt == ntt - 1)
            for dc in range(DC):
                @on("tensor")
                def _(fc=fc, tt=tt, dc=dc, g=g2, last=last):
                    ins = nc.tensor.matmul(
                        psB[g % 2][:],
                        wfc[fc % 2][:, dc * 128:(dc + 1) * 128],
                        h1fm[:, dc * 4096 + tt * 512: dc * 4096 + (tt + 1) * 512],
                        start=(dc == 0), stop=(dc == DC - 1))
                    if dc == DC - 1:
                        ins.then_inc(p2mg.h, 1)
            p2mg.inc(1)
            if last:
                fw_rel[fc] = p2mg.n

            wt("vector", p2mg, p2mg.n)

            @on("vector")
            def _(fc=fc, tt=tt, g=g2, obuf=obuf, par=par):
                nc.vector.tensor_scalar(
                    obuf[par][:, tt * 512:(tt + 1) * 512], psB[g % 2][:],
                    bqkv[:, fc:fc + 1], None, OP.add).then_inc(p2ev.h, 1)
            p2ev.inc(1)
        wt("scalar", p2ev, p2ev.n)

        @on("scalar")
        def _(head=head, is_q=is_q, par=par, obuf=obuf):
            tgt = qhh[head] if is_q else khh[head]
            nc.scalar.dma_start(tgt[:, :], obuf[par][:]
                                ).then_inc((s_q if is_q else s_k)[par].h, 16)
        sst[par].inc(16)

        flush_norm()
        nm_after = c_nm.n
        if is_q and s_nq.n > 0:
            wt("vector", s_nq, s_nq.n)
        wt("vector", c_nm, nm_after)
        sq_rel = []
        for tt in range(ntt):
            @on("vector")
            def _(tt=tt, par=par, obuf=obuf):
                v = nc.vector
                v.drain()
                v.tensor_mul(sqbuf[:, tt * 512:(tt + 1) * 512],
                             obuf[par][:, tt * 512:(tt + 1) * 512],
                             obuf[par][:, tt * 512:(tt + 1) * 512]
                             ).then_inc(c_sq3.h, 1)
            sq_rel.append(c_sq3.inc(1))

        def mk_flush(head=head, is_q=is_q, ntt=ntt, sq_rel=tuple(sq_rel),
                     nr_prev=nr_end["v"]):
            def fn():
                nrel = {}
                for tt in range(ntt):
                    wt("tensor", c_sq3, sq_rel[tt])
                    if tt == 0 and nr_prev > 0:
                        wt("tensor", c_nr, nr_prev)
                    if tt >= 4:
                        wt("tensor", c_nr, nrel[tt - 4])

                    @on("tensor")
                    def _(tt=tt):
                        pa, off = aslots[tt % 4]
                        nc.tensor.matmul(pa[0:1, off:off + 512],
                                         ones_bf[:, 0:1],
                                         sqbuf[:, tt * 512:(tt + 1) * 512],
                                         start=True, stop=True
                                         ).then_inc(c_nm.h, 1)
                    c_nm.inc(1)
                    wt("vector", c_nm, c_nm.n)

                    @on("vector")
                    def _(tt=tt, is_q=is_q):
                        v = nc.vector
                        pa, off = aslots[tt % 4]
                        v.drain()
                        if is_q:
                            v.tensor_copy(nrow2[0:1, tt * 512:(tt + 1) * 512],
                                          pa[0:1, off:off + 512]
                                          ).then_inc(c_nr.h, 1)
                        else:
                            v.reduce_max(nkt[0:1, tt:tt + 1],
                                         pa[0:1, off:off + 512],
                                         axis=AX).then_inc(c_nr.h, 1)
                    nrel[tt] = c_nr.inc(1)
                if is_q:
                    @on("vector")
                    def _():
                        v = nc.vector
                        v.drain()
                        v.tensor_scalar_max(nrow2[0:1, :], nrow2[0:1, :], 1e-6
                                            ).then_inc(c_nr.h, 1)
                    c_nr.inc(1)
                    wt("scalar", c_nr, c_nr.n)
                    if head >= 2:
                        wt("scalar", s_nq, s_nq.n - 16)

                    @on("scalar")
                    def _(head=head):
                        nc.scalar.dma_start(nqh[head:head + 1, :],
                                            nrow2[0:1, :]).then_inc(s_nq.h, 16)
                    s_nq.inc(16)
                else:
                    @on("vector")
                    def _(head=head):
                        v = nc.vector
                        v.drain()
                        v.reduce_max(sm1[0:1, head:head + 1], nkt[0:1, 0:8],
                                     axis=AX).then_inc(c_nr.h, 1)
                    c_nr.inc(1)
                nr_end["v"] = c_nr.n
            return fn
        pend["fn"] = mk_flush()

    flush_norm()

    @on("vector")
    def _():
        v = nc.vector
        v.drain()
        v.tensor_scalar_max(sm1[0:1, 0:16], sm1[0:1, 0:16], 1e-12
                            ).then_inc(c_nr.h, 1)
    c_nr.inc(1)
    wt("scalar", c_nr, c_nr.n)

    @on("scalar")
    def _():
        nc.scalar.dma_start(mkh[0:1, :], sm1[0:1, 0:16]).then_inc(s_nq.h, 16)
    s_nq.inc(16)

    wt("tensor", d_th, 256)
    wt("tensor", d_th2, 256)
    wt("tensor", c_bias, 1)
    wt("tensor", d_bias, d_bias.n)
    wt("tensor", d_bv, 16)
    for i in range(4):
        g2 += 1
        if g2 > 2:
            wt("tensor", p2ev, g2 - 2)

        @on("tensor")
        def _(i=i, g=g2):
            nc.tensor.matmul(psB[g % 2][:], ones_rbf[0:1, :],
                             bv_bf[0:1, i * 512:(i + 1) * 512],
                             start=True, stop=True).then_inc(p2mg.h, 1)
        p2mg.inc(1)
        wt("vector", p2mg, p2mg.n)

        @on("vector")
        def _(i=i, g=g2):
            nc.vector.tensor_scalar(bvB[:, i * 512:(i + 1) * 512], psB[g % 2][:],
                                    1.0 / 128.0, None, OP.mult
                                    ).then_inc(p2ev.h, 1)
        p2ev.inc(1)
    vgrp = 0
    for vg in range(4):
        wt("tensor", d_wv[vg % 2], 16 * (vg // 2 + 1))
        for tc in range(32):
            g2 += 1
            vgrp += 1
            if g2 > 2:
                wt("tensor", p2ev, g2 - 2)
            wt("vector", s_v[vgrp % 2], s_v[vgrp % 2].n)
            @on("tensor")
            def _(vg=vg, g=g2):
                nc.tensor.matmul(psB[g % 2][:], onesm[:],
                                 bvB[:, vg * 512:(vg + 1) * 512],
                                 start=True, stop=False, skip_group_check=True)
            for dc in range(DC):
                @on("tensor")
                def _(vg=vg, tc=tc, dc=dc, g=g2, fin=(dc == DC - 1)):
                    ins = nc.tensor.matmul(
                        psB[g % 2][:],
                        h1fm[:, dc * 4096 + tc * 128: dc * 4096 + (tc + 1) * 128],
                        wvb[0][:, dc * 512:(dc + 1) * 512],
                        start=False, stop=fin, skip_group_check=True)
                    if fin:
                        ins.then_inc(p2mg.h, 1)
            p2mg.inc(1)
            if tc == 31:
                fwv_rel[vg] = p2mg.n

            wt("vector", p2mg, p2mg.n)

            @on("vector")
            def _(g=g2, vp=vgrp % 2):
                nc.vector.tensor_copy(vst[vp][:], psB[g % 2][:]
                                      ).then_inc(p2ev.h, 1)
            p2ev.inc(1)
            wt("scalar", p2ev, p2ev.n)

            @on("scalar")
            def _(vg=vg, tc=tc, vp=vgrp % 2):
                nc.scalar.dma_start(vhh[tc][:, vg * 512:(vg + 1) * 512],
                                    vst[vp][:]).then_inc(s_v[vp].h, 16)
            s_v[vgrp % 2].inc(16)

    barrier(extra_sync_waits=[(s_k[0], s_k[0].n), (s_k[1], s_k[1].n),
                              (s_q[0], s_q[0].n), (s_q[1], s_q[1].n),
                              (s_v[0], s_v[0].n), (s_v[1], s_v[1].n),
                              (s_sk, 32)])

    st2.close()
    st12.close()
    st3a = ExitStack()
    kq = st3a.enter_context(nc.sbuf_tensor("kq", [128, 16 * 2048], BF16))
    st3 = ExitStack()
    Kh = [st3.enter_context(nc.sbuf_tensor("Kh0", [128, 4096], BF16)),
          st3.enter_context(nc.sbuf_tensor("Kh1", [128, 4096], BF16))]
    Vh = [st3.enter_context(nc.sbuf_tensor("Vh0", [128, 4096], BF16)),
          st3.enter_context(nc.sbuf_tensor("Vh1", [128, 4096], BF16))]
    Qh = [st3.enter_context(nc.sbuf_tensor("Qh0", [128, 2048], BF16)),
          st3.enter_context(nc.sbuf_tensor("Qh1", [128, 2048], BF16))]
    PT = [st3.enter_context(nc.sbuf_tensor("PT0", [128, 2048], BF16)),
          st3.enter_context(nc.sbuf_tensor("PT1", [128, 2048], BF16))]
    negCB = [st3.enter_context(nc.sbuf_tensor("negCB0", [128, 512], BF16)),
             st3.enter_context(nc.sbuf_tensor("negCB1", [128, 512], BF16))]
    ctxst = [st3.enter_context(nc.sbuf_tensor("ctxst0", [128, 2048], BF16)),
             st3.enter_context(nc.sbuf_tensor("ctxst1", [128, 2048], BF16))]
    nq_sb = st3.enter_context(nc.sbuf_tensor("nq_sb", [1, 2048], BF16))
    dstg = st3.enter_context(nc.sbuf_tensor("dstg", [64, 512], F32))
    rcps = st3.enter_context(nc.sbuf_tensor("rcps", [64, 512], F32))
    den_sb = [st3.enter_context(nc.sbuf_tensor("den_sb0", [1, 512], F32)),
              st3.enter_context(nc.sbuf_tensor("den_sb1", [1, 512], F32))]
    SKnb = st3.enter_context(nc.sbuf_tensor("SKnb", [128, 2048], F32))
    RSKb = st3.enter_context(nc.sbuf_tensor("RSKb", [128, 2048], F32))
    kst = [st3.enter_context(nc.sbuf_tensor("kst0", [128, 2048], F32))]
    kst.append(kst[0])
    kqt = st3.enter_context(nc.sbuf_tensor("kqt", [128, 2048], F32))
    negCrow = [st3.enter_context(nc.sbuf_tensor("negCrow0", [1, 2048], BF16)),
               st3.enter_context(nc.sbuf_tensor("negCrow1", [1, 2048], BF16))]
    skrow = kst[0][0:1, :]
    skrow2 = kqt[0:1, :]
    kqu = st3.enter_context(nc.sbuf_tensor("kqu", [128, 2048], I32))
    kqa = st3.enter_context(nc.sbuf_tensor("kqa", [128, 2048], F32))

    @on("sync")
    def _():
        nc.sync.dma_start(skrow, sksh.rearrange("(o f) -> o f", o=1)
                          ).then_inc(d_sk3.h, 16)
        nc.sync.dma_start(skrow2, skrh.rearrange("(o f) -> o f", o=1)
                          ).then_inc(d_sk3.h, 16)
    d_sk3.inc(32)
    wt("tensor", d_sk3, 32)
    for i in range(8):
        src = skrow if i < 4 else skrow2
        dstt = SKnb if i < 4 else RSKb
        col = (i % 4) * 512
        if i >= 2:
            wt("tensor", p3be, i - 1)

        @on("tensor")
        def _(i=i, src=src, col=col):
            nc.tensor.matmul(psB[i % 2][:], ones_f[0:1, :],
                             src[0:1, col:col + 512], start=True, stop=True
                             ).then_inc(p3bc.h, 1)
        p3bc.inc(1)
        wt("vector", p3bc, p3bc.n)

        @on("vector")
        def _(i=i, dstt=dstt, col=col):
            nc.vector.tensor_copy(dstt[:, col:col + 512], psB[i % 2][:]
                                  ).then_inc(p3be.h, 1)
        p3be.inc(1)

    wt("sync", s_nq, s_nq.n)

    @on("sync")
    def _():
        nc.sync.dma_start(nkt[0:1, :], mkh[0:1, :]).then_inc(d_mk.h, 16)
    d_mk.inc(16)
    wt("scalar", d_mk, d_mk.n)

    @on("scalar")
    def _():
        nc.scalar.activation(lnmk[0:1, :], nkt[0:1, :], AF.Ln
                             ).then_inc(p1s.h, 1)
    p1s.inc(1)
    wt("vector", p1s, p1s.n)

    @on("vector")
    def _():
        nc.vector.drain()
        nc.vector.tensor_scalar_mul(lnmk[0:1, :], lnmk[0:1, :], 0.5
                                    ).then_inc(c_nr.h, 1)
    c_nr.inc(1)

    def emit_cchain(hh):
        par2 = hh % 2
        if hh >= 1:
            wt("sync", c_negC, hh)
        wt("sync", c_nr, 1)

        @on("sync")
        def _(hh=hh):
            nc.sync.dma_start(nq_sb[0:1, :], nqh[hh:hh + 1, :]
                              ).then_inc(d_nq[hh % 2].h, 16)
        d_nq[hh % 2].inc(16)
        wt("scalar", d_nq[hh % 2], d_nq[hh % 2].n)

        @on("scalar")
        def _():
            nc.scalar.activation(kqa[0:1, :], nq_sb[0:1, :], AF.Ln
                                 ).then_inc(p1s.h, 1)
        p1s.inc(1)
        wt("vector", p1s, p1s.n)

        @on("vector")
        def _(hh=hh):
            v = nc.vector
            v.drain()
            v.tensor_scalar(kqt[0:1, :], kqa[0:1, :], 0.5,
                            lnmk[0:1, hh:hh + 1], OP.mult, OP.add
                            ).then_inc(c_sq3.h, 1)
        c_sq3.inc(1)
        wt("scalar", c_sq3, c_sq3.n)

        @on("scalar")
        def _():
            nc.scalar.activation(kqa[0:1, :], kqt[0:1, :], AF.Exp
                                 ).then_inc(p1s.h, 1)
        p1s.inc(1)
        wt("vector", p1s, p1s.n)

        @on("vector")
        def _(par2=par2):
            nc.vector.drain()
            nc.vector.tensor_scalar(negCrow[par2][0:1, :], kqa[0:1, :],
                                    -1.001 / 128.0, -1.0 / 128.0,
                                    OP.mult, OP.add).then_inc(c_negC.h, 1)
        c_negC.inc(1)

    emit_cchain(0)

    negCB_rdy = {0: 0, 1: 0}
    negCB_war = {0: 0, 1: 0}

    def emit_negCB(hh, qq):
        dpar = qq % 2
        wt("tensor", c_negC, hh + 1)
        wt("tensor", mmS, negCB_war[dpar])

        @on("tensor")
        def _(hh=hh, qq=qq):
            nc.tensor.matmul(psB[0][:], ones_rbf[0:1, :],
                             negCrow[hh % 2][0:1, qq * 512:(qq + 1) * 512],
                             start=True, stop=True).then_inc(c_nm.h, 1)
        c_nm.inc(1)
        wt("vector", c_nm, c_nm.n)

        @on("vector")
        def _(dpar=dpar):
            v = nc.vector
            v.drain()
            v.tensor_copy(negCB[dpar][:], psB[0][:]).then_inc(c_nr.h, 1)
        negCB_rdy[dpar] = c_nr.inc(1)

    wt("tensor", p3be, p3be.n)
    emit_negCB(0, 0)

    smm_rel = {}
    pvh_rel = {}
    for h in range(H):
        par = h % 2
        if h >= 2:
            wt("sync", mmS, smm_rel[h - 2])
            wt("sync", mmPV, pvh_rel[h - 2])

        @on("sync")
        def _(h=h, par=par):
            nc.sync.dma_start(Kh[par][:], khh[h][:, :]).then_inc(d_k3[par].h, 16)
            nc.sync.dma_start(Vh[par][:],
                              vhh[:, :, h * 128:(h + 1) * 128].transpose([1, 0, 2])
                              ).then_inc(d_v3[par].h, 16)
            nc.sync.dma_start(Qh[par][:], qhh[h][:, :]).then_inc(d_q3[par].h, 16)
        d_k3[par].inc(16)
        d_v3[par].inc(16)
        d_q3[par].inc(16)

        if h <= 1:
            wt("sync", p3be, 8)
        if h >= 1:
            wt("sync", f_kn[0], h)

        @on("sync")
        def _(h=h):
            nc.sync.dma_start(kst[0][:], kern_d[:, h * 2048:(h + 1) * 2048]
                              ).then_inc(d_kn[0].h, 16)
        d_kn[0].inc(16)

        quantq = deque()

        def _qp1(h=h):
            wt("vector", d_kn[0], 16 * (h + 1))
            wt("vector", p3be, p3be.n)

            @on("vector")
            def _():
                v = nc.vector
                v.tensor_mul(kqt[:], kst[0][:], RSKb[:])
                v.drain()

        def _qp2():
            @on("vector")
            def _():
                v = nc.vector
                v.tensor_copy(kqu[:], kqt[:])
                v.drain()

        def _qp3():
            @on("vector")
            def _():
                v = nc.vector
                v.tensor_sub(kqa[:], kqt[:], kqu[:])
                v.drain()

        def _qp4():
            @on("vector")
            def _():
                v = nc.vector
                v.scalar_tensor_tensor(kqt[:], kqa[:], 0.5, kqu[:],
                                       OP.is_ge, OP.add)
                v.drain()

        def _qp5():
            @on("vector")
            def _():
                v = nc.vector
                v.scalar_tensor_tensor(kqa[:], kqa[:], -0.5, kqt[:], OP.is_lt,
                                       OP.subtract)
                v.drain()

        def _qp6(h=h):
            @on("vector")
            def _(h=h):
                nc.vector.tensor_mul(kq[:, h * 2048:(h + 1) * 2048], kqa[:],
                                     SKnb[:]).then_inc(f_kn[0].h, 1)
            f_kn[0].inc(1)
        for _p in (_qp1, _qp2, _qp3, _qp4, _qp5, _qp6):
            quantq.append(_p)

        wt("tensor", c_negC, c_negC.n)
        wt("tensor", d_v3[par], 16 * (h // 2 + 1))
        wt("tensor", d_k3[par], 16 * (h // 2 + 1))
        wt("tensor", d_q3[par], 16 * (h // 2 + 1))
        exp_rel = {-1: 0}
        pv_rel = {-2: 0, -1: 0}
        def emit_pvden(qt, ss):
            sp = ss % 2
            first = (ss == 0)
            last = (ss == 7)
            if first:
                wt("tensor", ev3, ev3_prev)
            wt("tensor", expd, exp_rel[ss])
            for j in range(4):
                kc = ss * 4 + j

                @on("tensor")
                def _(par=par, kc=kc, j=j, first=first, last=last, sp=sp):
                    nc.tensor.matmul(
                        psC[:], Vh[par][:, kc * 128:(kc + 1) * 128],
                        PT[sp][:, j * 512:(j + 1) * 512],
                        start=(first and j == 0), stop=(last and j == 3),
                        skip_group_check=True)
            for j in range(4):
                @on("tensor")
                def _(j=j, sp=sp, first=first, last=last, fin=(j == 3)):
                    ins = nc.tensor.matmul(
                        psD[0:1, :], ones_bf[:, 0:1],
                        PT[sp][:, j * 512:(j + 1) * 512],
                        start=(first and j == 0), stop=(last and j == 3),
                        skip_group_check=True)
                    if fin:
                        ins.then_inc(mmPV.h, 1)
            pv_rel[ss] = mmPV.inc(1)
            if qt == 3 and ss == 7:
                pvh_rel[h] = mmPV.n

        halves = None
        for qt in range(4):
            wt("tensor", c_nr, negCB_rdy[qt % 2])
            ev3_prev = ev3.n

            hv = [(psA[0], 0), (psA[0], 512), (psA[1], 0), (psA[1], 512)]
            for ss in range(8):
                sp = ss % 2
                wt("tensor", expd, exp_rel.get(ss - 1, 0))
                for j in range(4):
                    @on("tensor")
                    def _(j=j, hv=hv, qt=qt):
                        pa, off = hv[j]
                        nc.tensor.matmul(pa[:, off:off + 512], onesm[:],
                                         negCB[qt % 2][:], start=True,
                                         stop=False, skip_group_check=True)
                for j in range(4):
                    kc = ss * 4 + j

                    @on("tensor")
                    def _(par=par, qt=qt, kc=kc, j=j, fin=(j == 3), hv=hv):
                        pa, off = hv[j]
                        ins = nc.tensor.matmul(
                            pa[:, off:off + 512],
                            Kh[par][:, kc * 128:(kc + 1) * 128],
                            Qh[par][:, qt * 512:(qt + 1) * 512],
                            start=False, stop=True, skip_group_check=True)
                        if fin:
                            ins.then_inc(mmS.h, 1)
                mmS.inc(1)
                if qt == 3 and ss == 7:
                    smm_rel[h] = mmS.n
                if qt >= 2 and quantq:
                    quantq.popleft()()

                wt("scalar", mmS, mmS.n)
                wt("scalar", mmPV, pv_rel.get(ss - 2, 0))

                @on("scalar")
                def _(sp=sp):
                    nc.scalar.activation(PT[sp][:, 0:1024], psA[0][:], AF.Exp,
                                         scale=SMS)
                    nc.scalar.activation(PT[sp][:, 1024:2048], psA[1][:],
                                         AF.Exp, scale=SMS).then_inc(expd.h, 1)
                exp_rel[ss] = expd.inc(1)

                if ss >= 1:
                    emit_pvden(qt, ss - 1)
            emit_pvden(qt, 7)
            negCB_war[qt % 2] = mmS.n
            if qt < 3:
                emit_negCB(h, qt + 1)
            elif h + 1 < H:
                emit_negCB(h + 1, 0)
            if qt == 1 and h + 1 < H:
                emit_cchain(h + 1)
            exp_rel = {-1: exp_rel[7]}
            pv_rel = {-2: pv_rel[6], -1: pv_rel[7]}

            dpar = (h * 4 + qt) % 2
            wt("vector", mmPV, mmPV.n)
            if qt == 0:
                wt("vector", s_ctx[par], s_ctx[par].n)
            if h * 4 + qt >= 2:
                wt("vector", s_den[dpar], s_den[dpar].n)

            @on("vector")
            def _(h=h, qt=qt, par=par, dpar=dpar):
                v = nc.vector
                v.tensor_copy(ctxst[par][:, qt * 512:(qt + 1) * 512], psC[:])
                v.tensor_copy(den_sb[dpar][0:1, :], psD[0:1, :])
                v.drain().then_inc(ev3.h, 1)
            ev3.inc(1)
            wt("scalar", ev3, ev3.n)

            @on("scalar")
            def _(h=h, qt=qt, dpar=dpar):
                nc.scalar.dma_start(denh[h * 4 + qt:h * 4 + qt + 1, :],
                                    den_sb[dpar][0:1, :]
                                    ).then_inc(s_den[dpar].h, 16)
            s_den[dpar].inc(16)

        wt("scalar", ev3, ev3.n)

        @on("scalar")
        def _(h=h, par=par):
            nc.scalar.dma_start(ctxh[h][:, :], ctxst[par][:]
                                ).then_inc(s_ctx[par].h, 16)
        s_ctx[par].inc(16)

        while quantq:
            quantq.popleft()()

    wt("sync", s_den[0], s_den[0].n)
    wt("sync", s_den[1], s_den[1].n)

    @on("sync")
    def _():
        nc.sync.dma_start(dstg[:], denh[:, :]).then_inc(d_sk3.h, 16)
    d_sk3.inc(16)
    wt("vector", d_sk3, d_sk3.n)

    @on("vector")
    def _():
        v = nc.vector
        v.reciprocal(rcps[:], dstg[:])
        v.drain().then_inc(c_rec.h, 1)
    c_rec.inc(1)
    wt("scalar", c_rec, c_rec.n)

    @on("scalar")
    def _():
        nc.scalar.dma_start(rech.rearrange("(q h) f -> h q f", h=16), rcps[:]
                            ).then_inc(s_rec.h, 16)
    s_rec.inc(16)

    barrier(extra_sync_waits=[(s_ctx[0], s_ctx[0].n), (s_ctx[1], s_ctx[1].n),
                              (s_rec, 16)])
    st3.close()
    st4 = ExitStack()
    ctxq = [st4.enter_context(nc.sbuf_tensor("ctxq0", [128, 8192], BF16)),
            st4.enter_context(nc.sbuf_tensor("ctxq1", [128, 8192], BF16))]
    recrow = [st4.enter_context(nc.sbuf_tensor("recrow0", [1, 1024], F32))]
    recrow.append(recrow[0])
    wob = st4.enter_context(nc.sbuf_tensor("wob", [128, 32768], BF16))
    y1 = st4.enter_context(nc.sbuf_tensor("y1", [128, 2048], F32))
    xres = [st4.enter_context(nc.sbuf_tensor("xres0", [128, 2048], BF16)),
            st4.enter_context(nc.sbuf_tensor("xres1", [128, 2048], BF16))]
    obB = st4.enter_context(nc.sbuf_tensor("obB", [128, 2048], BF16))
    sqt4 = st4.enter_context(nc.sbuf_tensor("sqt4", [128, 2048], BF16))
    x2st = [st4.enter_context(nc.sbuf_tensor("x2st0", [128, 2048], BF16))]
    x2st.append(x2st[0])
    ob_bf = sqt4[0:1, :]

    @on("gpsimd")
    def _():
        nc.gpsimd.dma_start(ob_bf[:], obrow_d[:]).then_inc(d_ob.h, 16)
    d_ob.inc(16)
    for fg in range(4):
        @on("gpsimd")
        def _(fg=fg):
            nc.gpsimd.dma_start(
                wob[:, fg * 8192:(fg + 1) * 8192],
                wo_d.rearrange("p (dc f) -> p dc f", dc=16)[:, :,
                                                            fg * 512:(fg + 1) * 512]
                ).then_inc(d_wo[fg // 2].h, 16)
        d_wo[fg // 2].inc(16)
    wt("tensor", d_ob, 16)
    for i in range(4):
        if i >= 2:
            wt("tensor", p4ev, p4ev.n - 1)

        @on("tensor")
        def _(i=i):
            nc.tensor.matmul(psB[i % 2][:], ones_rbf[0:1, :],
                             ob_bf[0:1, i * 512:(i + 1) * 512],
                             start=True, stop=True).then_inc(p4mg.h, 1)
        p4mg.inc(1)
        wt("vector", p4mg, p4mg.n)

        @on("vector")
        def _(i=i):
            nc.vector.tensor_scalar(obB[:, i * 512:(i + 1) * 512], psB[i % 2][:],
                                    1.0 / 128.0, None, OP.mult
                                    ).then_inc(p4ev.h, 1)
        p4ev.inc(1)
    wt("tensor", p4ev, p4ev.n)
    p4ev_base = p4ev.n

    g4 = 0
    rr_rel = {}
    for qt in range(4):
        par = qt % 2
        if qt >= 2:
            wt("sync", c_ctxn, 16 * (qt - 1))

        @on("sync")
        def _(qt=qt, par=par):
            nc.sync.dma_start(
                ctxq[par][:],
                ctxh[:, :, qt * 512:(qt + 1) * 512].transpose([1, 0, 2])
                ).then_inc(d_cq[par].h, 16)
        d_cq[par].inc(16)

        wt("tensor", d_cq[par], 16 * (qt // 2 + 1))
        wt("vector", d_cq[par], 16 * (qt // 2 + 1))
        bnk3 = [psB[0], psB[1], psC]
        for h in range(H):
            if h % 2 == 0:
                hp = h // 2
                if qt * 8 + hp >= 1:
                    wt("sync", c_nm, rr_rel["last"])

                @on("sync")
                def _(qt=qt, hp=hp):
                    nc.sync.dma_start(recrow[hp % 2][:],
                                      rech[qt * 16 + hp * 2: qt * 16 + hp * 2 + 2, :]
                                      ).then_inc(d_rr[hp % 2].h, 16)
                d_rr[hp % 2].inc(16)
            wt("tensor", d_rr[(h // 2) % 2], d_rr[(h // 2) % 2].n)
            if h >= 3 or qt >= 1:
                wt("tensor", c_ctxn, max(c_ctxn.n - 2, 0))

            @on("tensor")
            def _(h=h, qt=qt):
                nc.tensor.matmul(bnk3[h % 3][:],
                                 ones_f[0:1, :],
                                 recrow[(h // 2) % 2][0:1, (h % 2) * 512:(h % 2) * 512 + 512],
                                 start=True, stop=True).then_inc(c_nm.h, 1)
            c_nm.inc(1)
            if h % 2 == 1:
                rr_rel["last"] = c_nm.n
            wt("vector", c_nm, c_nm.n)

            @on("vector")
            def _(h=h, par=par):
                nc.vector.tensor_mul(ctxq[par][:, h * 512:(h + 1) * 512],
                                     ctxq[par][:, h * 512:(h + 1) * 512],
                                     bnk3[h % 3][:]).then_inc(c_ctxn.h, 1)
            c_ctxn.inc(1)

        wt("tensor", c_ctxn, c_ctxn.n)
        for tc in range(4):
            gi = qt * 4 + tc
            if gi >= 2:
                wt("gpsimd", c_x2, gi - 1)

            @on("gpsimd")
            def _(qt=qt, tc=tc, gi=gi):
                r0 = qt * 512 + tc * 128
                nc.gpsimd.dma_start(xres[gi % 2][:], x_in[r0:r0 + 128, :]
                                    ).then_inc(d_xr[gi % 2].h, 16)
            d_xr[gi % 2].inc(16)

            for fg in range(4):
                g4 += 1
                wt("tensor", d_wo[fg // 2], 32)
                if g4 > 2:
                    wt("tensor", p4ev, p4ev_base + g4 - 2)
                @on("tensor")
                def _(fg=fg, g=g4):
                    nc.tensor.matmul(psA[g % 2][:, 0:512], onesm[:],
                                     obB[:, fg * 512:(fg + 1) * 512],
                                     start=True, stop=False,
                                     skip_group_check=True)
                for dc in range(DC):
                    @on("tensor")
                    def _(tc=tc, dc=dc, g=g4, fg=fg, par=par, fin=(dc == DC - 1)):
                        ins = nc.tensor.matmul(
                            psA[g % 2][:, 0:512],
                            ctxq[par][:, dc * 512 + tc * 128: dc * 512 + (tc + 1) * 128],
                            wob[:, fg * 8192 + dc * 512:fg * 8192 + (dc + 1) * 512],
                            start=False, stop=fin, skip_group_check=True)
                        if fin:
                            ins.then_inc(p4mg.h, 1)
                p4mg.inc(1)

                wt("vector", p4mg, p4mg.n)
                if fg == 0 and gi >= 1:
                    wt("vector", p4s, p4s.n)

                @on("vector")
                def _(fg=fg, g=g4):
                    if fg == 0:
                        nc.vector.drain()
                    nc.vector.tensor_copy(y1[:, fg * 512:(fg + 1) * 512],
                                          psA[g % 2][:, 0:512]
                                          ).then_inc(p4ev.h, 1)
                p4ev.inc(1)

            wt("scalar", p4ev, p4ev.n)
            if gi >= 1:
                wt("scalar", p4v, p4v.n)

            @on("scalar")
            def _():
                nc.scalar.activation(sqt4[:], y1[:], AF.Square,
                                     accum_out=smal[:, 0:1]).then_inc(p4s.h, 1)
            p4s.inc(1)
            wt("vector", p4s, p4s.n)

            @on("vector")
            def _():
                v = nc.vector
                v.reduce_sum(smal[:, 1:2], y1[:], axis=AX)
                v.drain()
                v.tensor_scalar_mul(smal[:, 2:3], smal[:, 1:2], 1.0 / 2048)
                v.tensor_scalar_mul(smal[:, 3:4], smal[:, 0:1], 1.0 / 2048)
                v.drain()
                v.tensor_mul(smal[:, 4:5], smal[:, 2:3], smal[:, 2:3])
                v.drain()
                v.tensor_sub(smal[:, 4:5], smal[:, 3:4], smal[:, 4:5]
                             ).then_inc(p4v.h, 1)
            p4v.inc(1)
            wt("scalar", p4v, p4v.n)

            @on("scalar")
            def _():
                s = nc.scalar
                s.activation(smal[:, 5:6], smal[:, 4:5], AF.Ln, bias=epsv[:, 0:1])
                s.drain()
                s.activation(smal[:, 6:7], smal[:, 5:6], AF.Exp, scale=-0.5
                             ).then_inc(p4s.h, 1)
            p4s.inc(1)
            wt("vector", p4s, p4s.n)
            wt("vector", d_xr[gi % 2], d_xr[gi % 2].n)
            if gi >= 1:
                wt("vector", s_x2[(gi + 1) % 2], s_x2[(gi + 1) % 2].n)

            @on("vector")
            def _(gi=gi):
                v = nc.vector
                v.tensor_scalar(y1[:], y1[:], smal[:, 2:3], None, OP.subtract)
                v.drain()
                v.scalar_tensor_tensor(x2st[gi % 2][:], y1[:], smal[:, 6:7],
                                       xres[gi % 2][:], OP.mult, OP.add
                                       ).then_inc(c_x2.h, 1)
            c_x2.inc(1)
            wt("scalar", c_x2, c_x2.n)

            @on("scalar")
            def _(qt=qt, tc=tc, gi=gi):
                r0 = qt * 512 + tc * 128
                nc.scalar.dma_start(x2h[r0:r0 + 128, :], x2st[gi % 2][:]
                                    ).then_inc(s_x2[gi % 2].h, 16)
            s_x2[gi % 2].inc(16)

    barrier(extra_sync_waits=[(s_x2[0], s_x2[0].n), (s_x2[1], s_x2[1].n)])
    st4.close()
    st5 = ExitStack()
    x2q = st5.enter_context(nc.sbuf_tensor("x2q", [128, 4 * 2048], BF16))
    h2 = st5.enter_context(nc.sbuf_tensor("h2", [128, 2048], F32))
    qtt = st5.enter_context(nc.sbuf_tensor("qtt", [128, 2048], F32))
    quu = st5.enter_context(nc.sbuf_tensor("quu", [128, 2048], I32))
    qaa = st5.enter_context(nc.sbuf_tensor("qaa", [128, 2048], F32))
    xqtile = [st5.enter_context(nc.sbuf_tensor("xqtile0", [128, 2048], BF16)),
              st5.enter_context(nc.sbuf_tensor("xqtile1", [128, 2048], BF16))]
    xqfm = [st5.enter_context(nc.sbuf_tensor("xqfm0", [128, 8192], BF16)),
            st5.enter_context(nc.sbuf_tensor("xqfm1", [128, 8192], BF16))]
    zt = st5.enter_context(nc.sbuf_tensor("zt", [128, 2048], F32))
    yq = st5.enter_context(nc.sbuf_tensor("yq", [128, 2048], F32))
    sqt5 = st5.enter_context(nc.sbuf_tensor("sqt5", [128, 2048], BF16))
    xmod5 = st5.enter_context(nc.sbuf_tensor("xmod5", [128, 2048], F32))
    outt = st5.enter_context(nc.sbuf_tensor("outt0", [128, 2048], F32))
    bqB = st5.enter_context(nc.sbuf_tensor("bqB", [128, 2048], BF16))

    wt("tensor", c_bias, 1)
    for i in range(4):
        if i >= 2:
            wt("tensor", p5ev, p5ev.n - 1)

        @on("tensor")
        def _(i=i):
            nc.tensor.matmul(psB[i % 2][:], ones_rbf[0:1, :],
                             bq_bf[0:1, i * 512:(i + 1) * 512],
                             start=True, stop=True).then_inc(p5mg.h, 1)
        p5mg.inc(1)
        wt("vector", p5mg, p5mg.n)

        @on("vector")
        def _(i=i):
            nc.vector.tensor_scalar(bqB[:, i * 512:(i + 1) * 512], psB[i % 2][:],
                                    1.0 / 128.0, None, OP.mult
                                    ).then_inc(p5ev.h, 1)
        p5ev.inc(1)
    wt("tensor", p5ev, p5ev.n)
    p5mg_base = p5mg.n
    p5ev_base = p5ev.n
    for qt in range(4):
        par = qt % 2
        for tc in range(4):
            gi = qt * 4 + tc
            if qt >= 1:
                wt("sync", c_out, qt * 4)

            @on("sync")
            def _(qt=qt, tc=tc, par=par):
                r0 = qt * 512 + tc * 128
                nc.sync.dma_start(x2q[:, tc * 2048:(tc + 1) * 2048],
                                  x2h[r0:r0 + 128, :]).then_inc(d_x2[par].h, 16)
            d_x2[par].inc(16)

        for tc in range(4):
            gi = qt * 4 + tc
            wt("scalar", d_x2[par], d_x2[par].n)
            if gi >= 1:
                wt("scalar", p5v, p5v.n)

            @on("scalar")
            def _(tc=tc):
                nc.scalar.activation(sqt5[:], x2q[:, tc * 2048:(tc + 1) * 2048],
                                     AF.Square, accum_out=smal[:, 0:1]
                                     ).then_inc(p5s.h, 1)
            p5s.inc(1)
            wt("vector", d_x2[par], d_x2[par].n)
            wt("vector", p5s, p5s.n)

            @on("vector")
            def _(tc=tc):
                v = nc.vector
                v.reduce_sum(smal[:, 1:2], x2q[:, tc * 2048:(tc + 1) * 2048],
                             axis=AX)
                v.drain()
                v.tensor_scalar_mul(smal[:, 2:3], smal[:, 1:2], 1.0 / 2048)
                v.tensor_scalar_mul(smal[:, 3:4], smal[:, 0:1], 1.0 / 2048)
                v.drain()
                v.tensor_mul(smal[:, 4:5], smal[:, 2:3], smal[:, 2:3])
                v.drain()
                v.tensor_sub(smal[:, 4:5], smal[:, 3:4], smal[:, 4:5]
                             ).then_inc(p5v.h, 1)
            p5v.inc(1)
            wt("scalar", p5v, p5v.n)

            @on("scalar")
            def _():
                s = nc.scalar
                s.activation(smal[:, 5:6], smal[:, 4:5], AF.Ln, bias=epsv[:, 0:1])
                s.drain()
                s.activation(smal[:, 6:7], smal[:, 5:6], AF.Exp, scale=-0.5
                             ).then_inc(p5s.h, 1)
            p5s.inc(1)
            wt("vector", p5s, p5s.n)
            if gi >= 2:
                wt("vector", s_xq[gi % 2], s_xq[gi % 2].n)

            @on("vector")
            def _(tc=tc, gi=gi):
                v = nc.vector
                v.scalar_tensor_tensor(smal[:, 7:8], smal[:, 2:3], -1.0,
                                       smal[:, 6:7], OP.mult, OP.mult)
                v.drain()
                v.tensor_scalar(h2[:], x2q[:, tc * 2048:(tc + 1) * 2048],
                                smal[:, 6:7], smal[:, 7:8], OP.mult, OP.add)
                v.drain()
                v.reduce_max(smal[:, 8:9], h2[:], axis=AX,
                             apply_absolute_value=True)
                v.drain()
                v.tensor_scalar_max(smal[:, 8:9], smal[:, 8:9], 1e-30)
                v.drain()
                v.reciprocal(smal[:, 9:10], smal[:, 8:9])
                v.drain()
                v.tensor_scalar_mul(smal[:, 10:11], smal[:, 9:10], 127.0)
                v.tensor_scalar_mul(smal[:, 11:12], smal[:, 8:9], -1.0 / 127.0)
                v.drain()
                v.tensor_scalar(qtt[:], h2[:], smal[:, 10:11], None, OP.mult)
                v.drain()
                v.tensor_copy(quu[:], qtt[:])
                v.drain()
                v.tensor_sub(qaa[:], qtt[:], quu[:])
                v.drain()
                v.scalar_tensor_tensor(qtt[:], qaa[:], 0.5, quu[:],
                                       OP.is_ge, OP.add)
                v.drain()
                v.scalar_tensor_tensor(qtt[:], qaa[:], 1.5, qtt[:],
                                       OP.is_ge, OP.add)
                v.drain()
                v.scalar_tensor_tensor(h2[:], qaa[:], -0.5, qtt[:],
                                       OP.is_lt, OP.subtract)
                v.drain()
                v.scalar_tensor_tensor(h2[:], qaa[:], -1.5, h2[:],
                                       OP.is_lt, OP.add)
                v.drain()
                v.tensor_scalar(xqtile[gi % 2][:], h2[:], smal[:, 11:12], None,
                                OP.mult).then_inc(c_xq.h, 1)
            c_xq.inc(1)
            wt("scalar", c_xq, c_xq.n)

            @on("scalar")
            def _(qt=qt, tc=tc, gi=gi):
                r0 = qt * 512 + tc * 128
                nc.scalar.dma_start(xqh[r0:r0 + 128, :], xqtile[gi % 2][:]
                                    ).then_inc(s_xq[gi % 2].h, 16)
            s_xq[gi % 2].inc(16)

        wt("sync", s_xq[0], s_xq[0].n)
        wt("sync", s_xq[1], s_xq[1].n)
        if qt >= 2:
            wt("sync", p5mg, p5mg.n - 4)
        for dc in range(DC):
            @on("sync")
            def _(qt=qt, dc=dc, par=par):
                nc.sync.dma_start(
                    xqfm[par][:, dc * 512:(dc + 1) * 512],
                    xqh[qt * 512:(qt + 1) * 512, dc * 128:(dc + 1) * 128],
                    transpose=True).then_inc(d_xqt[par].h, 16)
            d_xqt[par].inc(16)

        wt("tensor", d_xqt[par], d_xqt[par].n)
        wt("tensor", c_bias, 1)
        for tc in range(4):
            gi = qt * 4 + tc
            for fg in range(4):
                g5 = qt * 16 + tc * 4 + fg + 1
                if g5 > 2:
                    wt("tensor", p5ev, p5ev_base + g5 - 2)

                @on("tensor")
                def _(fg=fg, g=g5):
                    nc.tensor.matmul(psA[g % 2][:, 0:512], onesm[:],
                                     bqB[:, fg * 512:(fg + 1) * 512],
                                     start=True, stop=False,
                                     skip_group_check=True)
                for dc in range(DC):
                    @on("tensor")
                    def _(tc=tc, dc=dc, fg=fg, par=par, g=g5, fin=(dc == DC - 1)):
                        ins = nc.tensor.matmul(
                            psA[g % 2][:, 0:512],
                            xqfm[par][:, dc * 512 + tc * 128: dc * 512 + (tc + 1) * 128],
                            kq[:, dc * 2048 + fg * 512: dc * 2048 + (fg + 1) * 512],
                            start=False, stop=fin, skip_group_check=True)
                        if fin:
                            ins.then_inc(p5mg.h, 1)
                p5mg.inc(1)
                wt("scalar", p5mg, p5mg.n)

                @on("scalar")
                def _(fg=fg, g=g5):
                    nc.scalar.activation(zt[:, fg * 512:(fg + 1) * 512],
                                         psA[g % 2][:, 0:512], AF.Relu
                                         ).then_inc(p5ev.h, 1)
                p5ev.inc(1)

            wt("vector", p5ev, p5ev.n)
            if gi >= 1:
                wt("vector", s_out, 16 * gi)

            @on("vector")
            def _(tc=tc, gi=gi):
                v = nc.vector
                v.reduce_max(smal[:, 8:9], zt[:], axis=AX)
                v.drain()
                v.tensor_scalar_max(smal[:, 8:9], smal[:, 8:9], 1e-30)
                v.drain()
                v.reciprocal(smal[:, 9:10], smal[:, 8:9])
                v.drain()
                v.tensor_scalar_mul(smal[:, 10:11], smal[:, 9:10], 127.0)
                v.tensor_scalar_mul(smal[:, 11:12], smal[:, 8:9], -1.0 / 127.0)
                v.drain()
                v.tensor_scalar(qtt[:], zt[:], smal[:, 10:11], None, OP.mult)
                v.drain()
                v.tensor_copy(quu[:], qtt[:])
                v.drain()
                v.tensor_sub(qaa[:], qtt[:], quu[:])
                v.drain()
                v.scalar_tensor_tensor(qtt[:], qaa[:], 0.5, quu[:],
                                       OP.is_ge, OP.add)
                v.drain()
                v.scalar_tensor_tensor(qtt[:], qaa[:], 1.5, qtt[:],
                                       OP.is_ge, OP.add)
                v.drain()
                v.scalar_tensor_tensor(zt[:], qaa[:], -0.5, qtt[:],
                                       OP.is_lt, OP.subtract)
                v.drain()
                v.scalar_tensor_tensor(zt[:], qaa[:], -1.5, zt[:],
                                       OP.is_lt, OP.add)
                v.drain()
                v.tensor_scalar(yq[:], zt[:], smal[:, 11:12], None, OP.mult
                                ).then_inc(p5v.h, 1)
            p5v.inc(1)
            wt("scalar", p5v, p5v.n)

            @on("scalar")
            def _():
                nc.scalar.activation(sqt5[:], yq[:], AF.Square,
                                     accum_out=smal[:, 0:1]).then_inc(p5s.h, 1)
            p5s.inc(1)
            wt("vector", p5s, p5s.n)

            @on("vector")
            def _():
                v = nc.vector
                v.reduce_sum(smal[:, 1:2], yq[:], axis=AX)
                v.drain()
                v.tensor_scalar_mul(smal[:, 2:3], smal[:, 1:2], 1.0 / 2048)
                v.tensor_scalar_mul(smal[:, 3:4], smal[:, 0:1], 1.0 / 2048)
                v.drain()
                v.tensor_mul(smal[:, 4:5], smal[:, 2:3], smal[:, 2:3])
                v.drain()
                v.tensor_sub(smal[:, 4:5], smal[:, 3:4], smal[:, 4:5]
                             ).then_inc(p5v.h, 1)
            p5v.inc(1)
            wt("scalar", p5v, p5v.n)

            @on("scalar")
            def _():
                s = nc.scalar
                s.activation(smal[:, 5:6], smal[:, 4:5], AF.Ln, bias=epsv[:, 0:1])
                s.drain()
                s.activation(smal[:, 6:7], smal[:, 5:6], AF.Exp, scale=-0.5
                             ).then_inc(p5s.h, 1)
            p5s.inc(1)
            wt("vector", p5s, p5s.n)

            @on("vector")
            def _(tc=tc, gi=gi):
                v = nc.vector
                v.scalar_tensor_tensor(smal[:, 7:8], smal[:, 2:3], -1.0,
                                       smal[:, 6:7], OP.mult, OP.mult)
                v.drain()
                v.tensor_scalar(xmod5[:], x2q[:, tc * 2048:(tc + 1) * 2048],
                                smal[:, 7:8], None, OP.add)
                v.drain()
                v.scalar_tensor_tensor(outt[:], yq[:], smal[:, 6:7],
                                       xmod5[:], OP.mult, OP.add
                                       ).then_inc(c_out.h, 1)
            c_out.inc(1)
            wt("scalar", c_out, c_out.n)

            @on("scalar")
            def _(qt=qt, tc=tc, gi=gi):
                r0 = qt * 512 + tc * 128
                nc.scalar.dma_start(out_d[r0:r0 + 128, :], outt[:]
                                    ).then_inc(s_out.h, 16)
            s_out.inc(16)

    wt("scalar", s_out, s_out.n)

    with nc.Block() as blk:
        @blk.sync
        def _(eng):
            for f in OPS["sync"]:
                f()

        @blk.gpsimd
        def _(eng):
            for f in OPS["gpsimd"]:
                f()

        @blk.tensor
        def _(eng):
            for f in OPS["tensor"]:
                f()

        @blk.vector
        def _(eng):
            for f in OPS["vector"]:
                f()

        @blk.scalar
        def _(eng):
            for f in OPS["scalar"]:
                f()

    st5.close()
    st3a.close()
    ctx.close()
    return nc


def _arrange(inputs):
    Wqkv = np.asarray(inputs["L0_attn_Wqkv_w"], np.float32)
    Wo = np.asarray(inputs["L0_attn_out_w"], np.float32)
    bq = np.asarray(inputs["L0_attn_Wqkv_b"], np.float32)
    ob = np.asarray(inputs["L0_attn_out_b"], np.float32)
    kern = np.asarray(inputs["L0_ffn_kernel"], np.float32)
    fb = np.asarray(inputs["L0_ffn_bias"], np.float32)

    WqkvT = Wqkv.T
    wqk = np.ascontiguousarray(
        WqkvT[:, :4096].reshape(16, 128, 32, 128).transpose(1, 2, 0, 3)
        .reshape(128, 32 * 2048))
    wv = np.ascontiguousarray(
        WqkvT[:, 4096:].reshape(16, 128, 4, 512).transpose(1, 2, 0, 3)
        .reshape(128, 4 * 8192))
    wo = np.ascontiguousarray(
        Wo.T.reshape(16, 128, 2048).transpose(1, 0, 2).reshape(128, 16 * 2048))
    kn = np.ascontiguousarray(
        kern.reshape(16, 128, 2048).transpose(1, 0, 2).reshape(128, 16 * 2048))
    kt = np.ascontiguousarray(
        kern.T.reshape(16, 128, 2048).transpose(1, 0, 2).reshape(128, 16 * 2048))
    bqkv = np.ascontiguousarray(bq[:6144].reshape(48, 128).T)
    return {
        "wqk": wqk, "wv": wv, "wo": wo, "kern": kn, "kernT": kt,
        "bqkv": bqkv,
        "bvrow": np.ascontiguousarray(bq[4096:6144][None, :]),
        "obrow": np.ascontiguousarray(ob[None, :]),
        "fbrow": np.ascontiguousarray(fb[None, :]),
    }


def kernel(L0_attn_hidden_states, L0_attn_Wqkv_w, L0_attn_Wqkv_b,
           L0_attn_out_w, L0_attn_out_b, L0_ffn_kernel, L0_ffn_bias):
    x = np.asarray(L0_attn_hidden_states, np.float32)
    shared = _arrange(dict(
        L0_attn_Wqkv_w=L0_attn_Wqkv_w, L0_attn_Wqkv_b=L0_attn_Wqkv_b,
        L0_attn_out_w=L0_attn_out_w, L0_attn_out_b=L0_attn_out_b,
        L0_ffn_kernel=L0_ffn_kernel, L0_ffn_bias=L0_ffn_bias))

    nc = build_program()
    in_maps = []
    for c in range(NCORE):
        seq, half = c // 2, c % 2
        own = x[seq, half * 2048:(half + 1) * 2048, :]
        par = x[seq, (1 - half) * 2048:(2 - half) * 2048, :]
        xa = np.ascontiguousarray(np.concatenate([own, par], axis=0))
        m = {"x": xa}
        m.update(shared)
        in_maps.append(m)

    res = run_bass_kernel_spmd(nc, in_maps, core_ids=list(range(NCORE)))
    out = np.empty((4, 4096, 2048), np.float32)
    for c in range(NCORE):
        seq, half = c // 2, c % 2
        out[seq, half * 2048:(half + 1) * 2048, :] = res.results[c]["out"]
    return out
